# version 19
# baseline (speedup 1.0000x reference)
import sys
if "/opt/trn_rl_repo" not in sys.path:
    sys.path.insert(0, "/opt/trn_rl_repo")
import numpy as np
import ml_dtypes
from contextlib import ExitStack
import concourse.bass as bass
import concourse.mybir as mybir
from concourse.bass_utils import run_bass_kernel_spmd

F32 = mybir.dt.float32
BF16 = mybir.dt.bfloat16
AF = mybir.ActivationFunctionType
ALU = mybir.AluOpType

B, T, D = 4, 8192, 1024
DBG = {}
TL = 4096
DC = 8
HC = 32
LN_EPS = 1e-5
HALO = 32
SHIFTS = (0, -32, -16, -4, -1, 1, 4, 16, 32)


def _pedrain(p, tt, val):
    if DBG.get('drain'):
        tt.wait_ge(p.h('pe'), val)


class Prog:
    def __init__(self, nc, stack):
        self.nc = nc
        self.stack = stack
        self.block = stack.enter_context(nc.Block())
        self._sems = {}

    def sem(self, name):
        if name not in self._sems:
            h = self.stack.enter_context(self.nc.semaphore(name))
            self._sems[name] = [h, 0]
        return self._sems[name]

    def h(self, name):
        return self.sem(name)[0]

    def v(self, name):
        return self.sem(name)[1]

    def inc(self, instr, name, n=1):
        s = self.sem(name)
        instr.then_inc(s[0], n)
        s[1] += n
        return s[1]


class WScratch:
    def __init__(self, nc, name, d_in, d_out):
        self.d_in, self.d_out = d_in, d_out
        self.kK, self.mK = d_in // 128, d_out // 128
        self.cols = self.kK * 128
        self.scr = nc.dram_tensor(f"{name}_scr", [self.mK, 128, self.cols], BF16)
        self.cast_sem = f"cast_{name}"
        self.cast_done = 0

    def emit_cast(self, p, w_hbm_ap, n_groups=1):
        src = w_hbm_ap.rearrange("(k p) (m c) -> m p k c", p=128, c=128)
        scr = self.scr.ap().rearrange("m p (k c) -> m p k c", c=128)
        mK = self.mK
        self.grp = max(1, (mK + n_groups - 1) // n_groups)
        self.gsems = []
        self.gdone = []
        for g0 in range(0, mK, self.grp):
            sem = f"{self.cast_sem}_{g0}"

            def go(g, g0=g0, sem=sem):
                for m in range(g0, min(g0 + self.grp, mK)):
                    p.inc(g.dma_start(out=scr[m], in_=src[m]), sem, 16)
            p.block.gpsimd(go)
            self.gsems.append(sem)
            self.gdone.append(p.v(sem))
        self.cast_done = -1

    def cast_wait(self, m):
        gi = min(m // self.grp, len(self.gsems) - 1)
        return (self.gsems[gi], self.gdone[gi])

    def block_ap(self, m):
        return self.scr.ap()[m]


def emit_ln512(p, cst, r_tiles, out_tiles, pb, state, norm_emit=None, ve_r=None):
    if ve_r is None:
        ve_r = p.v('ve')
    rows = cst['rows']
    mu_row = rows[:, 0:512]
    e2_row = rows[:, 512:1024]
    var_row = rows[:, 1024:1536]
    sd_row = rows[:, 512:1024]
    rs_row = rows[:, 1536:2048]
    r2 = state['r2']

    g_rows_v = state.get('rows_read_ve', 0)

    def mm_sum2(t):
        if ve_r > 0:
            t.wait_ge(p.h('ve'), ve_r)
        if g_rows_v > 0:
            t.wait_ge(p.h('ve'), g_rows_v)
        for mc in range(DC):
            mm = t.matmul(pb['sum'][0:1, :], cst['ones_sc_bf'], r_tiles[mc],
                          start=(mc == 0), stop=(mc == DC - 1))
        p.inc(mm, 'pe')
    p.block.tensor(mm_sum2)

    g_sq_ac = state.get('sq_read_pe', 0)

    def ve_sq(v):
        if g_sq_ac > 0:
            v.wait_ge(p.h('pe'), g_sq_ac)
        for mc in range(DC):
            ins = v.tensor_tensor(r2[:, mc * 512:(mc + 1) * 512], r_tiles[mc],
                                  r_tiles[mc], ALU.mult)
        p.inc(ins, 've')
    p.block.vector(ve_sq)
    ac_sq = p.v('ve')

    def mm_sq(t):
        t.wait_ge(p.h('ve'), ac_sq)
        if g_rows_v > 0:
            t.wait_ge(p.h('ve'), g_rows_v)
        for mc in range(DC):
            mm = t.matmul(pb['sq'][0:1, :], cst['ones_sc_bf'], r2[:, mc * 512:(mc + 1) * 512],
                          start=(mc == 0), stop=(mc == DC - 1))
        p.inc(mm, 'pe')
    p.block.tensor(mm_sq)
    pe_sq = p.v('pe')
    state['sq_read_pe'] = pe_sq

    def ve_rows(v):
        v.wait_ge(p.h('pe'), pe_sq)
        v.tensor_copy(mu_row, pb['sum'][0:1, :])
        c0 = p.inc(v.tensor_copy(e2_row, pb['sq'][0:1, :]), 've')
        v.wait_ge(p.h('ve'), c0)
        c1 = p.inc(v.scalar_tensor_tensor(var_row, mu_row, -1.0, mu_row,
                                          ALU.mult, ALU.mult), 've')
        v.wait_ge(p.h('ve'), c1)
        ins = v.tensor_tensor(var_row, e2_row, var_row, ALU.add)
        p.inc(ins, 've')
    p.block.vector(ve_rows)
    ve_var = p.v('ve')
    state['rows_read_ve'] = ve_var

    def act_sd(s):
        s.wait_ge(p.h('ve'), ve_var)
        ins = s.activation(sd_row, var_row, AF.Sqrt, bias=cst['eps_ln'][0:1, :])
        p.inc(ins, 'ac')
    p.block.scalar(act_sd)
    ac_sd = p.v('ac')

    def ve_rs(v):
        v.wait_ge(p.h('ac'), ac_sd)
        ins = v.reciprocal(rs_row, sd_row)
        p.inc(ins, 've')
    p.block.vector(ve_rs)
    ve_rsv = p.v('ve')

    g_norm = state.get('norm_done_ve', 0)

    def mm_bc(t):
        t.wait_ge(p.h('ve'), ve_rsv)
        if g_norm > 0:
            t.wait_ge(p.h('ve'), g_norm)
        t.matmul(pb['mu'], cst['ones1_f'], mu_row, start=True, stop=True)
        mm = t.matmul(pb['rs'], cst['ones1_f'], rs_row, start=True, stop=True)
        p.inc(mm, 'pe')
    p.block.tensor(mm_bc)
    pe_bc = p.v('pe')

    tmp = cst['lntmp']

    def ve_norm(v):
        v.wait_ge(p.h('pe'), pe_bc)
        if norm_emit is not None:
            ins = norm_emit(v, pb['mu'], pb['rs'], tmp)
        else:
            for mc in range(DC):
                v.tensor_tensor(tmp, r_tiles[mc], pb['mu'], ALU.subtract)
                ins = v.tensor_tensor(out_tiles[mc], tmp, pb['rs'], ALU.mult)
        p.inc(ins, 've')
    p.block.vector(ve_norm)
    state['norm_done_ve'] = p.v('ve')
    return p.v('ve')


def emit_l0attn(p, cst, wsg, wsv, x0bf, x0f, x_out, stack, x_ready=None):
    nc = p.nc
    NT = 1024
    WH = NT + 64
    ntiles = TL // NT
    xw_sb = [stack.enter_context(nc.sbuf_tensor(f"at_xw{i}", [128, DC * WH], BF16)).ap()
             for i in range(2)]
    vsb = stack.enter_context(nc.sbuf_tensor("at_v", [128, DC * WH], BF16)).ap()
    accf = stack.enter_context(nc.sbuf_tensor("at_acc", [128, DC * NT], BF16)).ap()
    accb = stack.enter_context(nc.sbuf_tensor("at_accb", [128, 2 * DC * NT], BF16)).ap()
    gsb = stack.enter_context(nc.sbuf_tensor("at_g", [128, 9 * NT], BF16)).ap()
    gtmp = stack.enter_context(nc.sbuf_tensor("at_gt", [128, NT], BF16)).ap()
    Rb = stack.enter_context(nc.sbuf_tensor("at_R", [128, DC * NT], BF16)).ap()
    r2 = stack.enter_context(nc.sbuf_tensor("at_r2", [128, 4096], BF16)).ap()
    xout_st = stack.enter_context(nc.sbuf_tensor("at_xo", [128, DC * NT], BF16)).ap()
    Wb = stack.enter_context(nc.sbuf_tensor("at_W", [128, 4 * 1024], BF16)).ap()
    t2f = stack.enter_context(nc.sbuf_tensor("at_t2", [128, 512], F32)).ap()
    pg = [stack.enter_context(nc.psum_tensor(f"at_pg{i}", [128, 1024], F32)).ap() for i in range(2)]
    pln = stack.enter_context(nc.psum_tensor("at_pln", [1, 1024], F32)).ap()
    pbc = stack.enter_context(nc.psum_tensor("at_pbc", [128, 1024], F32)).ap()
    pb = {'sum': pln[0:1, 0:512], 'sq': pln[0:1, 512:1024],
          'mu': pbc[:, 0:512], 'rs': pbc[:, 512:1024]}
    ln_state = {'r2': r2}

    x0bf_r = x0bf.ap().rearrange("c p t -> p c t")
    x0f_r = x0f.ap().rearrange("c p t -> p c t")
    x_out_r = x_out.ap().rearrange("c p t -> p c t")

    wslot = [Wb[:, i * 1024:(i + 1) * 1024] for i in range(4)]
    pend_ln = None
    wguard = [0, 0, 0, 0]
    xw_guard = [0, 0]
    xf_guard = 0
    gsb_guard = [0]
    pg_guard_ac = [0, 0]
    pg_guard_ve = [0, 0]
    accb_guard = (0, 0)
    rb_guard = (0, 0)
    xout_guard = 0
    wsl_i = 0

    def wload(ws, m):
        nonlocal wsl_i
        s = wsl_i % 4
        wsl_i += 1
        g = wguard[s]
        cd = ws.cast_done
        sem = f"atw{s}"

        def go(sy, ws=ws, m=m, s=s, g=g, cd=ws.cast_wait(m), sem=sem):
            if g > 0:
                sy.wait_ge(p.h('pe'), g)
            sy.wait_ge(p.h(cd[0]), cd[1])
            p.inc(sy.dma_start(out=wslot[s], in_=ws.block_ap(m)), sem, 16)
        p.block.sync(go)
        return s, (sem, p.v(sem))

    for t in range(ntiles):
        xw = xw_sb[t % 2]
        g_xw = xw_guard[t % 2]
        g_xf = xf_guard

        xw_sem = f"atxw{t % 2}"

        def loadx(sy, t=t, g_xw=g_xw, g_xf=g_xf, xw=xw, xw_sem=xw_sem):
            if x_ready is not None and x_ready[1] > 0:
                sy.wait_ge(p.h(x_ready[0]), x_ready[1])
            if g_xw > 0:
                sy.wait_ge(p.h('pe'), g_xw)
            if g_xf > 0:
                sy.wait_ge(p.h('ve'), g_xf)
            p.inc(sy.dma_start(out=xw.rearrange("p (c t) -> p c t", c=DC),
                               in_=x0bf_r[:, :, t * NT:t * NT + WH]), xw_sem, 16)
        p.block.sync(loadx)
        al_x = (xw_sem, p.v(xw_sem))
        al_f = al_x

        for m in range(DC):
            s, wlv = wload(wsv, m)
            gv0, gv1 = pg_guard_ac[0], pg_guard_ac[1]
            gve0, gve1 = pg_guard_ve[0], pg_guard_ve[1]

            def vmm(tt, s=s, wlv=wlv, al_x=al_x, gv0=gv0, gv1=gv1, gve0=gve0, gve1=gve1, xw=xw):
                tt.wait_ge(p.h(wlv[0]), wlv[1])
                tt.wait_ge(p.h(al_x[0]), al_x[1])
                for g in (gv0, gv1):
                    if g > 0:
                        tt.wait_ge(p.h('ac'), g)
                for g in (gve0, gve1):
                    if g > 0:
                        tt.wait_ge(p.h('ve'), g)
                for k in range(DC):
                    st = (k == 0)
                    sp = (k == DC - 1)
                    tt.matmul(pg[0][:, 0:512], wslot[s][:, k * 128:(k + 1) * 128],
                              xw[:, k * WH:k * WH + 512], start=st, stop=sp)
                    tt.matmul(pg[0][:, 512:1024], wslot[s][:, k * 128:(k + 1) * 128],
                              xw[:, k * WH + 512:k * WH + 1024], start=st, stop=sp)
                    ins = tt.matmul(pg[1][:, 0:64], wslot[s][:, k * 128:(k + 1) * 128],
                                    xw[:, k * WH + 1024:k * WH + 1088], start=st, stop=sp)
                p.inc(ins, 'pe')
            p.block.tensor(vmm)
            wguard[s] = p.v('pe')
            pe_v = p.v('pe')

            def vevac(v, m=m, pe_v=pe_v):
                v.wait_ge(p.h('pe'), pe_v)
                v.tensor_copy(vsb[:, m * WH:m * WH + 1024], pg[0][:, :])
                ins = v.tensor_copy(vsb[:, m * WH + 1024:m * WH + 1088], pg[1][:, 0:64])
                p.inc(ins, 've')
            p.block.vector(vevac)
            pg_guard_ve[0] = pg_guard_ve[1] = p.v('ve')
            pg_guard_ac[0] = pg_guard_ac[1] = 0

        acb_pe, acb_ac = accb_guard
        for m in range(DC):
            ac_base = p.v('ac')
            ve_base = p.v('ve')
            pe_g_j = []
            for j in range(9):
                gm = j * DC + m
                s, wlv = wload(wsg, gm)
                jp = j % 2
                g_ve = pg_guard_ve[jp]
                g_ac = pg_guard_ac[jp] if j < 2 else ac_base + (j - 2) + 1

                def gmm(tt, s=s, wlv=wlv, jp=jp, g_ac=g_ac, g_ve=g_ve, al_x=al_x, xw=xw):
                    tt.wait_ge(p.h(wlv[0]), wlv[1])
                    tt.wait_ge(p.h(al_x[0]), al_x[1])
                    if g_ac > 0:
                        tt.wait_ge(p.h('ac'), g_ac)
                    if g_ve > 0:
                        tt.wait_ge(p.h('ve'), g_ve)
                    for k in range(DC):
                        st = (k == 0)
                        sp = (k == DC - 1)
                        tt.matmul(pg[jp][:, 0:512], wslot[s][:, k * 128:(k + 1) * 128],
                                  xw[:, k * WH + HALO:k * WH + HALO + 512], start=st, stop=sp)
                        ins = tt.matmul(pg[jp][:, 512:1024], wslot[s][:, k * 128:(k + 1) * 128],
                                        xw[:, k * WH + HALO + 512:k * WH + HALO + 1024], start=st, stop=sp)
                    p.inc(ins, 'pe')
                p.block.tensor(gmm)
                wguard[s] = p.v('pe')
                pe_g_j.append(p.v('pe'))
                pg_guard_ve[jp] = 0

            mp9 = 0
            gsb_prev = gsb_guard[0]

            def sigs(sc, pe_g_j=tuple(pe_g_j), gsb_prev=gsb_prev, mp9=mp9):
                for j in range(9):
                    if gsb_prev > 0:
                        sc.wait_ge(p.h('ve'), gsb_prev + j + 1)
                    sc.wait_ge(p.h('pe'), pe_g_j[j])
                    ins = sc.activation(gsb[:, (mp9 + j) * NT:(mp9 + j + 1) * NT],
                                        pg[j % 2][:, :], AF.Sigmoid)
                    p.inc(ins, 'ac')
            p.block.scalar(sigs)
            pg_guard_ac[0] = p.v('ac')
            pg_guard_ac[1] = p.v('ac') - 1

            def gates(v, m=m, t=t, mp9=mp9, ac_base=ac_base, acb_pe=acb_pe, acb_ac=acb_ac):
                for j in range(9):
                    v.wait_ge(p.h('ac'), ac_base + j + 1)
                    gg = gsb[:, (mp9 + j) * NT:(mp9 + j + 1) * NT]
                    vv = vsb[:, m * WH + HALO + SHIFTS[j]:m * WH + HALO + SHIFTS[j] + NT]
                    if j == 0:
                        ins = v.tensor_tensor(accf[:, m * NT:(m + 1) * NT], gg, vv, ALU.mult)
                    elif j < 8:
                        v.tensor_tensor(gtmp[:, 0:NT], gg, vv, ALU.mult)
                        ins = v.tensor_tensor(accf[:, m * NT:(m + 1) * NT],
                                              accf[:, m * NT:(m + 1) * NT],
                                              gtmp[:, 0:NT], ALU.add)
                    else:
                        v.tensor_tensor(gtmp[:, 0:NT], gg, vv, ALU.mult)
                        if acb_pe > 0:
                            v.wait_ge(p.h('pe'), acb_pe)
                        if acb_ac > 0:
                            v.wait_ge(p.h('ac'), acb_ac)
                        ins = v.tensor_tensor(accb[:, (t % 2) * DC * NT + m * NT:(t % 2) * DC * NT + (m + 1) * NT],
                                              accf[:, m * NT:(m + 1) * NT],
                                              gtmp[:, 0:NT], ALU.add)
                    p.inc(ins, 've')
            p.block.vector(gates)
            gsb_guard[0] = ve_base
        xw_guard[t % 2] = p.v('pe')

        ve_gate_done = p.v('ve')
        al_this = al_f

        def make_ln(t=t, ve_gate_done=ve_gate_done, al_f=al_this, xw=xw, asl=(t % 2)):
            def flush():
                nonlocal xout_guard, accb_guard, rb_guard, xf_guard
                rb_pe, rb_ac = rb_guard
                for n in range(2):
                    acc_sl = [accb[:, asl * DC * NT + mc * NT + n * 512:
                                   asl * DC * NT + mc * NT + n * 512 + 512] for mc in range(DC)]
                    r_sl = [Rb[:, mc * NT + n * 512:mc * NT + n * 512 + 512] for mc in range(DC)]
                    o_sl = [xout_st[:, mc * NT + n * 512:mc * NT + n * 512 + 512] for mc in range(DC)]

                    def inner_norm(v, mu_ap, rs_ap, tmp, n=n, acc_sl=acc_sl, r_sl=r_sl,
                                   al_f=al_f, rb_pe=rb_pe, rb_ac=rb_ac, xw=xw):
                        v.wait_ge(p.h(al_f[0]), al_f[1])
                        if rb_pe > 0:
                            v.wait_ge(p.h('pe'), rb_pe)
                        if rb_ac > 0:
                            v.wait_ge(p.h('ac'), rb_ac)
                        for mc in range(DC):
                            v.tensor_tensor(tmp, acc_sl[mc], mu_ap, ALU.subtract)
                            v.scalar_tensor_tensor(t2f, tmp, cst['gamma'][:, mc:mc + 1], rs_ap,
                                                   ALU.mult, ALU.mult)
                            ins = v.scalar_tensor_tensor(
                                r_sl[mc], t2f, cst['beta'][:, mc:mc + 1],
                                xw[:, mc * WH + HALO + n * 512:mc * WH + HALO + n * 512 + 512],
                                ALU.add, ALU.add)
                        return ins

                    emit_ln512(p, cst, acc_sl, r_sl, pb, ln_state, norm_emit=inner_norm,
                               ve_r=(ve_gate_done if n == 0 else None))
                    if xout_guard > 0 and n == 0:
                        def vwait(v, g=xout_guard):
                            v.wait_ge(p.h('spx1'), g)
                        p.block.vector(vwait)
                    emit_ln512(p, cst, r_sl, o_sl, pb, ln_state)
                accb_guard = (p.v('pe'), p.v('ac'))
                rb_guard = accb_guard
                xf_guard = p.v('ve')
                ve_done = p.v('ve')

                def store(g, t=t, ve_done=ve_done):
                    g.wait_ge(p.h('ve'), ve_done)
                    p.inc(g.dma_start(out=x_out_r[:, :, t * NT:(t + 1) * NT],
                                      in_=xout_st.rearrange("p (c t) -> p c t", c=DC)), 'spx1', 16)
                p.block.gpsimd(store)
                xout_guard = p.v('spx1')
            return flush

        if pend_ln is not None:
            pend_ln()
        pend_ln = make_ln()

    pend_ln()
    return ('spx1', p.v('spx1'))


def emit_l0ffn(p, cst, ws1, ws2, x_in, x_out, stack, x_ready=None):
    nc = p.nc
    NT = 1024
    ntiles = TL // NT
    A = stack.enter_context(nc.sbuf_tensor("ff_A", [128, 3 * DC * NT], BF16)).ap()
    Bh = stack.enter_context(nc.sbuf_tensor("ff_B", [128, HC * NT], BF16)).ap()
    W = stack.enter_context(nc.sbuf_tensor("ff_W", [128, 8192], BF16)).ap()
    R = stack.enter_context(nc.sbuf_tensor("ff_R", [128, 8192], BF16)).ap()
    r2 = stack.enter_context(nc.sbuf_tensor("ff_r2", [128, 4096], BF16)).ap()
    pbx = [stack.enter_context(nc.psum_tensor(f"ff_p{i}", [128, 512], F32)).ap() for i in range(8)]
    pbank = pbx[:4]
    pb = {'sum': pbx[4], 'sq': pbx[5], 'mu': pbx[6], 'rs': pbx[7]}

    xin_ring = [A[:, 0:DC * NT], A[:, DC * NT:2 * DC * NT]]
    xout_st = A[:, 2 * DC * NT:3 * DC * NT]
    wslot = [W[:, 0:4096], W[:, 4096:8192]]
    x_in_r = x_in.ap().rearrange("c p t -> p c t")
    x_out_r = x_out.ap().rearrange("c p t -> p c t")

    xin_guard = [0, 0]
    wslot_guard = [0, 0]
    bank_guard = [('ac', 0)] * 4
    xout_guard = 0
    ln_state = {'r2': r2}

    def load_xtile(t):
        s = t % 2
        g = xin_guard[s]
        sem = f"ffx{s}"

        def go(sy, s=s, g=g, sem=sem, t=t):
            if x_ready is not None and x_ready[1] > 0:
                sy.wait_ge(p.h(x_ready[0]), x_ready[1])
            if g > 0:
                sy.wait_ge(p.h('ve'), g)
            p.inc(sy.dma_start(out=xin_ring[s].rearrange("p (c t) -> p c t", c=DC),
                               in_=x_in_r[:, :, t * NT:(t + 1) * NT]), sem, 16)
        p.block.sync(go)
        return (sem, p.v(sem))

    al_tile = [None] * (ntiles + 1)
    al_tile[0] = load_xtile(0)

    for t in range(ntiles):
        xin = xin_ring[t % 2]
        if t + 1 < ntiles:
            al_tile[t + 1] = load_xtile(t + 1)
        al_need = al_tile[t]

        for mg in range(0, HC, 4):
            sidx = (mg // 4) % 2
            wg = wslot_guard[sidx]
            cast_need = ws1.cast_done

            wsem = f"ffw{sidx}"

            def wload(sy, mg=mg, sidx=sidx, wg=wg, cast_need=ws1.cast_wait(mg + 3), ws=ws1, wsem=wsem):
                if wg > 0:
                    sy.wait_ge(p.h('pe'), wg)
                sy.wait_ge(p.h(cast_need[0]), cast_need[1])
                src = ws.scr.ap()[mg:mg + 4].rearrange("m p c -> p m c")
                dst = wslot[sidx].rearrange("p (m c) -> p m c", m=4)
                p.inc(sy.dma_start(out=dst, in_=src), wsem, 16)
            p.block.sync(wload)
            wl_here = (wsem, p.v(wsem))

            for mi in range(4):
                m = mg + mi
                wblk = wslot[sidx][:, mi * 1024:(mi + 1) * 1024]
                b0 = (m % 2) * 2

                def mm(tt, wblk=wblk, b0=b0, g0=bank_guard[b0], g1=bank_guard[b0 + 1],
                       wl_here=wl_here, al_need=al_need, xin=xin):
                    tt.wait_ge(p.h(wl_here[0]), wl_here[1])
                    tt.wait_ge(p.h(al_need[0]), al_need[1])
                    for gn, gv in (g0, g1):
                        if gv > 0:
                            tt.wait_ge(p.h(gn), gv)
                    for k in range(DC):
                        for n in range(2):
                            ins = tt.matmul(pbank[b0 + n][:, :],
                                            wblk[:, k * 128:(k + 1) * 128],
                                            xin[:, k * NT + n * 512:k * NT + (n + 1) * 512],
                                            start=(k == 0), stop=(k == DC - 1))
                    p.inc(ins, 'pe')
                p.block.tensor(mm)
                pe_here = p.v('pe')

                def evac(s, m=m, b0=b0, pe_here=pe_here, t=t):
                    s.wait_ge(p.h('pe'), pe_here)
                    for n in range(2):
                        ins = s.activation(Bh[:, m * NT + n * 512:m * NT + (n + 1) * 512],
                                           pbank[b0 + n][:, :], AF.Gelu,
                                           bias=cst['b1'][:, m:m + 1])
                    p.inc(ins, 'ac')
                p.block.scalar(evac)
                bank_guard[b0] = bank_guard[b0 + 1] = ('ac', p.v('ac'))
            wslot_guard[sidx] = p.v('pe')
        ac_h_done = p.v('ac')

        r_half = [[R[:, (n * DC + mc) * 512:(n * DC + mc) * 512 + 512] for mc in range(DC)]
                  for n in range(2)]
        ve_r_guard = ln_state.get('norm_done_ve', 0)
        for m2 in range(DC):
            sidx = m2 % 2
            wg = wslot_guard[sidx]

            wsem = f"ffw{sidx}"

            def wload2(sy, m2=m2, sidx=sidx, wg=wg, cast_need=ws2.cast_wait(m2), wsem=wsem):
                if wg > 0:
                    sy.wait_ge(p.h('pe'), wg)
                sy.wait_ge(p.h(cast_need[0]), cast_need[1])
                p.inc(sy.dma_start(out=wslot[sidx], in_=ws2.scr.ap()[m2]), wsem, 16)
            p.block.sync(wload2)
            wl_here = (wsem, p.v(wsem))
            b0 = (m2 % 2) * 2

            def mm2(tt, m2=m2, sidx=sidx, b0=b0, g0=bank_guard[b0], g1=bank_guard[b0 + 1],
                    wl_here=wl_here, ac_h_done=ac_h_done):
                tt.wait_ge(p.h(wl_here[0]), wl_here[1])
                tt.wait_ge(p.h('ac'), ac_h_done)
                for gn, gv in (g0, g1):
                    if gv > 0:
                        tt.wait_ge(p.h(gn), gv)
                for k2 in range(HC):
                    for n in range(2):
                        ins = tt.matmul(pbank[b0 + n][:, :],
                                        wslot[sidx][:, k2 * 128:(k2 + 1) * 128],
                                        Bh[:, k2 * NT + n * 512:k2 * NT + (n + 1) * 512],
                                        start=(k2 == 0), stop=(k2 == HC - 1))
                p.inc(ins, 'pe')
            p.block.tensor(mm2)
            pe_here = p.v('pe')

            def resid(v, m2=m2, b0=b0, pe_here=pe_here, xin=xin, ve_r_guard=ve_r_guard,
                      al_need=al_need):
                v.wait_ge(p.h('pe'), pe_here)
                v.wait_ge(p.h(al_need[0]), al_need[1])
                if ve_r_guard > 0:
                    v.wait_ge(p.h('ve'), ve_r_guard)
                for n in range(2):
                    ins = v.scalar_tensor_tensor(
                        r_half[n][m2], pbank[b0 + n][:, :], cst['b2'][:, m2:m2 + 1],
                        xin[:, m2 * NT + n * 512:m2 * NT + (n + 1) * 512],
                        ALU.add, ALU.add)
                p.inc(ins, 've')
            p.block.vector(resid)
            bank_guard[b0] = bank_guard[b0 + 1] = ('ve', p.v('ve'))
            wslot_guard[sidx] = pe_here
        xin_guard[t % 2] = p.v('ve')

        sp_guard = xout_guard
        for n in range(2):
            out_tiles = [xout_st[:, mc * NT + n * 512:mc * NT + (n + 1) * 512] for mc in range(DC)]
            if sp_guard > 0 and n == 0:
                def vwait(v, sp_guard=sp_guard):
                    v.wait_ge(p.h('spx2'), sp_guard)
                p.block.vector(vwait)
            emit_ln512(p, cst, r_half[n], out_tiles, pb, ln_state)
        ve_done = p.v('ve')

        DBG['xin'] = xin
        DBG['Bh'] = Bh
        DBG['R'] = R

        def store(g, t=t, ve_done=ve_done):
            g.wait_ge(p.h('ve'), ve_done)
            p.inc(g.dma_start(out=x_out_r[:, :, t * NT:(t + 1) * NT],
                              in_=xout_st.rearrange("p (c t) -> p c t", c=DC)), 'spx2', 16)
        p.block.gpsimd(store)
        xout_guard = p.v('spx2')

    return ('spx2', p.v('spx2'))


def emit_l1attn(p, cst, wsqkv, wsproj, x_in, x_out, stack, tag="", groups=None):
    nc = p.nc
    NH, DH, CH = 16, 64, 128
    NC_ = TL // CH
    PR = 8
    kT_hbm = nc.dram_tensor(f"l1a{tag}_kT", [DC, 128, TL], BF16)
    v_hbm = nc.dram_tensor(f"l1a{tag}_v", [NC_, 128, 1024], BF16)
    cc_in = nc.dram_tensor(f"l1a{tag}_ccin", [128, 520], F32)
    cc_out = nc.dram_tensor(f"l1a{tag}_ccout", [128, 520], F32)

    qT = stack.enter_context(nc.sbuf_tensor("aa_qT", [128, PR * TL], BF16)).ap()
    ckpt = stack.enter_context(nc.sbuf_tensor("aa_ck", [128, NC_ * 520], BF16)).ap()
    qkw = stack.enter_context(nc.sbuf_tensor("aa_qkw", [128, 2 * 1024], BF16)).ap()
    vw = stack.enter_context(nc.sbuf_tensor("aa_vw", [128, 8 * 1024], BF16)).ap()
    x2g = vw
    kcr = stack.enter_context(nc.sbuf_tensor("aa_kcr", [128, 2 * 1024], BF16)).ap()
    vcr = stack.enter_context(nc.sbuf_tensor("aa_vcr", [128, 2 * 1024], BF16)).ap()
    ktm = stack.enter_context(nc.sbuf_tensor("aa_ktm", [128, 2 * 1024], BF16)).ap()
    ypair = stack.enter_context(nc.sbuf_tensor("aa_yp", [128, 2 * 1024], BF16)).ap()
    yTg = stack.enter_context(nc.sbuf_tensor("aa_yTg", [128, PR * 512], BF16)).ap()
    x2r = stack.enter_context(nc.sbuf_tensor("aa_x2r", [128, DC * 512], BF16)).ap()
    r2 = stack.enter_context(nc.sbuf_tensor("aa_r2", [128, 4096], BF16)).ap()
    ssb = stack.enter_context(nc.sbuf_tensor("aa_ssb", [128, 2 * 128], BF16)).ap()
    S32 = stack.enter_context(nc.sbuf_tensor("aa_S", [128, 520], F32)).ap()
    Sini = stack.enter_context(nc.sbuf_tensor("aa_Si", [128, 520], BF16)).ap()
    ccs = stack.enter_context(nc.sbuf_tensor("aa_ccs", [128, 520], F32)).ap()
    recv = stack.enter_context(nc.sbuf_tensor("aa_rcv", [128, 520], F32)).ap()
    drow = stack.enter_context(nc.sbuf_tensor("aa_dr", [128, 4], F32)).ap()
    Wp = stack.enter_context(nc.sbuf_tensor("aa_Wp", [128, 2 * 1024], BF16)).ap()

    partA_stack = ExitStack()
    x2c = partA_stack.enter_context(nc.sbuf_tensor("aa_x2c", [128, 2 * 1024], BF16)).ap()
    vstage = partA_stack.enter_context(nc.sbuf_tensor("aa_vst", [128, 2 * 1024], BF16)).ap()
    kstage = partA_stack.enter_context(nc.sbuf_tensor("aa_kst", [128, 2 * 512], BF16)).ap()
    msb = partA_stack.enter_context(nc.sbuf_tensor("aa_msb", [128, 512], F32)).ap()
    esb = partA_stack.enter_context(nc.sbuf_tensor("aa_esb", [128, 512], F32)).ap()
    pa = [stack.enter_context(nc.psum_tensor(f"aa_pa{i}", [128, 512], F32)).ap() for i in range(4)]
    pq = [stack.enter_context(nc.psum_tensor(f"aa_pq{i}", [128, 1024], BF16)).ap() for i in range(2)]
    pv_stack = ExitStack()
    pv = pv_stack.enter_context(nc.psum_tensor("aa_pv", [128, 1024], F32)).ap()
    ln_state = {'r2': r2}

    x_in_r = x_in.ap().rearrange("c p t -> p c t")
    x_out_r = x_out.ap().rearrange("c p t -> p c t")
    kT_r = kT_hbm.ap().rearrange("c p t -> p c t")

    pa_guard = [('pe', 0), ('pe', 0), ('pe', 0), ('pe', 0)]
    pq_guard = [0, 0]

    def load_vw(sy):
        cd = wsqkv.cast_wait(23)
        sy.wait_ge(p.h(cd[0]), cd[1])
        for m in range(8):
            p.inc(sy.dma_start(out=vw[:, m * 1024:(m + 1) * 1024],
                               in_=wsqkv.block_ap(16 + m)), 'avw', 16)
    p.block.sync(load_vw)
    wl_vw = p.v('avw')

    x2c_guard = [0, 0]
    vst_guard = [0, 0]
    pv_guard = 0
    for tc in range(NC_):
        s = tc % 2
        g = x2c_guard[s]

        xsem = f"axc{s}"

        def lx(sy, tc=tc, s=s, g=g, xsem=xsem):
            if g > 0:
                sy.wait_ge(p.h('pe'), g)
            p.inc(sy.dma_start(out=x2c[:, s * 1024:(s + 1) * 1024].rearrange("p (k t) -> p k t", k=DC),
                               in_=x_in_r[:, :, tc * 128:(tc + 1) * 128]), xsem, 16)
        p.block.sync(lx)
        al_h = (xsem, p.v(xsem))
        gv = pv_guard

        def mmv(tt, s=s, al_h=al_h, gv=gv):
            tt.wait_ge(p.h('avw'), wl_vw)
            tt.wait_ge(p.h(al_h[0]), al_h[1])
            if gv > 0:
                tt.wait_ge(p.h('ve'), gv)
            for k in range(DC):
                for n in range(2):
                    ins = tt.matmul(
                        pv[:, n * 512:(n + 1) * 512],
                        x2c[:, s * 1024 + k * 128:s * 1024 + (k + 1) * 128],
                        vw.rearrange("p (m t) -> p m t", m=8)[:, n * 4:(n + 1) * 4,
                                                             k * 128:(k + 1) * 128],
                        start=(k == 0), stop=(k == DC - 1))
            p.inc(ins, 'pe')
        p.block.tensor(mmv)
        x2c_guard[s] = p.v('pe')
        pe_v = p.v('pe')
        gs = vst_guard[s]

        ssem = f"asv{s}"

        def ev(v, s=s, pe_v=pe_v, gs=gs, ssem=ssem):
            v.wait_ge(p.h('pe'), pe_v)
            if gs > 0:
                v.wait_ge(p.h(ssem), gs)
            ins = v.tensor_copy(vstage[:, s * 1024:(s + 1) * 1024], pv[:, :])
            p.inc(ins, 've')
        p.block.vector(ev)
        pv_guard = p.v('ve')
        ve_v = p.v('ve')

        def spl(g, tc=tc, s=s, ve_v=ve_v, ssem=ssem):
            g.wait_ge(p.h('ve'), ve_v)
            p.inc(g.dma_start(out=v_hbm.ap()[tc], in_=vstage[:, s * 1024:(s + 1) * 1024]),
                  ssem, 16)
        p.block.gpsimd(spl)
        vst_guard[s] = p.v(ssem)

    pv_stack.close()
    pb = {'sum': pa[0][0:1, 0:512], 'sq': pa[1][0:1, 0:512],
          'mu': pa[2][:, :], 'rs': pa[3][:, :]}

    pe_vdone = p.v('pe')
    x2g_guard = [pe_vdone, pe_vdone]
    kst_guard = [0, 0]
    msb_guard = 0
    esb_guard = 0
    qkw_guard = [0, 0]
    for g in range(8):
        s = g % 2
        gg = x2g_guard[s]

        gsem = f"axg{s}"

        def lg(sy, g=g, s=s, gg=gg, gsem=gsem):
            if gg > 0:
                sy.wait_ge(p.h('pe'), gg)
            p.inc(sy.dma_start(out=x2g[:, s * 4096:(s + 1) * 4096].rearrange("p (k t) -> p k t", k=DC),
                               in_=x_in_r[:, :, g * 512:(g + 1) * 512]), gsem, 16)
        p.block.sync(lg)
        al_g = (gsem, p.v(gsem))

        for m in range(16):
            ws_ = m % 2
            gq = qkw_guard[ws_]

            qsem = f"aqk{ws_}"

            def lw(sy, m=m, ws_=ws_, gq=gq, qsem=qsem, cd=wsqkv.cast_wait(m)):
                if gq > 0:
                    sy.wait_ge(p.h('pe'), gq)
                sy.wait_ge(p.h(cd[0]), cd[1])
                p.inc(sy.dma_start(out=qkw[:, ws_ * 1024:(ws_ + 1) * 1024],
                                   in_=wsqkv.block_ap(m)), qsem, 16)
            p.block.sync(lw)
            wl_h = (qsem, p.v(qsem))
            bk = m % 4
            gn, gvv = pa_guard[bk]

            def mmq(tt, s=s, m=m, ws_=ws_, bk=bk, gn=gn, gvv=gvv, wl_h=wl_h, al_g=al_g):
                tt.wait_ge(p.h(wl_h[0]), wl_h[1])
                tt.wait_ge(p.h(al_g[0]), al_g[1])
                if gvv > 0:
                    tt.wait_ge(p.h(gn), gvv)
                for k in range(DC):
                    ins = tt.matmul(pa[bk][:, :],
                                    qkw[:, ws_ * 1024 + k * 128:ws_ * 1024 + (k + 1) * 128],
                                    x2g[:, s * 4096 + k * 512:s * 4096 + (k + 1) * 512],
                                    start=(k == 0), stop=(k == DC - 1))
                p.inc(ins, 'pe')
            p.block.tensor(mmq)
            qkw_guard[ws_] = p.v('pe')
            pe_q = p.v('pe')
            g_m = msb_guard

            def vmin(v, bk=bk, pe_q=pe_q, g_m=g_m):
                v.wait_ge(p.h('pe'), pe_q)
                if g_m > 0:
                    v.wait_ge(p.h('ac'), g_m)
                ins = v.tensor_scalar(msb, pa[bk][:, :], 0.0, None, ALU.min)
                p.inc(ins, 've')
            p.block.vector(vmin)
            ve_m = p.v('ve')

            g_e = esb_guard

            def aexp(sc, ve_m=ve_m, g_e=g_e):
                sc.wait_ge(p.h('ve'), ve_m)
                if g_e > 0:
                    sc.wait_ge(p.h('ve'), g_e)
                ins = sc.activation(esb, msb, AF.Exp)
                p.inc(ins, 'ac')
            p.block.scalar(aexp)
            msb_guard = p.v('ac')
            ac_e = p.v('ac')

            if m < 8:
                dst = qT[:, m * TL + g * 512:m * TL + (g + 1) * 512]
            else:
                dst = kstage[:, s * 512:(s + 1) * 512]

            def vphi(v, bk=bk, ac_e=ac_e, dst=dst, is_k=(m >= 8),
                     gk=(kst_guard[s] if m >= 8 else 0), ksem=f"ask{s}"):
                v.wait_ge(p.h('ac'), ac_e)
                if is_k and gk > 0:
                    v.wait_ge(p.h(ksem), gk)
                ins = v.scalar_tensor_tensor(dst, pa[bk][:, :], 1.0, esb, ALU.add, ALU.max)
                p.inc(ins, 've')
            p.block.vector(vphi)
            pa_guard[bk] = ('ve', p.v('ve'))
            esb_guard = p.v('ve')
            ve_phi = p.v('ve')

            if m >= 8:
                def spk(gp, m=m, g=g, s=s, ve_phi=ve_phi, ksem=f"ask{s}"):
                    gp.wait_ge(p.h('ve'), ve_phi)
                    p.inc(gp.dma_start(out=kT_hbm.ap()[m - 8][:, g * 512:(g + 1) * 512],
                                       in_=kstage[:, s * 512:(s + 1) * 512]), ksem, 16)
                p.block.gpsimd(spk)
                kst_guard[s] = p.v(f"ask{s}")
        x2g_guard[s] = p.v('pe')
    sp_partA = [('asv0', p.v('asv0')), ('asv1', p.v('asv1')),
                ('ask0', p.v('ask0')), ('ask1', p.v('ask1'))]
    if DBG.get('attn_kv_tap') is not None:
        kvt = DBG['attn_kv_tap']

        def dkv(gp, sp_partA=tuple(sp_partA)):
            for sn, sv in sp_partA:
                if sv > 0:
                    gp.wait_ge(p.h(sn), sv)
            p.inc(gp.dma_start(out=kvt[0].rearrange("p (m t) -> p m t", m=DC),
                               in_=kT_r[:, :, 0:128]), 'sp', 16)
            p.inc(gp.dma_start(out=kvt[1], in_=v_hbm.ap()[0]), 'sp', 16)
            p.inc(gp.dma_start(out=kvt[2].rearrange("p (m t) -> p m t", t=128),
                               in_=qT.rearrange("p (m t) -> p m t", m=PR)[:, :, 0:128]), 'sp', 16)
        p.block.gpsimd(dkv)

    partA_stack.close()
    Rb = stack.enter_context(nc.sbuf_tensor("aa_R", [128, 2 * DC * 512], BF16)).ap()
    xo3 = stack.enter_context(nc.sbuf_tensor("aa_xo3", [128, DC * 512], BF16)).ap()

    def zeroS(g):
        ins = g.memset(S32, 0.0)
        p.inc(ins, 'pool')
    p.block.gpsimd(zeroS)
    pool_z = p.v('pool')

    def zwait(v):
        v.wait_ge(p.h('pool'), pool_z)
    p.block.vector(zwait)

    kcr_guard = [0, 0]
    vcr_guard = [0, 0]
    ktm_guard = [0, 0]
    ps_guard = [('pe', 0), ('pe', 0), ('pe', 0), ('pe', 0)]
    for c in range(NC_):
        s = c % 2
        g1, g2 = kcr_guard[s], vcr_guard[s]

        ksem, vsem = f"akc{s}", f"avc{s}"

        def lkv(sy, c=c, s=s, g1=g1, g2=g2, ksem=ksem, vsem=vsem, spA=tuple(sp_partA)):
            for sn, sv in spA:
                if sv > 0:
                    sy.wait_ge(p.h(sn), sv)
            if g1 > 0:
                sy.wait_ge(p.h('pe'), g1)
            p.inc(sy.dma_start(out=kcr[:, s * 1024:(s + 1) * 1024].rearrange("p (m t) -> p m t", m=DC),
                               in_=kT_r[:, :, c * 128:(c + 1) * 128]), ksem, 16)
            if g2 > 0:
                sy.wait_ge(p.h('pe'), g2)
            p.inc(sy.dma_start(out=vcr[:, s * 1024:(s + 1) * 1024], in_=v_hbm.ap()[c]), vsem, 16)
        p.block.sync(lkv)
        al_c = [(ksem, p.v(ksem)), (vsem, p.v(vsem))]
        gq_ = pq_guard[s]

        def ktr(tt, c=c, s=s, al_c=tuple(al_c), gq_=gq_):
            for sn, sv in al_c:
                tt.wait_ge(p.h(sn), sv)
            if gq_ > 0:
                tt.wait_ge(p.h('ve'), gq_)
            for m in range(PR):
                ins = tt.transpose(pq[s][:, m * 128:(m + 1) * 128],
                                   kcr[:, s * 1024 + m * 128:s * 1024 + (m + 1) * 128],
                                   cst['identbf'])
            v_ = p.inc(ins, 'pe')
            _pedrain(p, tt, v_)
        p.block.tensor(ktr)
        pe_tr = p.v('pe')

        def kev(v, s=s, pe_tr=pe_tr):
            v.wait_ge(p.h('pe'), pe_tr)
            ins = v.tensor_copy(ktm[:, s * 1024:(s + 1) * 1024], pq[s][:, :])
            p.inc(ins, 've')
        p.block.vector(kev)
        pq_guard[s] = p.v('ve')
        ktm_guard[s] = 0
        ve_k = p.v('ve')

        for pr in range(PR):
            bk = pr % 4
            gn, gv_ = ps_guard[bk]

            def smm(tt, c=c, s=s, pr=pr, bk=bk, gn=gn, gv_=gv_, ve_k=ve_k, al_c=tuple(al_c)):
                tt.wait_ge(p.h('ve'), ve_k)
                for sn, sv in al_c:
                    tt.wait_ge(p.h(sn), sv)
                if gv_ > 0:
                    tt.wait_ge(p.h(gn), gv_)
                kb = ktm[:, s * 1024 + pr * 128:s * 1024 + pr * 128 + 64]
                ko = ktm[:, s * 1024 + pr * 128 + 64:s * 1024 + pr * 128 + 128]
                ve_ = vcr[:, s * 1024 + (2 * pr) * 64:s * 1024 + (2 * pr) * 64 + 64]
                vo = vcr[:, s * 1024 + (2 * pr + 1) * 64:s * 1024 + (2 * pr + 1) * 64 + 64]
                tt.matmul(pa[bk][0:64, 0:64], kb, ve_, start=True, stop=False)
                tt.matmul(pa[bk][0:64, 64:65], kb, cst['ones_bf'], start=False, stop=True)
                tt.matmul(pa[bk][64:128, 0:64], ko, vo, start=True, stop=False,
                          tile_position=(0, 64))
                ins = tt.matmul(pa[bk][64:128, 64:65], ko, cst['ones_bf'], start=False,
                                stop=True, tile_position=(0, 64))
                v_ = p.inc(ins, 'pe')
                _pedrain(p, tt, v_)
            p.block.tensor(smm)
            pe_s = p.v('pe')

            def sadd(v, c=c, pr=pr, bk=bk, pe_s=pe_s):
                v.tensor_copy(ckpt[:, c * 520 + pr * 65:c * 520 + (pr + 1) * 65],
                              S32[:, pr * 65:(pr + 1) * 65])
                v.wait_ge(p.h('pe'), pe_s)
                ins = v.tensor_tensor(S32[:, pr * 65:(pr + 1) * 65],
                                      S32[:, pr * 65:(pr + 1) * 65],
                                      pa[bk][:, 0:65], ALU.add)
                p.inc(ins, 've')
            p.block.vector(sadd)
            ps_guard[bk] = ('ve', p.v('ve'))
        kcr_guard[s] = vcr_guard[s] = p.v('pe')
    pa_guard = ps_guard

    if DBG.get('ck_tap') is not None:
        ckt = DBG['ck_tap']
        vck = p.v('ve')

        def dck(gp, vck=vck):
            gp.wait_ge(p.h('ve'), vck)
            p.inc(gp.dma_start(out=ckt[0], in_=ckpt[:, 0:1040]), 'sp', 16)
            p.inc(gp.dma_start(out=ckt[1][:, 0:520], in_=S32), 'sp', 16)
        p.block.gpsimd(dck)

    def vccs(v):
        ins = v.tensor_scalar(ccs, S32, cst['rolei'], None, ALU.mult)
        p.inc(ins, 've')
    p.block.vector(vccs)
    ve_ccs = p.v('ve')

    def stccs(sy, ve_ccs=ve_ccs):
        sy.wait_ge(p.h('ve'), ve_ccs)
        p.inc(sy.dma_start(out=cc_in.ap(), in_=ccs), 'accs', 16)
    p.block.sync(stccs)
    al_ccs = p.v('accs')

    def docc(gp, al_ccs=al_ccs):
        gp.wait_ge(p.h('accs'), al_ccs)
        p.inc(gp.collective_compute(
            "AllReduce", ALU.add,
            replica_groups=(groups if groups is not None else [[0, 1], [2, 3], [4, 5], [6, 7]]),
            ins=[cc_in.ap().opt()],
            outs=[cc_out.ap().opt()]), 'cc', 1)
    p.block.gpsimd(docc)
    cc_v = p.v('cc')

    def ldcc(sy, cc_v=cc_v):
        sy.wait_ge(p.h('cc'), cc_v)
        p.inc(sy.dma_start(out=recv, in_=cc_out.ap()), 'arcv', 16)
    p.block.sync(ldcc)
    al_rcv = p.v('arcv')

    def vsini(v, al_rcv=al_rcv):
        v.wait_ge(p.h('arcv'), al_rcv)
        ins = v.tensor_scalar(Sini, recv, cst['role'], None, ALU.mult)
        p.inc(ins, 've')
    p.block.vector(vsini)
    ve_sini = p.v('ve')

    x2r_guard = 0
    pend_ln3 = None
    rb3_guard = [0, 0]
    yp_guard = [0, 0]
    ssb_guard = [0, 0]
    dr_guard = [0, 0]
    Wp_guard = [0, 0]
    wp_i = 0
    yTg_guard = 0
    xo_guard = 0
    for g in range(8):
        gx = x2r_guard

        def lxr(sy, g=g, gx=gx):
            if gx > 0:
                sy.wait_ge(p.h('ve'), gx)
            p.inc(sy.dma_start(out=x2r.rearrange("p (k t) -> p k t", k=DC),
                               in_=x_in_r[:, :, g * 512:(g + 1) * 512]), 'axr', 16)
        p.block.sync(lxr)
        al_xr = p.v('axr')

        for ci in range(4):
            c = g * 4 + ci
            s = c % 2
            g1, g2 = kcr_guard[s], vcr_guard[s]

            ksem, vsem = f"akc{s}", f"avc{s}"

            def lkv2(sy, c=c, s=s, g1=g1, g2=g2, ksem=ksem, vsem=vsem):
                if g1 > 0:
                    sy.wait_ge(p.h('pe'), g1)
                p.inc(sy.dma_start(out=kcr[:, s * 1024:(s + 1) * 1024].rearrange("p (m t) -> p m t", m=DC),
                                   in_=kT_r[:, :, c * 128:(c + 1) * 128]), ksem, 16)
                if g2 > 0:
                    sy.wait_ge(p.h('pe'), g2)
                p.inc(sy.dma_start(out=vcr[:, s * 1024:(s + 1) * 1024], in_=v_hbm.ap()[c]), vsem, 16)
            p.block.sync(lkv2)
            al_c = [(ksem, p.v(ksem)), (vsem, p.v(vsem))]

            def emit_scores(hd):
                pr, hf = hd // 2, hd % 2
                base = hf * 64
                bk = hd % 4
                gn, gv_ = pa_guard[bk]
                tp = (64, 0) if hf else None

                def smm2(tt, c=c, s=s, pr=pr, hf=hf, base=base, bk=bk, gn=gn, gv_=gv_,
                         al_c=tuple(al_c), tp=tp, first=(c == 0 and hd == 0), ve_sini=ve_sini):
                    if first:
                        tt.wait_ge(p.h('ve'), ve_sini)
                    for sn, sv in al_c:
                        tt.wait_ge(p.h(sn), sv)
                    if gv_ > 0:
                        tt.wait_ge(p.h(gn), gv_)
                    ins = tt.matmul(pa[bk][:, 0:128],
                                    kcr[base:base + 64, s * 1024 + pr * 128:s * 1024 + (pr + 1) * 128],
                                    qT[base:base + 64, pr * TL + c * 128:pr * TL + (c + 1) * 128],
                                    start=True, stop=True, tile_position=tp)
                    p.inc(ins, 'pe')
                p.block.tensor(smm2)
                pe_sc = p.v('pe')
                gsb_ = ssb_guard[hd % 2]

                def mask(v, bk=bk, hh=hd % 2, pe_sc=pe_sc, gsb_=gsb_):
                    v.wait_ge(p.h('pe'), pe_sc)
                    if gsb_ > 0:
                        v.wait_ge(p.h('pe'), gsb_)
                    ins = v.tensor_tensor(ssb[:, hh * 128:(hh + 1) * 128], cst['maskT'],
                                          pa[bk][:, 0:128], ALU.mult)
                    p.inc(ins, 've')
                p.block.vector(mask)
                return p.v('ve')

            def emit_numdiv(hd, ve_msk):
                pr, hf = hd // 2, hd % 2
                base = hf * 64
                bk = hd % 4
                tp = (64, 0) if hf else None

                def nmm(tt, c=c, s=s, pr=pr, hf=hf, hd=hd, base=base, bk=bk, hh=hd % 2,
                        ve_msk=ve_msk, tp=tp):
                    tt.wait_ge(p.h('ve'), ve_msk)
                    tt.matmul(pa[bk][:, 0:64], ssb[:, hh * 128:(hh + 1) * 128],
                              vcr[:, s * 1024 + hd * 64:s * 1024 + (hd + 1) * 64],
                              start=True, stop=False)
                    tt.matmul(pa[bk][:, 64:65], ssb[:, hh * 128:(hh + 1) * 128],
                              cst['ones_bf'], start=False, stop=False)
                    qsl = qT[base:base + 64, pr * TL + c * 128:pr * TL + (c + 1) * 128]
                    tt.matmul(pa[bk][:, 0:65], qsl,
                              ckpt[base:base + 64, c * 520 + pr * 65:c * 520 + (pr + 1) * 65],
                              start=False, stop=False, tile_position=tp)
                    ins = tt.matmul(pa[bk][:, 0:65], qsl,
                                    Sini[base:base + 64, pr * 65:(pr + 1) * 65],
                                    start=False, stop=True, tile_position=tp)
                    p.inc(ins, 'pe')
                p.block.tensor(nmm)
                pe_n = p.v('pe')

                def vden(v, bk=bk, hh=hd % 2, pe_n=pe_n):
                    v.wait_ge(p.h('pe'), pe_n)
                    c1 = p.inc(v.tensor_scalar(drow[:, hh * 2:hh * 2 + 1], pa[bk][:, 64:65],
                                               1e-6, None, ALU.add), 've')
                    v.wait_ge(p.h('ve'), c1)
                    ins = v.reciprocal(drow[:, hh * 2 + 1:hh * 2 + 2],
                                       drow[:, hh * 2:hh * 2 + 1])
                    p.inc(ins, 've')
                p.block.vector(vden)
                ve_d = p.v('ve')
                gyp = yp_guard[s] if hd == 0 else 0

                def ydiv(v, bk=bk, hh=hd % 2, s=s, pr=pr, base=base, ve_d=ve_d, gyp=gyp):
                    v.wait_ge(p.h('ve'), ve_d)
                    if gyp > 0:
                        v.wait_ge(p.h('pe'), gyp)
                    ins = v.tensor_scalar(
                        ypair[:, s * 1024 + pr * 128 + base:s * 1024 + pr * 128 + base + 64],
                        pa[bk][:, 0:64], drow[:, hh * 2 + 1:hh * 2 + 2], None, ALU.mult)
                    p.inc(ins, 've')
                p.block.vector(ydiv)
                pa_guard[bk] = ('ve', p.v('ve'))
                ssb_guard[hd % 2] = pe_n

            vmsk = [0] * NH
            for hd in range(NH):
                vmsk[hd] = emit_scores(hd)
                if hd >= 1:
                    emit_numdiv(hd - 1, vmsk[hd - 1])
            emit_numdiv(NH - 1, vmsk[NH - 1])
            kcr_guard[s] = vcr_guard[s] = p.v('pe')
            ac_y = p.v('ve')
            tapv = 0
            if DBG.get('yp_tap') is not None and c == DBG['attn_c']:
                tap2 = DBG['yp_tap']

                def dyp(v, s=s, ac_y=ac_y):
                    v.wait_ge(p.h('ve'), ac_y)
                    ins = v.tensor_copy(tap2, ypair[:, s * 1024:(s + 1) * 1024])
                    p.inc(ins, 've')
                p.block.vector(dyp)
                tapv = p.v('ve')
            gq_ = pq_guard[s]

            def ytr(tt, s=s, ac_y=ac_y, gq_=gq_, tapv=tapv):
                tt.wait_ge(p.h('ve'), ac_y)
                if tapv > 0:
                    tt.wait_ge(p.h('ve'), tapv)
                if gq_ > 0:
                    tt.wait_ge(p.h('ve'), gq_)
                for pr2 in range(PR):
                    ins = tt.transpose(pq[s][:, pr2 * 128:(pr2 + 1) * 128],
                                       ypair[:, s * 1024 + pr2 * 128:s * 1024 + (pr2 + 1) * 128],
                                       cst['identbf'])
                v_ = p.inc(ins, 'pe')
                _pedrain(p, tt, v_)
            p.block.tensor(ytr)
            yp_guard[s] = p.v('pe')
            pe_y = p.v('pe')
            gyt = yTg_guard if ci == 0 else 0
            gsp = xo_guard if ci == 0 else 0

            def yev(v, s=s, ci=ci, pe_y=pe_y, gyt=gyt, gsp=gsp):
                v.wait_ge(p.h('pe'), pe_y)
                if gyt > 0:
                    v.wait_ge(p.h('pe'), gyt)
                if gsp > 0:
                    v.wait_ge(p.h('spx3'), gsp)
                ins = v.tensor_copy(
                    yTg.rearrange("p (m t) -> p m t", m=PR)[:, :, ci * 128:(ci + 1) * 128],
                    pq[s][:, :].rearrange("p (m t) -> p m t", m=PR))
                p.inc(ins, 've')
            p.block.vector(yev)
            pq_guard[s] = p.v('ve')
        ve_y = p.v('ve')

        r_sl = [Rb[:, (g % 2) * DC * 512 + mc * 512:(g % 2) * DC * 512 + (mc + 1) * 512]
                for mc in range(DC)]
        for m in range(DC):
            ws_ = wp_i % 2
            wp_i += 1
            gw = Wp_guard[ws_]

            psem = f"awp{ws_}"

            def lwp(sy, m=m, ws_=ws_, gw=gw, psem=psem, cd=wsproj.cast_wait(m)):
                if gw > 0:
                    sy.wait_ge(p.h('pe'), gw)
                sy.wait_ge(p.h(cd[0]), cd[1])
                p.inc(sy.dma_start(out=Wp[:, ws_ * 1024:(ws_ + 1) * 1024],
                                   in_=wsproj.block_ap(m)), psem, 16)
            p.block.sync(lwp)
            wl_p = (psem, p.v(psem))
            bk = m % 4
            gn, gv_ = pa_guard[bk]

            def pmm(tt, m=m, ws_=ws_, bk=bk, gn=gn, gv_=gv_, wl_p=wl_p, ve_y=ve_y):
                tt.wait_ge(p.h(wl_p[0]), wl_p[1])
                tt.wait_ge(p.h('ve'), ve_y)
                if gv_ > 0:
                    tt.wait_ge(p.h(gn), gv_)
                for kp in range(PR):
                    ins = tt.matmul(pa[bk][:, :],
                                    Wp[:, ws_ * 1024 + kp * 128:ws_ * 1024 + (kp + 1) * 128],
                                    yTg[:, kp * 512:(kp + 1) * 512],
                                    start=(kp == 0), stop=(kp == PR - 1))
                v_ = p.inc(ins, 'pe')
                _pedrain(p, tt, v_)
            p.block.tensor(pmm)
            Wp_guard[ws_] = p.v('pe')
            pe_p = p.v('pe')

            g_rb3 = rb3_guard[g % 2]

            def resid(v, m=m, bk=bk, pe_p=pe_p, al_xr=al_xr, g_rb3=g_rb3):
                v.wait_ge(p.h('pe'), pe_p)
                v.wait_ge(p.h('axr'), al_xr)
                if g_rb3 > 0:
                    v.wait_ge(p.h('pe'), g_rb3)
                ins = v.tensor_tensor(r_sl[m], x2r[:, m * 512:(m + 1) * 512],
                                      pa[bk][:, :], ALU.add)
                p.inc(ins, 've')
            p.block.vector(resid)
            pa_guard[bk] = ('ve', p.v('ve'))
        x2r_guard = p.v('ve')
        yTg_guard = p.v('pe')
        ve_resid_done = p.v('ve')

        def make_ln3(g=g, gsl=(g % 2), ve_resid_done=ve_resid_done):
            def flush():
                nonlocal xo_guard, yTg_guard
                r_sl2 = [Rb[:, gsl * DC * 512 + mc * 512:gsl * DC * 512 + (mc + 1) * 512]
                         for mc in range(DC)]
                o_sl = [xo3[:, mc * 512:(mc + 1) * 512] for mc in range(DC)]
                ln_state['rows_read_ve'] = max(ln_state.get('rows_read_ve', 0), p.v('ve'))
                ln_state['norm_done_ve'] = max(ln_state.get('norm_done_ve', 0), p.v('ve'))
                if xo_guard > 0:
                    def vwait(v, g_=xo_guard):
                        v.wait_ge(p.h('spx3'), g_)
                    p.block.vector(vwait)
                emit_ln512(p, cst, r_sl2, o_sl, pb, ln_state, ve_r=ve_resid_done)
                rb3_guard[gsl] = p.v('pe')
                for i in range(4):
                    pa_guard[i] = ('ve', p.v('ve'))
                ve_done = p.v('ve')

                def store(gp, g=g, ve_done=ve_done):
                    gp.wait_ge(p.h('ve'), ve_done)
                    p.inc(gp.dma_start(out=x_out_r[:, :, g * 512:(g + 1) * 512],
                                       in_=xo3.rearrange("p (c t) -> p c t", c=DC)), 'spx3', 16)
                p.block.gpsimd(store)
                xo_guard = p.v('spx3')
            return flush

        if pend_ln3 is not None:
            pend_ln3()
        pend_ln3 = make_ln3()
    pend_ln3()

    return ('spx3', p.v('spx3'))


def emit_l1ffn(p, cst, wsf1, wsf2, wsh1, wsh2, wsf3, x_in, y_out, stack, x_ready=None):
    nc = p.nc
    NT = 1024
    ntiles = TL // NT
    A = stack.enter_context(nc.sbuf_tensor("lf_A", [128, HC * NT], BF16)).ap()
    Bh = stack.enter_context(nc.sbuf_tensor("lf_B", [128, HC * NT], BF16)).ap()
    X3 = stack.enter_context(nc.sbuf_tensor("lf_x", [128, DC * NT], BF16)).ap()
    Wb = stack.enter_context(nc.sbuf_tensor("lf_W", [128, 2 * 4096], BF16)).ap()
    Rb = stack.enter_context(nc.sbuf_tensor("lf_R", [128, DC * NT], BF16)).ap()
    sil = stack.enter_context(nc.sbuf_tensor("lf_sil", [128, 2 * NT], BF16)).ap()
    yst = stack.enter_context(nc.sbuf_tensor("lf_y", [128, 512], F32)).ap()
    rowb = stack.enter_context(nc.sbuf_tensor("lf_rows", [128, 8], F32)).ap()
    pbank = [stack.enter_context(nc.psum_tensor(f"lf_p{i}", [128, 512], F32)).ap() for i in range(4)]
    pq = [stack.enter_context(nc.psum_tensor(f"lf_q{i}", [128, 1024], BF16)).ap() for i in range(2)]
    rt = stack.enter_context(nc.psum_tensor("lf_rt", [128, 1024], F32)).ap()

    x_in_r = x_in.ap().rearrange("c p t -> p c t")
    rb_read_guard = [0]
    yout_guard = [0, 0]
    wslot = [Wb[:, i * 4096:(i + 1) * 4096] for i in range(2)]
    wguard = [0, 0]
    wsl_i = 0
    r2t = sil[:, 0:1024]
    bank_guard = [('ac', 0)] * 4
    sil_guard = [0, 0]
    bsc_guard = [0, 0]
    pq_guard = [0, 0]
    rt_guard_ac = 0
    srow, s2row, mu_c, e2_c, ng2, var_c, sd_c, rs_c = (rowb[:, i:i + 1] for i in range(8))

    def wload(ws, m, cols):
        nonlocal wsl_i
        s = wsl_i % 2
        wsl_i += 1
        g = wguard[s]
        cd = ws.cast_done

        wsem = f"lfw{s}"

        def go(sy, ws=ws, m=m, s=s, g=g, cd=ws.cast_wait(m), cols=cols, wsem=wsem):
            if g > 0:
                sy.wait_ge(p.h('pe'), g)
            sy.wait_ge(p.h(cd[0]), cd[1])
            p.inc(sy.dma_start(out=wslot[s][:, 0:cols], in_=ws.block_ap(m)), wsem, 16)
        p.block.sync(go)
        return s, (wsem, p.v(wsem))

    x3guards = [0, 0]
    al_xs = [None] * (ntiles + 1)

    def load_x3(t):
        s = 0
        g_x3 = x3guards[s]
        sem = f"lfx{s}"

        def loadx(sy, t=t, s=s, g_x3=g_x3, sem=sem):
            if x_ready is not None and x_ready[1] > 0:
                sy.wait_ge(p.h(x_ready[0]), x_ready[1])
            if g_x3 > 0:
                sy.wait_ge(p.h('ve'), g_x3)
            p.inc(sy.dma_start(out=X3[:, s * DC * NT:(s + 1) * DC * NT].rearrange("p (c t) -> p c t", c=DC),
                               in_=x_in_r[:, :, t * NT:(t + 1) * NT]), sem, 16)
        p.block.sync(loadx)
        return (sem, p.v(sem))

    for t in range(ntiles):
        al_x = load_x3(t)
        X3t = X3[:, 0:DC * NT]

        def mm_stage(ws, m, src_buf, kK, b0, wl_extra=None):
            s, wlv = wload(ws, m, kK * 128)
            g0, g1 = bank_guard[b0], bank_guard[b0 + 1]

            def mm(tt, s=s, wlv=wlv, m=m, src_buf=src_buf, kK=kK, b0=b0, g0=g0, g1=g1,
                   wl_extra=wl_extra):
                tt.wait_ge(p.h(wlv[0]), wlv[1])
                if wl_extra is not None:
                    for gn, gv in wl_extra:
                        if gv > 0:
                            tt.wait_ge(p.h(gn), gv)
                for gn, gv in {g0, g1}:
                    if gv > 0:
                        tt.wait_ge(p.h(gn), gv)
                for k in range(kK):
                    for n in range(2):
                        ins = tt.matmul(pbank[b0 + n][:, :],
                                        wslot[s][:, k * 128:(k + 1) * 128],
                                        src_buf(k, n),
                                        start=(k == 0), stop=(k == kK - 1))
                p.inc(ins, 'pe')
            p.block.tensor(mm)
            wguard[s] = p.v('pe')
            return p.v('pe')

        xsrc = lambda k, n, X3t=X3t: X3t[:, k * NT + n * 512:k * NT + (n + 1) * 512]

        for m in range(HC):
            pe_a = mm_stage(wsf1, m, xsrc, DC, 0, wl_extra=[al_x])
            pe_b = mm_stage(wsf2, m, xsrc, DC, 2, wl_extra=[al_x])
            mp = m % 2
            g_sil = sil_guard[mp]
            g_bsc = bsc_guard[mp]

            def ev_a(sc, m=m, mp=mp, pe_a=pe_a, g_sil=g_sil):
                sc.wait_ge(p.h('pe'), pe_a)
                if g_sil > 0:
                    sc.wait_ge(p.h('ve'), g_sil)
                for n in range(2):
                    ins = sc.activation(sil[:, mp * NT + n * 512:mp * NT + (n + 1) * 512],
                                        pbank[n][:, :], AF.Silu, scale=cst['s1'][:, m:m + 1])
                p.inc(ins, 'ac')
            p.block.scalar(ev_a)
            bank_guard[0] = bank_guard[1] = ('ac', p.v('ac'))

            ac_here = p.v('ac')

            def hmul(v, m=m, mp=mp, ac_here=ac_here, pe_b=pe_b):
                v.wait_ge(p.h('ac'), ac_here)
                v.wait_ge(p.h('pe'), pe_b)
                for n in range(2):
                    ins = v.scalar_tensor_tensor(
                        A[:, m * NT + n * 512:m * NT + (n + 1) * 512],
                        pbank[2 + n][:, :], cst['s2'][:, m:m + 1],
                        sil[:, mp * NT + n * 512:mp * NT + (n + 1) * 512],
                        ALU.mult, ALU.mult)
                p.inc(ins, 've')
            p.block.vector(hmul)
            bank_guard[2] = bank_guard[3] = ('ve', p.v('ve'))
            sil_guard[mp] = bsc_guard[mp] = p.v('ve')
        ve_hin = p.v('ve')

        hsrc = lambda k, n: A[:, k * NT + n * 512:k * NT + (n + 1) * 512]
        for m in range(HC):
            b0 = (m % 2) * 2
            pe_m = mm_stage(wsh1, m, hsrc, HC, b0, wl_extra=[('ve', ve_hin)])

            def ev(sc, m=m, b0=b0, pe_m=pe_m):
                sc.wait_ge(p.h('pe'), pe_m)
                for n in range(2):
                    ins = sc.activation(Bh[:, m * NT + n * 512:m * NT + (n + 1) * 512],
                                        pbank[b0 + n][:, :], AF.Gelu, scale=cst['sh1'][:, m:m + 1])
                p.inc(ins, 'ac')
            p.block.scalar(ev)
            bank_guard[b0] = bank_guard[b0 + 1] = ('ac', p.v('ac'))
        ac_h1 = p.v('ac')

        h1src = lambda k, n: Bh[:, k * NT + n * 512:k * NT + (n + 1) * 512]
        for m in range(HC):
            b0 = (m % 2) * 2
            pe_m = mm_stage(wsh2, m, h1src, HC, b0, wl_extra=[('ac', ac_h1)])

            def ev(sc, m=m, b0=b0, pe_m=pe_m):
                sc.wait_ge(p.h('pe'), pe_m)
                for n in range(2):
                    ins = sc.activation(A[:, m * NT + n * 512:m * NT + (n + 1) * 512],
                                        pbank[b0 + n][:, :], AF.Gelu, scale=cst['sh2'][:, m:m + 1])
                p.inc(ins, 'ac')
            p.block.scalar(ev)
            bank_guard[b0] = bank_guard[b0 + 1] = ('ac', p.v('ac'))
        ac_h2 = p.v('ac')

        h2src = lambda k, n: A[:, k * NT + n * 512:k * NT + (n + 1) * 512]
        for m3 in range(DC):
            b0 = (m3 % 2) * 2
            pe_m = mm_stage(wsf3, m3, h2src, HC, b0, wl_extra=[('ac', ac_h2)])
            g_rb = rb_read_guard[0]

            def resid(v, m3=m3, b0=b0, pe_m=pe_m, g_rb=g_rb, al_x=al_x, X3t=X3t):
                v.wait_ge(p.h('pe'), pe_m)
                v.wait_ge(p.h(al_x[0]), al_x[1])
                if g_rb > 0:
                    v.wait_ge(p.h('pe'), g_rb)
                for n in range(2):
                    ins = v.scalar_tensor_tensor(
                        Rb[:, m3 * NT + n * 512:m3 * NT + (n + 1) * 512],
                        pbank[b0 + n][:, :], cst['s3'][:, m3:m3 + 1],
                        X3t[:, m3 * NT + n * 512:m3 * NT + (n + 1) * 512],
                        ALU.mult, ALU.add)
                p.inc(ins, 've')
            p.block.vector(resid)
            bank_guard[b0] = bank_guard[b0 + 1] = ('ve', p.v('ve'))
        x3guards[0] = p.v('ve')
        ve_r = p.v('ve')

        for tc in range(DC):
            qp = tc % 2
            g_pq = pq_guard[qp]

            def trans(tt, tc=tc, qp=qp, g_pq=g_pq, ve_r=ve_r):
                tt.wait_ge(p.h('ve'), ve_r)
                if g_pq > 0:
                    tt.wait_ge(p.h('ve'), g_pq)
                for fc in range(DC):
                    ins = tt.transpose(pq[qp][:, fc * 128:(fc + 1) * 128],
                                       Rb[:, fc * NT + tc * 128:fc * NT + (tc + 1) * 128],
                                       cst['identbf'])
                p.inc(ins, 'pe')
            p.block.tensor(trans)
            pe_tr = p.v('pe')
            g_rt = rt_guard_ac

            def ve_sum(v, qp=qp, pe_tr=pe_tr, g_rt=g_rt):
                v.wait_ge(p.h('pe'), pe_tr)
                if g_rt > 0:
                    v.wait_ge(p.h('ac'), g_rt)
                ins = v.tensor_scalar(rt, pq[qp][:, :], 1.0, None, ALU.mult, ALU.add,
                                      accum_out=srow)
                p.inc(ins, 've')
            p.block.vector(ve_sum)
            ve_s = p.v('ve')
            pq_guard[qp] = ve_s

            def ac_sq(sc, ve_s=ve_s):
                sc.wait_ge(p.h('ve'), ve_s)
                ins = sc.activation(r2t, rt, AF.Square, accum_out=s2row)
                p.inc(ins, 'ac')
            p.block.scalar(ac_sq)
            ac_sqv = p.v('ac')
            rt_guard_ac = ac_sqv

            def ve_stats(v, ac_sqv=ac_sqv):
                v.wait_ge(p.h('ac'), ac_sqv)
                v.tensor_scalar(mu_c, srow, 1.0 / 1024.0, None, ALU.mult)
                c1 = p.inc(v.scalar_tensor_tensor(ng2, srow, -1.0 / 1024.0, srow,
                                                  ALU.mult, ALU.mult), 've')
                v.wait_ge(p.h('ve'), c1)
                ins = v.tensor_tensor(var_c, s2row, ng2, ALU.add)
                p.inc(ins, 've')
            p.block.vector(ve_stats)
            ve_st = p.v('ve')

            def ac_sd(sc, ve_st=ve_st):
                sc.wait_ge(p.h('ve'), ve_st)
                ins = sc.activation(sd_c, var_c, AF.Sqrt, bias=cst['eps_ln'],
                                    scale=1.0 / 1024.0)
                p.inc(ins, 'ac')
            p.block.scalar(ac_sd)
            ac_sdv = p.v('ac')

            for half in range(2):
                ysl = yst[:, 0:512]
                g_y = yout_guard[0]

                def ve_fin(v, ac_sdv=ac_sdv, ysl=ysl, g_y=g_y, half=half):
                    v.wait_ge(p.h('ac'), ac_sdv)
                    if half == 0:
                        c1 = p.inc(v.reciprocal(rs_c, sd_c), 've')
                        v.wait_ge(p.h('ve'), c1)
                    if g_y > 0:
                        v.wait_ge(p.h('out'), g_y)
                    ins = v.tensor_scalar(ysl, rt[:, half * 512:(half + 1) * 512],
                                          mu_c, rs_c, ALU.subtract, ALU.mult)
                    p.inc(ins, 've')
                p.block.vector(ve_fin)
                ve_f = p.v('ve')

                def store(sy, t=t, tc=tc, ysl=ysl, ve_f=ve_f, half=half):
                    sy.wait_ge(p.h('ve'), ve_f)
                    p.inc(sy.dma_start(
                        out=y_out.ap()[t * NT + tc * 128:t * NT + (tc + 1) * 128,
                                       half * 512:(half + 1) * 512],
                        in_=ysl), 'out', 16)
                p.block.sync(store)
                yout_guard[0] = p.v('out')
        rb_read_guard[0] = p.v('pe')

    return ('out', p.v('out'))


def phase_barrier(p):
    vals = [(name, s[1]) for name, s in p._sems.items()
            if s[1] > 0 and not name.startswith('cast') and name != 'c_al']

    def w(e):
        for name, v in vals:
            e.wait_ge(p.h(name), v)
    for eng in ('tensor', 'vector', 'scalar', 'sync', 'gpsimd'):
        getattr(p.block, eng)(w)


def alloc_psum(stack, nc, tag):
    pb = {}
    for i in range(6):
        pb[f'p{i}'] = stack.enter_context(nc.psum_tensor(f"{tag}_p{i}", [128, 512], F32)).ap()
    pb['q6'] = stack.enter_context(nc.psum_tensor(f"{tag}_p6", [128, 1024], BF16)).ap()
    pb['q7'] = stack.enter_context(nc.psum_tensor(f"{tag}_p7", [128, 1024], BF16)).ap()
    return pb


def make_consts(p, stack, const_in, row1_in):
    nc = p.nc
    cst = {}
    csb = stack.enter_context(nc.sbuf_tensor("consts", [128, 456], F32)).ap()
    row1 = stack.enter_context(nc.sbuf_tensor("row1", [1, 128], F32)).ap()
    cbf = stack.enter_context(nc.sbuf_tensor("constbf", [128, 4], BF16)).ap()
    identbf = stack.enter_context(nc.sbuf_tensor("identbf", [128, 128], BF16)).ap()
    misc = stack.enter_context(nc.sbuf_tensor("miscf32", [128, 2560], F32)).ap()

    def go(sy):
        p.inc(sy.dma_start(out=csb, in_=const_in), 'al', 16)
        p.inc(sy.dma_start(out=row1, in_=row1_in), 'al', 16)
    p.block.sync(go)

    def gocast(g):
        p.inc(g.dma_start(out=cbf[:, 0:2], in_=const_in[:, 0:2]), 'cast', 16)
        p.inc(g.dma_start(out=identbf, in_=const_in[:, 324:452]), 'cast', 16)
    p.block.gpsimd(gocast)
    cv = p.v('cast')
    av = p.v('al')

    def wall(e):
        e.wait_ge(p.h('al'), av)
        e.wait_ge(p.h('cast'), cv)
    for eng in ('tensor', 'vector', 'scalar', 'sync', 'gpsimd'):
        getattr(p.block, eng)(wall)

    cst.update(
        ones_sc_bf=cbf[:, 0:1], ones_bf=cbf[:, 1:2],
        ones1_f=row1, role=csb[:, 2:3], rolei=csb[:, 3:4],
        b1=csb[:, 4:36], b2=csb[:, 36:44], gamma=csb[:, 44:52], beta=csb[:, 52:60],
        s1=csb[:, 60:92], s2=csb[:, 92:124], sh1=csb[:, 124:156], sh2=csb[:, 156:188],
        s3=csb[:, 188:196], maskT=csb[:, 196:324], identbf=identbf,
        eps_ln=csb[:, 452:453], eps_att=csb[:, 453:454], one_col=csb[:, 454:455],
        lntmp=misc[:, 0:512], rows=misc[0:1, 512:2560])
    return cst


def host_consts(inputs, core_role):
    cc = np.zeros((128, 456), np.float32)
    cc[:, 0] = 1.0 / 1024.0
    cc[:, 1] = 1.0
    cc[:, 2] = float(core_role)
    cc[:, 3] = 1.0 - float(core_role)
    cc[:, 4:36] = inputs['L0_ffn_b1'].reshape(32, 128).T
    cc[:, 36:44] = inputs['L0_ffn_b2'].reshape(8, 128).T
    cc[:, 44:52] = inputs['L0_attn_gamma'].reshape(8, 128).T
    cc[:, 52:60] = inputs['L0_attn_beta'].reshape(8, 128).T
    cc[:, 60:92] = inputs['L1_ffn_w1_s'].reshape(32, 128).T
    cc[:, 92:124] = inputs['L1_ffn_w2_s'].reshape(32, 128).T
    cc[:, 124:156] = inputs['L1_ffn_h1_s'].reshape(32, 128).T
    cc[:, 156:188] = inputs['L1_ffn_h2_s'].reshape(32, 128).T
    cc[:, 188:196] = inputs['L1_ffn_w3_s'].reshape(8, 128).T
    cc[:, 196:324] = np.triu(np.ones((128, 128), np.float32))
    cc[:, 324:452] = np.eye(128, dtype=np.float32)
    cc[:, 452] = LN_EPS
    cc[:, 453] = 1e-6
    cc[:, 454] = 1.0
    row1 = np.ones((1, 128), np.float32)
    return np.ascontiguousarray(cc), row1


TLH = TL + 2 * HALO

_nc_cache = {}


def _build_l0attn():
    nc = bass.Bass("TRN2", target_bir_lowering=False)
    X0F = nc.dram_tensor("X0F", [DC, 128, TLH], F32, kind="ExternalInput")
    WG = nc.dram_tensor("WG", [1024, 9216], F32, kind="ExternalInput")
    WV = nc.dram_tensor("WV", [1024, 1024], F32, kind="ExternalInput")
    CONST = nc.dram_tensor("CONST", [128, 456], F32, kind="ExternalInput")
    ROW1 = nc.dram_tensor("ROW1", [1, 128], F32, kind="ExternalInput")
    X1 = nc.dram_tensor("X1", [DC, 128, TL], BF16, kind="ExternalOutput")
    xbf = nc.dram_tensor("xbf", [DC, 128, TLH], BF16)
    with ExitStack() as stack:
        p = Prog(nc, stack)
        cst = make_consts(p, stack, CONST.ap(), ROW1.ap())
        wsg = WScratch(nc, "wg", 1024, 9216)
        wsv = WScratch(nc, "wv", 1024, 1024)

        def xcast(g):
            p.inc(g.dma_start(out=xbf.ap(), in_=X0F.ap(), max_dma_last_dim=8192), 'cast', 16)
        p.block.gpsimd(xcast)
        x_ready = ('cast', p.v('cast'))
        wsv.emit_cast(p, WV.ap())
        wsg.emit_cast(p, WG.ap())
        done = emit_l0attn(p, cst, wsg, wsv, xbf, X0F, X1, stack, x_ready=x_ready)

        def fin(g):
            g.wait_ge(p.h(done[0]), done[1])
        p.block.gpsimd(fin)
    return nc


def _build_l0ffn():
    nc = bass.Bass("TRN2", target_bir_lowering=False)
    X1 = nc.dram_tensor("X1", [DC, 128, TL], BF16, kind="ExternalInput")
    W1 = nc.dram_tensor("W1", [1024, 4096], F32, kind="ExternalInput")
    W2 = nc.dram_tensor("W2", [4096, 1024], F32, kind="ExternalInput")
    CONST = nc.dram_tensor("CONST", [128, 456], F32, kind="ExternalInput")
    ROW1 = nc.dram_tensor("ROW1", [1, 128], F32, kind="ExternalInput")
    X2 = nc.dram_tensor("X2", [DC, 128, TL], BF16, kind="ExternalOutput")
    with ExitStack() as stack:
        p = Prog(nc, stack)
        cst = make_consts(p, stack, CONST.ap(), ROW1.ap())
        ws1 = WScratch(nc, "w1", 1024, 4096)
        ws2 = WScratch(nc, "w2", 4096, 1024)
        ws1.emit_cast(p, W1.ap())
        ws2.emit_cast(p, W2.ap())
        done = emit_l0ffn(p, cst, ws1, ws2, X1, X2, stack)

        def fin(g):
            g.wait_ge(p.h(done[0]), done[1])
        p.block.gpsimd(fin)
    return nc


def _build_l1attn():
    nc = bass.Bass("TRN2", target_bir_lowering=False)
    X2 = nc.dram_tensor("X2", [DC, 128, TL], BF16, kind="ExternalInput")
    WQKV = nc.dram_tensor("WQKV", [1024, 3072], F32, kind="ExternalInput")
    WP = nc.dram_tensor("WP", [1024, 1024], F32, kind="ExternalInput")
    CONST = nc.dram_tensor("CONST", [128, 456], F32, kind="ExternalInput")
    ROW1 = nc.dram_tensor("ROW1", [1, 128], F32, kind="ExternalInput")
    X3 = nc.dram_tensor("X3", [DC, 128, TL], BF16, kind="ExternalOutput")
    with ExitStack() as stack:
        p = Prog(nc, stack)
        cst = make_consts(p, stack, CONST.ap(), ROW1.ap())
        wsqkv = WScratch(nc, "qkv", 1024, 3072)
        wsp = WScratch(nc, "wp", 1024, 1024)
        wsqkv.emit_cast(p, WQKV.ap())
        wsp.emit_cast(p, WP.ap())
        done = emit_l1attn(p, cst, wsqkv, wsp, X2, X3, stack)

        def fin(g):
            g.wait_ge(p.h(done[0]), done[1])
        p.block.gpsimd(fin)
    return nc


def _build_l1ffn():
    nc = bass.Bass("TRN2", target_bir_lowering=False)
    X3 = nc.dram_tensor("X3", [DC, 128, TL], BF16, kind="ExternalInput")
    F1 = nc.dram_tensor("F1", [1024, 4096], F32, kind="ExternalInput")
    F2 = nc.dram_tensor("F2", [1024, 4096], F32, kind="ExternalInput")
    H1 = nc.dram_tensor("H1", [4096, 4096], F32, kind="ExternalInput")
    H2 = nc.dram_tensor("H2", [4096, 4096], F32, kind="ExternalInput")
    F3 = nc.dram_tensor("F3", [4096, 1024], F32, kind="ExternalInput")
    CONST = nc.dram_tensor("CONST", [128, 456], F32, kind="ExternalInput")
    ROW1 = nc.dram_tensor("ROW1", [1, 128], F32, kind="ExternalInput")
    Y = nc.dram_tensor("Y", [TL, 1024], F32, kind="ExternalOutput")
    with ExitStack() as stack:
        p = Prog(nc, stack)
        cst = make_consts(p, stack, CONST.ap(), ROW1.ap())
        wf1 = WScratch(nc, "f1", 1024, 4096)
        wf2 = WScratch(nc, "f2", 1024, 4096)
        wh1 = WScratch(nc, "h1", 4096, 4096)
        wh2 = WScratch(nc, "h2", 4096, 4096)
        wf3 = WScratch(nc, "f3", 4096, 1024)
        wf1.emit_cast(p, F1.ap())
        wf2.emit_cast(p, F2.ap())
        wh1.emit_cast(p, H1.ap())
        wh2.emit_cast(p, H2.ap())
        wf3.emit_cast(p, F3.ap())
        done = emit_l1ffn(p, cst, wf1, wf2, wh1, wh2, wf3, X3, Y, stack)

        def fin(g):
            g.wait_ge(p.h(done[0]), done[1])
        p.block.gpsimd(fin)
    return nc


FUSED = True


def _build_fused():
    nc = bass.Bass("TRN2", target_bir_lowering=False)
    X0F = nc.dram_tensor("X0F", [DC, 128, TLH], F32, kind="ExternalInput")
    WG = nc.dram_tensor("WG", [1024, 9216], F32, kind="ExternalInput")
    WV = nc.dram_tensor("WV", [1024, 1024], F32, kind="ExternalInput")
    W1 = nc.dram_tensor("W1", [1024, 4096], F32, kind="ExternalInput")
    W2 = nc.dram_tensor("W2", [4096, 1024], F32, kind="ExternalInput")
    WQKV = nc.dram_tensor("WQKV", [1024, 3072], F32, kind="ExternalInput")
    WP = nc.dram_tensor("WP", [1024, 1024], F32, kind="ExternalInput")
    F1 = nc.dram_tensor("F1", [1024, 4096], F32, kind="ExternalInput")
    F2 = nc.dram_tensor("F2", [1024, 4096], F32, kind="ExternalInput")
    H1 = nc.dram_tensor("H1", [4096, 4096], F32, kind="ExternalInput")
    H2 = nc.dram_tensor("H2", [4096, 4096], F32, kind="ExternalInput")
    F3 = nc.dram_tensor("F3", [4096, 1024], F32, kind="ExternalInput")
    CONST = nc.dram_tensor("CONST", [128, 456], F32, kind="ExternalInput")
    ROW1 = nc.dram_tensor("ROW1", [1, 128], F32, kind="ExternalInput")
    Y = nc.dram_tensor("Y", [TL, 1024], F32, kind="ExternalOutput")
    xbf = nc.dram_tensor("xbf", [DC, 128, TLH], BF16)
    X1 = nc.dram_tensor("X1i", [DC, 128, TL], BF16)
    X2 = nc.dram_tensor("X2i", [DC, 128, TL], BF16)
    X3 = nc.dram_tensor("X3i", [DC, 128, TL], BF16)

    with ExitStack() as gstack:
        p = Prog(nc, gstack)
        cst = make_consts(p, gstack, CONST.ap(), ROW1.ap())

        def xcast(g):
            p.inc(g.dma_start(out=xbf.ap(), in_=X0F.ap(), max_dma_last_dim=8192), 'cast', 16)
        p.block.gpsimd(xcast)
        x_ready = ('cast', p.v('cast'))

        wsv = WScratch(nc, "wv", 1024, 1024)
        wsg = WScratch(nc, "wg", 1024, 9216)
        ws1 = WScratch(nc, "w1", 1024, 4096)
        ws2 = WScratch(nc, "w2", 4096, 1024)
        wsqkv = WScratch(nc, "qkv", 1024, 3072)
        wsp = WScratch(nc, "wp", 1024, 1024)
        wf1 = WScratch(nc, "f1", 1024, 4096)
        wf2 = WScratch(nc, "f2", 1024, 4096)
        wh1 = WScratch(nc, "h1", 4096, 4096)
        wh2 = WScratch(nc, "h2", 4096, 4096)
        wf3 = WScratch(nc, "f3", 4096, 1024)
        for ws_, ap_ in [(wsv, WV), (wsg, WG), (ws1, W1), (ws2, W2),
                         (wsqkv, WQKV), (wsp, WP), (wf1, F1), (wf2, F2),
                         (wh1, H1), (wh2, H2), (wf3, F3)]:
            ws_.emit_cast(p, ap_.ap(), n_groups=(8 if ws_.mK == 72 else (4 if ws_.mK >= 32 else 1)))

        with ExitStack() as s1:
            emit_l0attn(p, cst, wsg, wsv, xbf, X0F, X1, s1, x_ready=x_ready)
            phase_barrier(p)
        with ExitStack() as s2:
            emit_l0ffn(p, cst, ws1, ws2, X1, X2, s2)
            phase_barrier(p)
        with ExitStack() as s3:
            emit_l1attn(p, cst, wsqkv, wsp, X2, X3, s3)
            phase_barrier(p)
        with ExitStack() as s4:
            done = emit_l1ffn(p, cst, wf1, wf2, wh1, wh2, wf3, X3, Y, s4)

        def fin(g):
            g.wait_ge(p.h(done[0]), done[1])
        p.block.gpsimd(fin)
    return nc


def kernel(**inputs):
    x = np.asarray(inputs['L0_attn_x'])

    ncs = _nc_cache
    if FUSED and 'fused' not in ncs:
        ncs['fused'] = _build_fused()
    if not FUSED and 'l0a' not in ncs:
        ncs['l0a'] = _build_l0attn()
        ncs['l0f'] = _build_l0ffn()
        ncs['l1a'] = _build_l1attn()
        ncs['l1f'] = _build_l1ffn()

    WGt = np.ascontiguousarray(np.asarray(inputs['L0_attn_gate_W']).T)
    WVt = np.ascontiguousarray(np.asarray(inputs['L0_attn_value_W']).T)
    W1 = np.ascontiguousarray(np.asarray(inputs['L0_ffn_w1']))
    W2 = np.ascontiguousarray(np.asarray(inputs['L0_ffn_w2']))
    WQKVt = np.ascontiguousarray(np.asarray(inputs['L1_attn_w_attn']).T)
    WPt = np.ascontiguousarray(np.asarray(inputs['L1_attn_w_proj']).T)
    F1t = np.ascontiguousarray(np.asarray(inputs['L1_ffn_w1_w']).T)
    F2t = np.ascontiguousarray(np.asarray(inputs['L1_ffn_w2_w']).T)
    H1t = np.ascontiguousarray(np.asarray(inputs['L1_ffn_h1_w']).T)
    H2t = np.ascontiguousarray(np.asarray(inputs['L1_ffn_h2_w']).T)
    F3t = np.ascontiguousarray(np.asarray(inputs['L1_ffn_w3_w']).T)

    ccs, row1s = [], []
    x0fs = []
    idx_all = np.arange(-HALO, TL + HALO)
    for core in range(8):
        b, h = core // 2, core % 2
        cc, row1 = host_consts(inputs, h)
        ccs.append(cc)
        row1s.append(row1)
        idx = (h * TL + idx_all) % T
        x0h = x[b][idx]
        x0fs.append(np.ascontiguousarray(x0h.T).reshape(DC, 128, TLH))

    cores = list(range(8))
    if FUSED:
        rf = run_bass_kernel_spmd(ncs['fused'], [
            {"X0F": x0fs[c], "WG": WGt, "WV": WVt, "W1": W1, "W2": W2,
             "WQKV": WQKVt, "WP": WPt, "F1": F1t, "F2": F2t, "H1": H1t,
             "H2": H2t, "F3": F3t, "CONST": ccs[c], "ROW1": row1s[c]}
            for c in range(8)], core_ids=cores)
        out = np.empty((B, T, D), np.float32)
        for core in range(8):
            b, h = core // 2, core % 2
            out[b, h * TL:(h + 1) * TL, :] = rf.results[core]["Y"]
        return out

    r1 = run_bass_kernel_spmd(ncs['l0a'], [
        {"X0F": x0fs[c], "WG": WGt, "WV": WVt, "CONST": ccs[c], "ROW1": row1s[c]}
        for c in range(8)], core_ids=cores)
    r2 = run_bass_kernel_spmd(ncs['l0f'], [
        {"X1": r1.results[c]["X1"], "W1": W1, "W2": W2, "CONST": ccs[c], "ROW1": row1s[c]}
        for c in range(8)], core_ids=cores)
    r3 = run_bass_kernel_spmd(ncs['l1a'], [
        {"X2": r2.results[c]["X2"], "WQKV": WQKVt, "WP": WPt, "CONST": ccs[c], "ROW1": row1s[c]}
        for c in range(8)], core_ids=cores)
    r4 = run_bass_kernel_spmd(ncs['l1f'], [
        {"X3": r3.results[c]["X3"], "F1": F1t, "F2": F2t, "H1": H1t, "H2": H2t,
         "F3": F3t, "CONST": ccs[c], "ROW1": row1s[c]}
        for c in range(8)], core_ids=cores)

    out = np.empty((B, T, D), np.float32)
    for core in range(8):
        b, h = core // 2, core % 2
        out[b, h * TL:(h + 1) * TL, :] = r4.results[core]["Y"]
    return out


# version 21
# speedup vs baseline: 1.1655x; 1.1655x over previous
import sys
if "/opt/trn_rl_repo" not in sys.path:
    sys.path.insert(0, "/opt/trn_rl_repo")
import numpy as np
import ml_dtypes
from contextlib import ExitStack
import concourse.bass as bass
import concourse.mybir as mybir
from concourse.bass_utils import run_bass_kernel_spmd

F32 = mybir.dt.float32
BF16 = mybir.dt.bfloat16
AF = mybir.ActivationFunctionType
ALU = mybir.AluOpType

B, T, D = 4, 8192, 1024
DBG = {}
TL = 4096
DC = 8
HC = 32
LN_EPS = 1e-5
HALO = 32
SHIFTS = (0, -32, -16, -4, -1, 1, 4, 16, 32)


def _pedrain(p, tt, val):
    if DBG.get('drain'):
        tt.wait_ge(p.h('pe'), val)


class Prog:
    def __init__(self, nc, stack):
        self.nc = nc
        self.stack = stack
        self.block = stack.enter_context(nc.Block())
        self._sems = {}

    def sem(self, name):
        if name not in self._sems:
            h = self.stack.enter_context(self.nc.semaphore(name))
            self._sems[name] = [h, 0]
        return self._sems[name]

    def h(self, name):
        return self.sem(name)[0]

    def v(self, name):
        return self.sem(name)[1]

    def inc(self, instr, name, n=1):
        s = self.sem(name)
        instr.then_inc(s[0], n)
        s[1] += n
        return s[1]


class WScratch:
    def __init__(self, nc, name, d_in, d_out):
        self.d_in, self.d_out = d_in, d_out
        self.kK, self.mK = d_in // 128, d_out // 128
        self.cols = self.kK * 128
        self.scr = nc.dram_tensor(f"{name}_scr", [self.mK, 128, self.cols], BF16)
        self.cast_sem = f"cast_{name}"
        self.cast_done = 0

    def emit_cast(self, p, w_hbm_ap, n_groups=1):
        src = w_hbm_ap.rearrange("(k p) (m c) -> m p k c", p=128, c=128)
        scr = self.scr.ap().rearrange("m p (k c) -> m p k c", c=128)
        mK = self.mK
        self.grp = max(1, (mK + n_groups - 1) // n_groups)
        self.gsems = []
        self.gdone = []
        for g0 in range(0, mK, self.grp):
            sem = f"{self.cast_sem}_{g0}"

            def go(g, g0=g0, sem=sem):
                for m in range(g0, min(g0 + self.grp, mK)):
                    p.inc(g.dma_start(out=scr[m], in_=src[m]), sem, 16)
            p.block.gpsimd(go)
            self.gsems.append(sem)
            self.gdone.append(p.v(sem))
        self.cast_done = -1

    def cast_wait(self, m):
        gi = min(m // self.grp, len(self.gsems) - 1)
        return (self.gsems[gi], self.gdone[gi])

    def block_ap(self, m):
        return self.scr.ap()[m]


def emit_ln512(p, cst, r_tiles, out_tiles, pb, state, norm_emit=None, ve_r=None):
    if ve_r is None:
        ve_r = p.v('ve')
    rows = cst['rows']
    mu_row = rows[:, 0:512]
    e2_row = rows[:, 512:1024]
    var_row = rows[:, 1024:1536]
    sd_row = rows[:, 512:1024]
    rs_row = rows[:, 1536:2048]
    r2 = state['r2']

    g_rows_v = state.get('rows_read_ve', 0)

    def mm_sum2(t):
        if ve_r > 0:
            t.wait_ge(p.h('ve'), ve_r)
        if g_rows_v > 0:
            t.wait_ge(p.h('ve'), g_rows_v)
        for mc in range(DC):
            mm = t.matmul(pb['sum'][0:1, :], cst['ones_sc_bf'], r_tiles[mc],
                          start=(mc == 0), stop=(mc == DC - 1))
        p.inc(mm, 'pe')
    p.block.tensor(mm_sum2)

    g_sq_ac = state.get('sq_read_pe', 0)

    def ve_sq(v):
        if g_sq_ac > 0:
            v.wait_ge(p.h('pe'), g_sq_ac)
        for mc in range(DC):
            ins = v.tensor_tensor(r2[:, mc * 512:(mc + 1) * 512], r_tiles[mc],
                                  r_tiles[mc], ALU.mult)
        p.inc(ins, 've')
    p.block.vector(ve_sq)
    ac_sq = p.v('ve')

    def mm_sq(t):
        t.wait_ge(p.h('ve'), ac_sq)
        if g_rows_v > 0:
            t.wait_ge(p.h('ve'), g_rows_v)
        for mc in range(DC):
            mm = t.matmul(pb['sq'][0:1, :], cst['ones_sc_bf'], r2[:, mc * 512:(mc + 1) * 512],
                          start=(mc == 0), stop=(mc == DC - 1))
        p.inc(mm, 'pe')
    p.block.tensor(mm_sq)
    pe_sq = p.v('pe')
    state['sq_read_pe'] = pe_sq

    def ve_rows(v):
        v.wait_ge(p.h('pe'), pe_sq)
        v.tensor_copy(mu_row, pb['sum'][0:1, :])
        c0 = p.inc(v.tensor_copy(e2_row, pb['sq'][0:1, :]), 've')
        v.wait_ge(p.h('ve'), c0)
        c1 = p.inc(v.scalar_tensor_tensor(var_row, mu_row, -1.0, mu_row,
                                          ALU.mult, ALU.mult), 've')
        v.wait_ge(p.h('ve'), c1)
        ins = v.tensor_tensor(var_row, e2_row, var_row, ALU.add)
        p.inc(ins, 've')
    p.block.vector(ve_rows)
    ve_var = p.v('ve')
    state['rows_read_ve'] = ve_var

    def act_sd(s):
        s.wait_ge(p.h('ve'), ve_var)
        ins = s.activation(sd_row, var_row, AF.Sqrt, bias=cst['eps_ln'][0:1, :])
        p.inc(ins, 'ac')
    p.block.scalar(act_sd)
    ac_sd = p.v('ac')

    def ve_rs(v):
        v.wait_ge(p.h('ac'), ac_sd)
        ins = v.reciprocal(rs_row, sd_row)
        p.inc(ins, 've')
    p.block.vector(ve_rs)
    ve_rsv = p.v('ve')

    g_norm = state.get('norm_done_ve', 0)

    def mm_bc(t):
        t.wait_ge(p.h('ve'), ve_rsv)
        if g_norm > 0:
            t.wait_ge(p.h('ve'), g_norm)
        t.matmul(pb['mu'], cst['ones1_f'], mu_row, start=True, stop=True)
        mm = t.matmul(pb['rs'], cst['ones1_f'], rs_row, start=True, stop=True)
        p.inc(mm, 'pe')
    p.block.tensor(mm_bc)
    pe_bc = p.v('pe')

    tmp = cst['lntmp']

    def ve_norm(v):
        v.wait_ge(p.h('pe'), pe_bc)
        if norm_emit is not None:
            ins = norm_emit(v, pb['mu'], pb['rs'], tmp)
        else:
            for mc in range(DC):
                v.tensor_tensor(tmp, r_tiles[mc], pb['mu'], ALU.subtract)
                ins = v.tensor_tensor(out_tiles[mc], tmp, pb['rs'], ALU.mult)
        p.inc(ins, 've')
    p.block.vector(ve_norm)
    state['norm_done_ve'] = p.v('ve')
    return p.v('ve')


def emit_l0attn(p, cst, wsg, wsv, x0bf, x0f, x_out, stack, x_ready=None):
    nc = p.nc
    NT = 1024
    WH = NT + 64
    ntiles = TL // NT
    xw_sb = [stack.enter_context(nc.sbuf_tensor(f"at_xw{i}", [128, DC * WH], BF16)).ap()
             for i in range(2)]
    vsb = stack.enter_context(nc.sbuf_tensor("at_v", [128, DC * WH], BF16)).ap()
    accf = stack.enter_context(nc.sbuf_tensor("at_acc", [128, DC * NT], BF16)).ap()
    accb = stack.enter_context(nc.sbuf_tensor("at_accb", [128, 2 * DC * NT], BF16)).ap()
    gsb = stack.enter_context(nc.sbuf_tensor("at_g", [128, 9 * NT], BF16)).ap()
    gtmp = stack.enter_context(nc.sbuf_tensor("at_gt", [128, NT], BF16)).ap()
    Rb = stack.enter_context(nc.sbuf_tensor("at_R", [128, DC * NT], BF16)).ap()
    r2 = stack.enter_context(nc.sbuf_tensor("at_r2", [128, 4096], BF16)).ap()
    xout_st = stack.enter_context(nc.sbuf_tensor("at_xo", [128, DC * NT], BF16)).ap()
    Wb = stack.enter_context(nc.sbuf_tensor("at_W", [128, 4 * 1024], BF16)).ap()
    t2f = stack.enter_context(nc.sbuf_tensor("at_t2", [128, 512], F32)).ap()
    pg = [stack.enter_context(nc.psum_tensor(f"at_pg{i}", [128, 1024], F32)).ap() for i in range(2)]
    pln = stack.enter_context(nc.psum_tensor("at_pln", [1, 1024], F32)).ap()
    pbc = stack.enter_context(nc.psum_tensor("at_pbc", [128, 1024], F32)).ap()
    pb = {'sum': pln[0:1, 0:512], 'sq': pln[0:1, 512:1024],
          'mu': pbc[:, 0:512], 'rs': pbc[:, 512:1024]}
    ln_state = {'r2': r2}

    x0bf_r = x0bf.ap().rearrange("c p t -> p c t")
    x0f_r = x0f.ap().rearrange("c p t -> p c t")
    x_out_r = x_out.ap().rearrange("c p t -> p c t")

    wslot = [Wb[:, i * 1024:(i + 1) * 1024] for i in range(4)]
    pend_ln = None
    wguard = [0, 0, 0, 0]
    xw_guard = [0, 0]
    xf_guard = 0
    gsb_guard = [0]
    pg_guard_ac = [0, 0]
    pg_guard_ve = [0, 0]
    accb_guard = (0, 0)
    rb_guard = (0, 0)
    xout_guard = 0
    wsl_i = 0

    def wload(ws, m):
        nonlocal wsl_i
        s = wsl_i % 4
        wsl_i += 1
        g = wguard[s]
        cd = ws.cast_done
        sem = f"atw{s}"

        def go(sy, ws=ws, m=m, s=s, g=g, cd=ws.cast_wait(m), sem=sem):
            if g > 0:
                sy.wait_ge(p.h('pe'), g)
            sy.wait_ge(p.h(cd[0]), cd[1])
            p.inc(sy.dma_start(out=wslot[s], in_=ws.block_ap(m)), sem, 16)
        p.block.sync(go)
        return s, (sem, p.v(sem))

    for t in range(ntiles):
        xw = xw_sb[t % 2]
        g_xw = xw_guard[t % 2]
        g_xf = xf_guard

        xw_sem = f"atxw{t % 2}"

        def loadx(sy, t=t, g_xw=g_xw, g_xf=g_xf, xw=xw, xw_sem=xw_sem):
            if x_ready is not None and x_ready[1] > 0:
                sy.wait_ge(p.h(x_ready[0]), x_ready[1])
            if g_xw > 0:
                sy.wait_ge(p.h('pe'), g_xw)
            if g_xf > 0:
                sy.wait_ge(p.h('ve'), g_xf)
            p.inc(sy.dma_start(out=xw.rearrange("p (c t) -> p c t", c=DC),
                               in_=x0bf_r[:, :, t * NT:t * NT + WH]), xw_sem, 16)
        p.block.sync(loadx)
        al_x = (xw_sem, p.v(xw_sem))
        al_f = al_x

        for m in range(DC):
            s, wlv = wload(wsv, m)
            gv0, gv1 = pg_guard_ac[0], pg_guard_ac[1]
            gve0, gve1 = pg_guard_ve[0], pg_guard_ve[1]

            def vmm(tt, s=s, wlv=wlv, al_x=al_x, gv0=gv0, gv1=gv1, gve0=gve0, gve1=gve1, xw=xw):
                tt.wait_ge(p.h(wlv[0]), wlv[1])
                tt.wait_ge(p.h(al_x[0]), al_x[1])
                for g in (gv0, gv1):
                    if g > 0:
                        tt.wait_ge(p.h('ac'), g)
                for g in (gve0, gve1):
                    if g > 0:
                        tt.wait_ge(p.h('ve'), g)
                for k in range(DC):
                    st = (k == 0)
                    sp = (k == DC - 1)
                    tt.matmul(pg[0][:, 0:512], wslot[s][:, k * 128:(k + 1) * 128],
                              xw[:, k * WH:k * WH + 512], start=st, stop=sp)
                    tt.matmul(pg[0][:, 512:1024], wslot[s][:, k * 128:(k + 1) * 128],
                              xw[:, k * WH + 512:k * WH + 1024], start=st, stop=sp)
                    ins = tt.matmul(pg[1][:, 0:64], wslot[s][:, k * 128:(k + 1) * 128],
                                    xw[:, k * WH + 1024:k * WH + 1088], start=st, stop=sp)
                p.inc(ins, 'pe')
            p.block.tensor(vmm)
            wguard[s] = p.v('pe')
            pe_v = p.v('pe')

            def vevac(v, m=m, pe_v=pe_v):
                v.wait_ge(p.h('pe'), pe_v)
                v.tensor_copy(vsb[:, m * WH:m * WH + 1024], pg[0][:, :])
                ins = v.tensor_copy(vsb[:, m * WH + 1024:m * WH + 1088], pg[1][:, 0:64])
                p.inc(ins, 've')
            p.block.vector(vevac)
            pg_guard_ve[0] = pg_guard_ve[1] = p.v('ve')
            pg_guard_ac[0] = pg_guard_ac[1] = 0

        acb_pe, acb_ac = accb_guard
        for m in range(DC):
            ac_base = p.v('ac')
            ve_base = p.v('ve')
            pe_g_j = []
            for j in range(9):
                gm = j * DC + m
                s, wlv = wload(wsg, gm)
                jp = j % 2
                g_ve = pg_guard_ve[jp]
                g_ac = pg_guard_ac[jp] if j < 2 else ac_base + (j - 2) + 1

                def gmm(tt, s=s, wlv=wlv, jp=jp, g_ac=g_ac, g_ve=g_ve, al_x=al_x, xw=xw):
                    tt.wait_ge(p.h(wlv[0]), wlv[1])
                    tt.wait_ge(p.h(al_x[0]), al_x[1])
                    if g_ac > 0:
                        tt.wait_ge(p.h('ac'), g_ac)
                    if g_ve > 0:
                        tt.wait_ge(p.h('ve'), g_ve)
                    for k in range(DC):
                        st = (k == 0)
                        sp = (k == DC - 1)
                        tt.matmul(pg[jp][:, 0:512], wslot[s][:, k * 128:(k + 1) * 128],
                                  xw[:, k * WH + HALO:k * WH + HALO + 512], start=st, stop=sp)
                        ins = tt.matmul(pg[jp][:, 512:1024], wslot[s][:, k * 128:(k + 1) * 128],
                                        xw[:, k * WH + HALO + 512:k * WH + HALO + 1024], start=st, stop=sp)
                    p.inc(ins, 'pe')
                p.block.tensor(gmm)
                wguard[s] = p.v('pe')
                pe_g_j.append(p.v('pe'))
                pg_guard_ve[jp] = 0

            mp9 = 0
            gsb_prev = gsb_guard[0]

            def sigs(sc, pe_g_j=tuple(pe_g_j), gsb_prev=gsb_prev, mp9=mp9):
                for j in range(9):
                    if gsb_prev > 0:
                        sc.wait_ge(p.h('ve'), gsb_prev + j + 1)
                    sc.wait_ge(p.h('pe'), pe_g_j[j])
                    ins = sc.activation(gsb[:, (mp9 + j) * NT:(mp9 + j + 1) * NT],
                                        pg[j % 2][:, :], AF.Sigmoid)
                    p.inc(ins, 'ac')
            p.block.scalar(sigs)
            pg_guard_ac[0] = p.v('ac')
            pg_guard_ac[1] = p.v('ac') - 1

            def gates(v, m=m, t=t, mp9=mp9, ac_base=ac_base, acb_pe=acb_pe, acb_ac=acb_ac):
                for j in range(9):
                    v.wait_ge(p.h('ac'), ac_base + j + 1)
                    gg = gsb[:, (mp9 + j) * NT:(mp9 + j + 1) * NT]
                    vv = vsb[:, m * WH + HALO + SHIFTS[j]:m * WH + HALO + SHIFTS[j] + NT]
                    if j == 0:
                        ins = v.tensor_tensor(accf[:, m * NT:(m + 1) * NT], gg, vv, ALU.mult)
                    elif j < 8:
                        v.tensor_tensor(gtmp[:, 0:NT], gg, vv, ALU.mult)
                        ins = v.tensor_tensor(accf[:, m * NT:(m + 1) * NT],
                                              accf[:, m * NT:(m + 1) * NT],
                                              gtmp[:, 0:NT], ALU.add)
                    else:
                        v.tensor_tensor(gtmp[:, 0:NT], gg, vv, ALU.mult)
                        if acb_pe > 0:
                            v.wait_ge(p.h('pe'), acb_pe)
                        if acb_ac > 0:
                            v.wait_ge(p.h('ac'), acb_ac)
                        ins = v.tensor_tensor(accb[:, (t % 2) * DC * NT + m * NT:(t % 2) * DC * NT + (m + 1) * NT],
                                              accf[:, m * NT:(m + 1) * NT],
                                              gtmp[:, 0:NT], ALU.add)
                    p.inc(ins, 've')
            p.block.vector(gates)
            gsb_guard[0] = ve_base
        xw_guard[t % 2] = p.v('pe')

        ve_gate_done = p.v('ve')
        al_this = al_f

        def make_ln(t=t, ve_gate_done=ve_gate_done, al_f=al_this, xw=xw, asl=(t % 2)):
            def flush():
                nonlocal xout_guard, accb_guard, rb_guard, xf_guard
                rb_pe, rb_ac = rb_guard
                for n in range(2):
                    acc_sl = [accb[:, asl * DC * NT + mc * NT + n * 512:
                                   asl * DC * NT + mc * NT + n * 512 + 512] for mc in range(DC)]
                    r_sl = [Rb[:, mc * NT + n * 512:mc * NT + n * 512 + 512] for mc in range(DC)]
                    o_sl = [xout_st[:, mc * NT + n * 512:mc * NT + n * 512 + 512] for mc in range(DC)]

                    def inner_norm(v, mu_ap, rs_ap, tmp, n=n, acc_sl=acc_sl, r_sl=r_sl,
                                   al_f=al_f, rb_pe=rb_pe, rb_ac=rb_ac, xw=xw):
                        v.wait_ge(p.h(al_f[0]), al_f[1])
                        if rb_pe > 0:
                            v.wait_ge(p.h('pe'), rb_pe)
                        if rb_ac > 0:
                            v.wait_ge(p.h('ac'), rb_ac)
                        for mc in range(DC):
                            v.tensor_tensor(tmp, acc_sl[mc], mu_ap, ALU.subtract)
                            v.scalar_tensor_tensor(t2f, tmp, cst['gamma'][:, mc:mc + 1], rs_ap,
                                                   ALU.mult, ALU.mult)
                            ins = v.scalar_tensor_tensor(
                                r_sl[mc], t2f, cst['beta'][:, mc:mc + 1],
                                xw[:, mc * WH + HALO + n * 512:mc * WH + HALO + n * 512 + 512],
                                ALU.add, ALU.add)
                        return ins

                    emit_ln512(p, cst, acc_sl, r_sl, pb, ln_state, norm_emit=inner_norm,
                               ve_r=(ve_gate_done if n == 0 else None))
                    if xout_guard > 0 and n == 0:
                        def vwait(v, g=xout_guard):
                            v.wait_ge(p.h('spx1'), g)
                        p.block.vector(vwait)
                    emit_ln512(p, cst, r_sl, o_sl, pb, ln_state)
                accb_guard = (p.v('pe'), p.v('ac'))
                rb_guard = accb_guard
                xf_guard = p.v('ve')
                ve_done = p.v('ve')

                def store(g, t=t, ve_done=ve_done):
                    g.wait_ge(p.h('ve'), ve_done)
                    p.inc(g.dma_start(out=x_out_r[:, :, t * NT:(t + 1) * NT],
                                      in_=xout_st.rearrange("p (c t) -> p c t", c=DC)), 'spx1', 16)
                p.block.gpsimd(store)
                xout_guard = p.v('spx1')
            return flush

        if pend_ln is not None:
            pend_ln()
        pend_ln = make_ln()

    pend_ln()
    return ('spx1', p.v('spx1'))


def emit_l0ffn(p, cst, ws1, ws2, x_in, x_out, stack, x_ready=None):
    nc = p.nc
    NT = 1024
    ntiles = TL // NT
    A = stack.enter_context(nc.sbuf_tensor("ff_A", [128, 3 * DC * NT], BF16)).ap()
    Bh = stack.enter_context(nc.sbuf_tensor("ff_B", [128, HC * NT], BF16)).ap()
    W = stack.enter_context(nc.sbuf_tensor("ff_W", [128, 8192], BF16)).ap()
    R = stack.enter_context(nc.sbuf_tensor("ff_R", [128, 8192], BF16)).ap()
    r2 = stack.enter_context(nc.sbuf_tensor("ff_r2", [128, 4096], BF16)).ap()
    pbx = [stack.enter_context(nc.psum_tensor(f"ff_p{i}", [128, 512], F32)).ap() for i in range(8)]
    pbank = pbx[:4]
    pb = {'sum': pbx[4], 'sq': pbx[5], 'mu': pbx[6], 'rs': pbx[7]}

    xin_ring = [A[:, 0:DC * NT], A[:, DC * NT:2 * DC * NT]]
    xout_st = A[:, 2 * DC * NT:3 * DC * NT]
    wslot = [W[:, 0:4096], W[:, 4096:8192]]
    x_in_r = x_in.ap().rearrange("c p t -> p c t")
    x_out_r = x_out.ap().rearrange("c p t -> p c t")

    xin_guard = [0, 0]
    wslot_guard = [0, 0]
    bank_guard = [('ac', 0)] * 4
    xout_guard = 0
    ln_state = {'r2': r2}

    def load_xtile(t):
        s = t % 2
        g = xin_guard[s]
        sem = f"ffx{s}"

        def go(sy, s=s, g=g, sem=sem, t=t):
            if x_ready is not None and x_ready[1] > 0:
                sy.wait_ge(p.h(x_ready[0]), x_ready[1])
            if g > 0:
                sy.wait_ge(p.h('ve'), g)
            p.inc(sy.dma_start(out=xin_ring[s].rearrange("p (c t) -> p c t", c=DC),
                               in_=x_in_r[:, :, t * NT:(t + 1) * NT]), sem, 16)
        p.block.sync(go)
        return (sem, p.v(sem))

    al_tile = [None] * (ntiles + 1)
    al_tile[0] = load_xtile(0)

    for t in range(ntiles):
        xin = xin_ring[t % 2]
        if t + 1 < ntiles:
            al_tile[t + 1] = load_xtile(t + 1)
        al_need = al_tile[t]

        for mg in range(0, HC, 4):
            sidx = (mg // 4) % 2
            wg = wslot_guard[sidx]
            cast_need = ws1.cast_done

            wsem = f"ffw{sidx}"

            def wload(sy, mg=mg, sidx=sidx, wg=wg, cast_need=ws1.cast_wait(mg + 3), ws=ws1, wsem=wsem):
                if wg > 0:
                    sy.wait_ge(p.h('pe'), wg)
                sy.wait_ge(p.h(cast_need[0]), cast_need[1])
                src = ws.scr.ap()[mg:mg + 4].rearrange("m p c -> p m c")
                dst = wslot[sidx].rearrange("p (m c) -> p m c", m=4)
                p.inc(sy.dma_start(out=dst, in_=src), wsem, 16)
            p.block.sync(wload)
            wl_here = (wsem, p.v(wsem))

            for mi in range(4):
                m = mg + mi
                wblk = wslot[sidx][:, mi * 1024:(mi + 1) * 1024]
                b0 = (m % 2) * 2

                def mm(tt, wblk=wblk, b0=b0, g0=bank_guard[b0], g1=bank_guard[b0 + 1],
                       wl_here=wl_here, al_need=al_need, xin=xin):
                    tt.wait_ge(p.h(wl_here[0]), wl_here[1])
                    tt.wait_ge(p.h(al_need[0]), al_need[1])
                    for gn, gv in (g0, g1):
                        if gv > 0:
                            tt.wait_ge(p.h(gn), gv)
                    for k in range(DC):
                        for n in range(2):
                            ins = tt.matmul(pbank[b0 + n][:, :],
                                            wblk[:, k * 128:(k + 1) * 128],
                                            xin[:, k * NT + n * 512:k * NT + (n + 1) * 512],
                                            start=(k == 0), stop=(k == DC - 1))
                    p.inc(ins, 'pe')
                p.block.tensor(mm)
                pe_here = p.v('pe')

                def evac(s, m=m, b0=b0, pe_here=pe_here, t=t):
                    s.wait_ge(p.h('pe'), pe_here)
                    for n in range(2):
                        ins = s.activation(Bh[:, m * NT + n * 512:m * NT + (n + 1) * 512],
                                           pbank[b0 + n][:, :], AF.Gelu,
                                           bias=cst['b1'][:, m:m + 1])
                    p.inc(ins, 'ac')
                p.block.scalar(evac)
                bank_guard[b0] = bank_guard[b0 + 1] = ('ac', p.v('ac'))
            wslot_guard[sidx] = p.v('pe')
        ac_h_done = p.v('ac')

        r_half = [[R[:, (n * DC + mc) * 512:(n * DC + mc) * 512 + 512] for mc in range(DC)]
                  for n in range(2)]
        ve_r_guard = ln_state.get('norm_done_ve', 0)
        for m2 in range(DC):
            sidx = m2 % 2
            wg = wslot_guard[sidx]

            wsem = f"ffw{sidx}"

            def wload2(sy, m2=m2, sidx=sidx, wg=wg, cast_need=ws2.cast_wait(m2), wsem=wsem):
                if wg > 0:
                    sy.wait_ge(p.h('pe'), wg)
                sy.wait_ge(p.h(cast_need[0]), cast_need[1])
                p.inc(sy.dma_start(out=wslot[sidx], in_=ws2.scr.ap()[m2]), wsem, 16)
            p.block.sync(wload2)
            wl_here = (wsem, p.v(wsem))
            b0 = (m2 % 2) * 2

            def mm2(tt, m2=m2, sidx=sidx, b0=b0, g0=bank_guard[b0], g1=bank_guard[b0 + 1],
                    wl_here=wl_here, ac_h_done=ac_h_done):
                tt.wait_ge(p.h(wl_here[0]), wl_here[1])
                tt.wait_ge(p.h('ac'), ac_h_done)
                for gn, gv in (g0, g1):
                    if gv > 0:
                        tt.wait_ge(p.h(gn), gv)
                for k2 in range(HC):
                    for n in range(2):
                        ins = tt.matmul(pbank[b0 + n][:, :],
                                        wslot[sidx][:, k2 * 128:(k2 + 1) * 128],
                                        Bh[:, k2 * NT + n * 512:k2 * NT + (n + 1) * 512],
                                        start=(k2 == 0), stop=(k2 == HC - 1))
                p.inc(ins, 'pe')
            p.block.tensor(mm2)
            pe_here = p.v('pe')

            def resid(v, m2=m2, b0=b0, pe_here=pe_here, xin=xin, ve_r_guard=ve_r_guard,
                      al_need=al_need):
                v.wait_ge(p.h('pe'), pe_here)
                v.wait_ge(p.h(al_need[0]), al_need[1])
                if ve_r_guard > 0:
                    v.wait_ge(p.h('ve'), ve_r_guard)
                for n in range(2):
                    ins = v.scalar_tensor_tensor(
                        r_half[n][m2], pbank[b0 + n][:, :], cst['b2'][:, m2:m2 + 1],
                        xin[:, m2 * NT + n * 512:m2 * NT + (n + 1) * 512],
                        ALU.add, ALU.add)
                p.inc(ins, 've')
            p.block.vector(resid)
            bank_guard[b0] = bank_guard[b0 + 1] = ('ve', p.v('ve'))
            wslot_guard[sidx] = pe_here
        xin_guard[t % 2] = p.v('ve')

        sp_guard = xout_guard
        for n in range(2):
            out_tiles = [xout_st[:, mc * NT + n * 512:mc * NT + (n + 1) * 512] for mc in range(DC)]
            if sp_guard > 0 and n == 0:
                def vwait(v, sp_guard=sp_guard):
                    v.wait_ge(p.h('spx2'), sp_guard)
                p.block.vector(vwait)
            emit_ln512(p, cst, r_half[n], out_tiles, pb, ln_state)
        ve_done = p.v('ve')

        DBG['xin'] = xin
        DBG['Bh'] = Bh
        DBG['R'] = R

        def store(g, t=t, ve_done=ve_done):
            g.wait_ge(p.h('ve'), ve_done)
            p.inc(g.dma_start(out=x_out_r[:, :, t * NT:(t + 1) * NT],
                              in_=xout_st.rearrange("p (c t) -> p c t", c=DC)), 'spx2', 16)
        p.block.gpsimd(store)
        xout_guard = p.v('spx2')

    return ('spx2', p.v('spx2'))


def emit_l1attn(p, cst, wsqkv, wsproj, x_in, x_out, stack, tag="", groups=None):
    nc = p.nc
    NH, DH, CH = 16, 64, 128
    NC_ = TL // CH
    PR = 8
    kT_hbm = nc.dram_tensor(f"l1a{tag}_kT", [DC, 128, TL], BF16)
    v_hbm = nc.dram_tensor(f"l1a{tag}_v", [NC_, 128, 1024], BF16)
    cc_in = nc.dram_tensor(f"l1a{tag}_ccin", [128, 520], F32)
    cc_out = nc.dram_tensor(f"l1a{tag}_ccout", [128, 520], F32)

    qT = stack.enter_context(nc.sbuf_tensor("aa_qT", [128, PR * TL], BF16)).ap()
    ckpt = stack.enter_context(nc.sbuf_tensor("aa_ck", [128, NC_ * 520], BF16)).ap()
    qkw = stack.enter_context(nc.sbuf_tensor("aa_qkw", [128, 2 * 1024], BF16)).ap()
    vw = stack.enter_context(nc.sbuf_tensor("aa_vw", [128, 8 * 1024], BF16)).ap()
    x2g = vw
    kcr = stack.enter_context(nc.sbuf_tensor("aa_kcr", [128, 2 * 1024], BF16)).ap()
    vcr = stack.enter_context(nc.sbuf_tensor("aa_vcr", [128, 2 * 1024], BF16)).ap()
    ktm = stack.enter_context(nc.sbuf_tensor("aa_ktm", [128, 2 * 1024], BF16)).ap()
    ypair = stack.enter_context(nc.sbuf_tensor("aa_yp", [128, 2 * 1024], BF16)).ap()
    yTg = stack.enter_context(nc.sbuf_tensor("aa_yTg", [128, PR * 512], BF16)).ap()
    x2r = stack.enter_context(nc.sbuf_tensor("aa_x2r", [128, DC * 512], BF16)).ap()
    r2 = stack.enter_context(nc.sbuf_tensor("aa_r2", [128, 4096], BF16)).ap()
    ssb = stack.enter_context(nc.sbuf_tensor("aa_ssb", [128, 2 * 128], BF16)).ap()
    S32 = stack.enter_context(nc.sbuf_tensor("aa_S", [128, 520], F32)).ap()
    Sini = stack.enter_context(nc.sbuf_tensor("aa_Si", [128, 520], BF16)).ap()
    ccs = stack.enter_context(nc.sbuf_tensor("aa_ccs", [128, 520], F32)).ap()
    recv = stack.enter_context(nc.sbuf_tensor("aa_rcv", [128, 520], F32)).ap()
    drow = stack.enter_context(nc.sbuf_tensor("aa_dr", [128, 4], F32)).ap()
    Wp = stack.enter_context(nc.sbuf_tensor("aa_Wp", [128, 2 * 1024], BF16)).ap()

    partA_stack = ExitStack()
    x2c = partA_stack.enter_context(nc.sbuf_tensor("aa_x2c", [128, 2 * 1024], BF16)).ap()
    vstage = partA_stack.enter_context(nc.sbuf_tensor("aa_vst", [128, 2 * 1024], BF16)).ap()
    kstage = partA_stack.enter_context(nc.sbuf_tensor("aa_kst", [128, 4 * 512], BF16)).ap()
    msb = partA_stack.enter_context(nc.sbuf_tensor("aa_msb", [128, 4 * 512], BF16)).ap()
    esb = partA_stack.enter_context(nc.sbuf_tensor("aa_esb", [128, 4 * 512], BF16)).ap()
    pa = [stack.enter_context(nc.psum_tensor(f"aa_pa{i}", [128, 512], F32)).ap() for i in range(4)]
    pq = [stack.enter_context(nc.psum_tensor(f"aa_pq{i}", [128, 1024], BF16)).ap() for i in range(2)]
    pv_stack = ExitStack()
    pv = pv_stack.enter_context(nc.psum_tensor("aa_pv", [128, 1024], F32)).ap()
    ln_state = {'r2': r2}

    x_in_r = x_in.ap().rearrange("c p t -> p c t")
    x_out_r = x_out.ap().rearrange("c p t -> p c t")
    kT_r = kT_hbm.ap().rearrange("c p t -> p c t")

    pa_guard = [('pe', 0), ('pe', 0), ('pe', 0), ('pe', 0)]
    pq_guard = [0, 0]

    def load_vw(sy):
        cd = wsqkv.cast_wait(23)
        sy.wait_ge(p.h(cd[0]), cd[1])
        for m in range(8):
            p.inc(sy.dma_start(out=vw[:, m * 1024:(m + 1) * 1024],
                               in_=wsqkv.block_ap(16 + m)), 'avw', 16)
    p.block.sync(load_vw)
    wl_vw = p.v('avw')

    x2c_guard = [0, 0]
    vst_guard = [0, 0]
    pv_guard = 0
    for tc in range(NC_):
        s = tc % 2
        g = x2c_guard[s]

        xsem = f"axc{s}"

        def lx(sy, tc=tc, s=s, g=g, xsem=xsem):
            if g > 0:
                sy.wait_ge(p.h('pe'), g)
            p.inc(sy.dma_start(out=x2c[:, s * 1024:(s + 1) * 1024].rearrange("p (k t) -> p k t", k=DC),
                               in_=x_in_r[:, :, tc * 128:(tc + 1) * 128]), xsem, 16)
        p.block.sync(lx)
        al_h = (xsem, p.v(xsem))
        gv = pv_guard

        def mmv(tt, s=s, al_h=al_h, gv=gv):
            tt.wait_ge(p.h('avw'), wl_vw)
            tt.wait_ge(p.h(al_h[0]), al_h[1])
            if gv > 0:
                tt.wait_ge(p.h('ve'), gv)
            for k in range(DC):
                for n in range(2):
                    ins = tt.matmul(
                        pv[:, n * 512:(n + 1) * 512],
                        x2c[:, s * 1024 + k * 128:s * 1024 + (k + 1) * 128],
                        vw.rearrange("p (m t) -> p m t", m=8)[:, n * 4:(n + 1) * 4,
                                                             k * 128:(k + 1) * 128],
                        start=(k == 0), stop=(k == DC - 1))
            p.inc(ins, 'pe')
        p.block.tensor(mmv)
        x2c_guard[s] = p.v('pe')
        pe_v = p.v('pe')
        gs = vst_guard[s]

        ssem = f"asv{s}"

        def ev(v, s=s, pe_v=pe_v, gs=gs, ssem=ssem):
            v.wait_ge(p.h('pe'), pe_v)
            if gs > 0:
                v.wait_ge(p.h(ssem), gs)
            ins = v.tensor_copy(vstage[:, s * 1024:(s + 1) * 1024], pv[:, :])
            p.inc(ins, 've')
        p.block.vector(ev)
        pv_guard = p.v('ve')
        ve_v = p.v('ve')

        def spl(g, tc=tc, s=s, ve_v=ve_v, ssem=ssem):
            g.wait_ge(p.h('ve'), ve_v)
            p.inc(g.dma_start(out=v_hbm.ap()[tc], in_=vstage[:, s * 1024:(s + 1) * 1024]),
                  ssem, 16)
        p.block.gpsimd(spl)
        vst_guard[s] = p.v(ssem)

    pv_stack.close()
    pb = {'sum': pa[0][0:1, 0:512], 'sq': pa[1][0:1, 0:512],
          'mu': pa[2][:, :], 'rs': pa[3][:, :]}

    pe_vdone = p.v('pe')
    x2g_guard = [pe_vdone, pe_vdone]
    kst_guard = [0, 0, 0, 0]
    msb_guard = 0
    esb_guard = 0
    qkw_guard = [0, 0]
    for g in range(8):
        s = g % 2
        gg = x2g_guard[s]

        gsem = f"axg{s}"

        def lg(sy, g=g, s=s, gg=gg, gsem=gsem):
            if gg > 0:
                sy.wait_ge(p.h('pe'), gg)
            p.inc(sy.dma_start(out=x2g[:, s * 4096:(s + 1) * 4096].rearrange("p (k t) -> p k t", k=DC),
                               in_=x_in_r[:, :, g * 512:(g + 1) * 512]), gsem, 16)
        p.block.sync(lg)
        al_g = (gsem, p.v(gsem))

        for mg in range(0, 16, 4):
            pe_m = [0] * 4
            for mi in range(4):
                m = mg + mi
                ws_ = m % 2
                gq = qkw_guard[ws_]
                qsem = f"aqk{ws_}"

                def lw(sy, m=m, ws_=ws_, gq=gq, qsem=qsem, cd=wsqkv.cast_wait(m)):
                    if gq > 0:
                        sy.wait_ge(p.h('pe'), gq)
                    sy.wait_ge(p.h(cd[0]), cd[1])
                    p.inc(sy.dma_start(out=qkw[:, ws_ * 1024:(ws_ + 1) * 1024],
                                       in_=wsqkv.block_ap(m)), qsem, 16)
                p.block.sync(lw)
                wl_h = (qsem, p.v(qsem))
                bk = m % 4
                gn, gvv = pa_guard[bk]

                def mmq(tt, s=s, m=m, ws_=ws_, bk=bk, gn=gn, gvv=gvv, wl_h=wl_h, al_g=al_g):
                    tt.wait_ge(p.h(wl_h[0]), wl_h[1])
                    tt.wait_ge(p.h(al_g[0]), al_g[1])
                    if gvv > 0:
                        tt.wait_ge(p.h(gn), gvv)
                    for k in range(DC):
                        ins = tt.matmul(pa[bk][:, :],
                                        qkw[:, ws_ * 1024 + k * 128:ws_ * 1024 + (k + 1) * 128],
                                        x2g[:, s * 4096 + k * 512:s * 4096 + (k + 1) * 512],
                                        start=(k == 0), stop=(k == DC - 1))
                    p.inc(ins, 'pe')
                p.block.tensor(mmq)
                qkw_guard[ws_] = p.v('pe')
                pe_m[mi] = p.v('pe')

            ac_base = p.v('ac')
            ve_base = p.v('ve')
            g_msb = msb_guard

            def vmins(v, pe_m=tuple(pe_m), g_msb=g_msb):
                for mi in range(4):
                    if g_msb > 0:
                        v.wait_ge(p.h('ac'), g_msb)
                    v.wait_ge(p.h('pe'), pe_m[mi])
                    ins = v.tensor_scalar(msb[:, mi * 512:(mi + 1) * 512],
                                          pa[mi % 4][:, :], 0.0, None, ALU.min)
                    p.inc(ins, 've')
            p.block.vector(vmins)

            g_esb = esb_guard

            def aexps(sc, ve_base=ve_base, g_esb=g_esb):
                for mi in range(4):
                    if g_esb > 0:
                        sc.wait_ge(p.h('ve'), g_esb)
                    sc.wait_ge(p.h('ve'), ve_base + mi + 1)
                    ins = sc.activation(esb[:, mi * 512:(mi + 1) * 512],
                                        msb[:, mi * 512:(mi + 1) * 512], AF.Exp)
                    p.inc(ins, 'ac')
            p.block.scalar(aexps)
            msb_guard = p.v('ac')

            for mi in range(4):
                m = mg + mi
                if m < 8:
                    dst = qT[:, m * TL + g * 512:m * TL + (g + 1) * 512]
                else:
                    dst = kstage[:, mi * 512:(mi + 1) * 512]

                def vphi(v, mi=mi, m=m, dst=dst, ac_need=ac_base + mi + 1,
                         gk=(kst_guard[mi] if m >= 8 else 0), ksem=f"ask{mi}"):
                    v.wait_ge(p.h('ac'), ac_need)
                    if m >= 8 and gk > 0:
                        v.wait_ge(p.h(ksem), gk)
                    ins = v.scalar_tensor_tensor(dst, pa[mi % 4][:, :], 1.0,
                                                 esb[:, mi * 512:(mi + 1) * 512],
                                                 ALU.add, ALU.max)
                    p.inc(ins, 've')
                p.block.vector(vphi)
                pa_guard[mi % 4] = ('ve', p.v('ve'))
                ve_phi = p.v('ve')

                if m >= 8:
                    def spk(gp, m=m, g=g, mi=mi, ve_phi=ve_phi, ksem=f"ask{mi}"):
                        gp.wait_ge(p.h('ve'), ve_phi)
                        p.inc(gp.dma_start(out=kT_hbm.ap()[m - 8][:, g * 512:(g + 1) * 512],
                                           in_=kstage[:, mi * 512:(mi + 1) * 512]), ksem, 16)
                    p.block.gpsimd(spk)
                    kst_guard[mi] = p.v(f"ask{mi}")
            esb_guard = p.v('ve')
        x2g_guard[s] = p.v('pe')
    sp_partA = [('asv0', p.v('asv0')), ('asv1', p.v('asv1'))] + \
               [(f'ask{i}', p.v(f'ask{i}')) for i in range(4)]
    if DBG.get('attn_kv_tap') is not None:
        kvt = DBG['attn_kv_tap']

        def dkv(gp, sp_partA=tuple(sp_partA)):
            for sn, sv in sp_partA:
                if sv > 0:
                    gp.wait_ge(p.h(sn), sv)
            p.inc(gp.dma_start(out=kvt[0].rearrange("p (m t) -> p m t", m=DC),
                               in_=kT_r[:, :, 0:128]), 'sp', 16)
            p.inc(gp.dma_start(out=kvt[1], in_=v_hbm.ap()[0]), 'sp', 16)
            p.inc(gp.dma_start(out=kvt[2].rearrange("p (m t) -> p m t", t=128),
                               in_=qT.rearrange("p (m t) -> p m t", m=PR)[:, :, 0:128]), 'sp', 16)
        p.block.gpsimd(dkv)

    partA_stack.close()
    Rb = stack.enter_context(nc.sbuf_tensor("aa_R", [128, 2 * DC * 512], BF16)).ap()
    xo3 = stack.enter_context(nc.sbuf_tensor("aa_xo3", [128, DC * 512], BF16)).ap()

    def zeroS(g):
        ins = g.memset(S32, 0.0)
        p.inc(ins, 'pool')
    p.block.gpsimd(zeroS)
    pool_z = p.v('pool')

    def zwait(v):
        v.wait_ge(p.h('pool'), pool_z)
    p.block.vector(zwait)

    kcr_guard = [0, 0]
    vcr_guard = [0, 0]
    ktm_guard = [0, 0]
    ps_guard = [('pe', 0), ('pe', 0), ('pe', 0), ('pe', 0)]
    for c in range(NC_):
        s = c % 2
        g1, g2 = kcr_guard[s], vcr_guard[s]

        ksem, vsem = f"akc{s}", f"avc{s}"

        def lkv(sy, c=c, s=s, g1=g1, g2=g2, ksem=ksem, vsem=vsem, spA=tuple(sp_partA)):
            for sn, sv in spA:
                if sv > 0:
                    sy.wait_ge(p.h(sn), sv)
            if g1 > 0:
                sy.wait_ge(p.h('pe'), g1)
            p.inc(sy.dma_start(out=kcr[:, s * 1024:(s + 1) * 1024].rearrange("p (m t) -> p m t", m=DC),
                               in_=kT_r[:, :, c * 128:(c + 1) * 128]), ksem, 16)
            if g2 > 0:
                sy.wait_ge(p.h('pe'), g2)
            p.inc(sy.dma_start(out=vcr[:, s * 1024:(s + 1) * 1024], in_=v_hbm.ap()[c]), vsem, 16)
        p.block.sync(lkv)
        al_c = [(ksem, p.v(ksem)), (vsem, p.v(vsem))]
        gq_ = pq_guard[s]

        def ktr(tt, c=c, s=s, al_c=tuple(al_c), gq_=gq_):
            for sn, sv in al_c:
                tt.wait_ge(p.h(sn), sv)
            if gq_ > 0:
                tt.wait_ge(p.h('ve'), gq_)
            for m in range(PR):
                ins = tt.transpose(pq[s][:, m * 128:(m + 1) * 128],
                                   kcr[:, s * 1024 + m * 128:s * 1024 + (m + 1) * 128],
                                   cst['identbf'])
            v_ = p.inc(ins, 'pe')
            _pedrain(p, tt, v_)
        p.block.tensor(ktr)
        pe_tr = p.v('pe')

        def kev(v, s=s, pe_tr=pe_tr):
            v.wait_ge(p.h('pe'), pe_tr)
            ins = v.tensor_copy(ktm[:, s * 1024:(s + 1) * 1024], pq[s][:, :])
            p.inc(ins, 've')
        p.block.vector(kev)
        pq_guard[s] = p.v('ve')
        ktm_guard[s] = 0
        ve_k = p.v('ve')

        for pr in range(PR):
            bk = pr % 4
            gn, gv_ = ps_guard[bk]

            def smm(tt, c=c, s=s, pr=pr, bk=bk, gn=gn, gv_=gv_, ve_k=ve_k, al_c=tuple(al_c)):
                tt.wait_ge(p.h('ve'), ve_k)
                for sn, sv in al_c:
                    tt.wait_ge(p.h(sn), sv)
                if gv_ > 0:
                    tt.wait_ge(p.h(gn), gv_)
                kb = ktm[:, s * 1024 + pr * 128:s * 1024 + pr * 128 + 64]
                ko = ktm[:, s * 1024 + pr * 128 + 64:s * 1024 + pr * 128 + 128]
                ve_ = vcr[:, s * 1024 + (2 * pr) * 64:s * 1024 + (2 * pr) * 64 + 64]
                vo = vcr[:, s * 1024 + (2 * pr + 1) * 64:s * 1024 + (2 * pr + 1) * 64 + 64]
                tt.matmul(pa[bk][0:64, 0:64], kb, ve_, start=True, stop=False)
                tt.matmul(pa[bk][0:64, 64:65], kb, cst['ones_bf'], start=False, stop=True)
                tt.matmul(pa[bk][64:128, 0:64], ko, vo, start=True, stop=False,
                          tile_position=(0, 64))
                ins = tt.matmul(pa[bk][64:128, 64:65], ko, cst['ones_bf'], start=False,
                                stop=True, tile_position=(0, 64))
                v_ = p.inc(ins, 'pe')
                _pedrain(p, tt, v_)
            p.block.tensor(smm)
            pe_s = p.v('pe')

            def sadd(v, c=c, pr=pr, bk=bk, pe_s=pe_s):
                v.tensor_copy(ckpt[:, c * 520 + pr * 65:c * 520 + (pr + 1) * 65],
                              S32[:, pr * 65:(pr + 1) * 65])
                v.wait_ge(p.h('pe'), pe_s)
                ins = v.tensor_tensor(S32[:, pr * 65:(pr + 1) * 65],
                                      S32[:, pr * 65:(pr + 1) * 65],
                                      pa[bk][:, 0:65], ALU.add)
                p.inc(ins, 've')
            p.block.vector(sadd)
            ps_guard[bk] = ('ve', p.v('ve'))
        kcr_guard[s] = vcr_guard[s] = p.v('pe')
    pa_guard = ps_guard

    if DBG.get('ck_tap') is not None:
        ckt = DBG['ck_tap']
        vck = p.v('ve')

        def dck(gp, vck=vck):
            gp.wait_ge(p.h('ve'), vck)
            p.inc(gp.dma_start(out=ckt[0], in_=ckpt[:, 0:1040]), 'sp', 16)
            p.inc(gp.dma_start(out=ckt[1][:, 0:520], in_=S32), 'sp', 16)
        p.block.gpsimd(dck)

    def vccs(v):
        ins = v.tensor_scalar(ccs, S32, cst['rolei'], None, ALU.mult)
        p.inc(ins, 've')
    p.block.vector(vccs)
    ve_ccs = p.v('ve')

    def stccs(sy, ve_ccs=ve_ccs):
        sy.wait_ge(p.h('ve'), ve_ccs)
        p.inc(sy.dma_start(out=cc_in.ap(), in_=ccs), 'accs', 16)
    p.block.sync(stccs)
    al_ccs = p.v('accs')

    def docc(gp, al_ccs=al_ccs):
        gp.wait_ge(p.h('accs'), al_ccs)
        p.inc(gp.collective_compute(
            "AllReduce", ALU.add,
            replica_groups=(groups if groups is not None else [[0, 1], [2, 3], [4, 5], [6, 7]]),
            ins=[cc_in.ap().opt()],
            outs=[cc_out.ap().opt()]), 'cc', 1)
    p.block.gpsimd(docc)
    cc_v = p.v('cc')

    def ldcc(sy, cc_v=cc_v):
        sy.wait_ge(p.h('cc'), cc_v)
        p.inc(sy.dma_start(out=recv, in_=cc_out.ap()), 'arcv', 16)
    p.block.sync(ldcc)
    al_rcv = p.v('arcv')

    def vsini(v, al_rcv=al_rcv):
        v.wait_ge(p.h('arcv'), al_rcv)
        ins = v.tensor_scalar(Sini, recv, cst['role'], None, ALU.mult)
        p.inc(ins, 've')
    p.block.vector(vsini)
    ve_sini = p.v('ve')

    x2r_guard = 0
    pend_ln3 = None
    rb3_guard = [0, 0]
    yp_guard = [0, 0]
    ssb_guard = [0, 0]
    dr_guard = [0, 0]
    Wp_guard = [0, 0]
    wp_i = 0
    yTg_guard = 0
    xo_guard = 0
    for g in range(8):
        gx = x2r_guard

        def lxr(sy, g=g, gx=gx):
            if gx > 0:
                sy.wait_ge(p.h('ve'), gx)
            p.inc(sy.dma_start(out=x2r.rearrange("p (k t) -> p k t", k=DC),
                               in_=x_in_r[:, :, g * 512:(g + 1) * 512]), 'axr', 16)
        p.block.sync(lxr)
        al_xr = p.v('axr')

        for ci in range(4):
            c = g * 4 + ci
            s = c % 2
            g1, g2 = kcr_guard[s], vcr_guard[s]

            ksem, vsem = f"akc{s}", f"avc{s}"

            def lkv2(sy, c=c, s=s, g1=g1, g2=g2, ksem=ksem, vsem=vsem):
                if g1 > 0:
                    sy.wait_ge(p.h('pe'), g1)
                p.inc(sy.dma_start(out=kcr[:, s * 1024:(s + 1) * 1024].rearrange("p (m t) -> p m t", m=DC),
                                   in_=kT_r[:, :, c * 128:(c + 1) * 128]), ksem, 16)
                if g2 > 0:
                    sy.wait_ge(p.h('pe'), g2)
                p.inc(sy.dma_start(out=vcr[:, s * 1024:(s + 1) * 1024], in_=v_hbm.ap()[c]), vsem, 16)
            p.block.sync(lkv2)
            al_c = [(ksem, p.v(ksem)), (vsem, p.v(vsem))]

            def emit_scores(hd):
                pr, hf = hd // 2, hd % 2
                base = hf * 64
                bk = hd % 4
                gn, gv_ = pa_guard[bk]
                tp = (64, 0) if hf else None

                def smm2(tt, c=c, s=s, pr=pr, hf=hf, base=base, bk=bk, gn=gn, gv_=gv_,
                         al_c=tuple(al_c), tp=tp, first=(c == 0 and hd == 0), ve_sini=ve_sini):
                    if first:
                        tt.wait_ge(p.h('ve'), ve_sini)
                    for sn, sv in al_c:
                        tt.wait_ge(p.h(sn), sv)
                    if gv_ > 0:
                        tt.wait_ge(p.h(gn), gv_)
                    ins = tt.matmul(pa[bk][:, 0:128],
                                    kcr[base:base + 64, s * 1024 + pr * 128:s * 1024 + (pr + 1) * 128],
                                    qT[base:base + 64, pr * TL + c * 128:pr * TL + (c + 1) * 128],
                                    start=True, stop=True, tile_position=tp)
                    p.inc(ins, 'pe')
                p.block.tensor(smm2)
                pe_sc = p.v('pe')
                gsb_ = ssb_guard[hd % 2]

                def mask(v, bk=bk, hh=hd % 2, pe_sc=pe_sc, gsb_=gsb_):
                    v.wait_ge(p.h('pe'), pe_sc)
                    if gsb_ > 0:
                        v.wait_ge(p.h('pe'), gsb_)
                    ins = v.tensor_tensor(ssb[:, hh * 128:(hh + 1) * 128], cst['maskT'],
                                          pa[bk][:, 0:128], ALU.mult)
                    p.inc(ins, 've')
                p.block.vector(mask)
                return p.v('ve')

            def emit_numdiv(hd, ve_msk):
                pr, hf = hd // 2, hd % 2
                base = hf * 64
                bk = hd % 4
                tp = (64, 0) if hf else None

                def nmm(tt, c=c, s=s, pr=pr, hf=hf, hd=hd, base=base, bk=bk, hh=hd % 2,
                        ve_msk=ve_msk, tp=tp):
                    tt.wait_ge(p.h('ve'), ve_msk)
                    tt.matmul(pa[bk][:, 0:64], ssb[:, hh * 128:(hh + 1) * 128],
                              vcr[:, s * 1024 + hd * 64:s * 1024 + (hd + 1) * 64],
                              start=True, stop=False)
                    tt.matmul(pa[bk][:, 64:65], ssb[:, hh * 128:(hh + 1) * 128],
                              cst['ones_bf'], start=False, stop=False)
                    qsl = qT[base:base + 64, pr * TL + c * 128:pr * TL + (c + 1) * 128]
                    tt.matmul(pa[bk][:, 0:65], qsl,
                              ckpt[base:base + 64, c * 520 + pr * 65:c * 520 + (pr + 1) * 65],
                              start=False, stop=False, tile_position=tp)
                    ins = tt.matmul(pa[bk][:, 0:65], qsl,
                                    Sini[base:base + 64, pr * 65:(pr + 1) * 65],
                                    start=False, stop=True, tile_position=tp)
                    p.inc(ins, 'pe')
                p.block.tensor(nmm)
                pe_n = p.v('pe')

                def vden(v, bk=bk, hh=hd % 2, pe_n=pe_n):
                    v.wait_ge(p.h('pe'), pe_n)
                    c1 = p.inc(v.tensor_scalar(drow[:, hh * 2:hh * 2 + 1], pa[bk][:, 64:65],
                                               1e-6, None, ALU.add), 've')
                    v.wait_ge(p.h('ve'), c1)
                    ins = v.reciprocal(drow[:, hh * 2 + 1:hh * 2 + 2],
                                       drow[:, hh * 2:hh * 2 + 1])
                    p.inc(ins, 've')
                p.block.vector(vden)
                ve_d = p.v('ve')
                gyp = yp_guard[s] if hd == 0 else 0

                def ydiv(v, bk=bk, hh=hd % 2, s=s, pr=pr, base=base, ve_d=ve_d, gyp=gyp):
                    v.wait_ge(p.h('ve'), ve_d)
                    if gyp > 0:
                        v.wait_ge(p.h('pe'), gyp)
                    ins = v.tensor_scalar(
                        ypair[:, s * 1024 + pr * 128 + base:s * 1024 + pr * 128 + base + 64],
                        pa[bk][:, 0:64], drow[:, hh * 2 + 1:hh * 2 + 2], None, ALU.mult)
                    p.inc(ins, 've')
                p.block.vector(ydiv)
                pa_guard[bk] = ('ve', p.v('ve'))
                ssb_guard[hd % 2] = pe_n

            vmsk = [0] * NH
            for hd in range(NH):
                vmsk[hd] = emit_scores(hd)
                if hd >= 1:
                    emit_numdiv(hd - 1, vmsk[hd - 1])
            emit_numdiv(NH - 1, vmsk[NH - 1])
            kcr_guard[s] = vcr_guard[s] = p.v('pe')
            ac_y = p.v('ve')
            tapv = 0
            if DBG.get('yp_tap') is not None and c == DBG['attn_c']:
                tap2 = DBG['yp_tap']

                def dyp(v, s=s, ac_y=ac_y):
                    v.wait_ge(p.h('ve'), ac_y)
                    ins = v.tensor_copy(tap2, ypair[:, s * 1024:(s + 1) * 1024])
                    p.inc(ins, 've')
                p.block.vector(dyp)
                tapv = p.v('ve')
            gq_ = pq_guard[s]

            def ytr(tt, s=s, ac_y=ac_y, gq_=gq_, tapv=tapv):
                tt.wait_ge(p.h('ve'), ac_y)
                if tapv > 0:
                    tt.wait_ge(p.h('ve'), tapv)
                if gq_ > 0:
                    tt.wait_ge(p.h('ve'), gq_)
                for pr2 in range(PR):
                    ins = tt.transpose(pq[s][:, pr2 * 128:(pr2 + 1) * 128],
                                       ypair[:, s * 1024 + pr2 * 128:s * 1024 + (pr2 + 1) * 128],
                                       cst['identbf'])
                v_ = p.inc(ins, 'pe')
                _pedrain(p, tt, v_)
            p.block.tensor(ytr)
            yp_guard[s] = p.v('pe')
            pe_y = p.v('pe')
            gyt = yTg_guard if ci == 0 else 0
            gsp = xo_guard if ci == 0 else 0

            def yev(v, s=s, ci=ci, pe_y=pe_y, gyt=gyt, gsp=gsp):
                v.wait_ge(p.h('pe'), pe_y)
                if gyt > 0:
                    v.wait_ge(p.h('pe'), gyt)
                if gsp > 0:
                    v.wait_ge(p.h('spx3'), gsp)
                ins = v.tensor_copy(
                    yTg.rearrange("p (m t) -> p m t", m=PR)[:, :, ci * 128:(ci + 1) * 128],
                    pq[s][:, :].rearrange("p (m t) -> p m t", m=PR))
                p.inc(ins, 've')
            p.block.vector(yev)
            pq_guard[s] = p.v('ve')
        ve_y = p.v('ve')

        r_sl = [Rb[:, (g % 2) * DC * 512 + mc * 512:(g % 2) * DC * 512 + (mc + 1) * 512]
                for mc in range(DC)]
        for m in range(DC):
            ws_ = wp_i % 2
            wp_i += 1
            gw = Wp_guard[ws_]

            psem = f"awp{ws_}"

            def lwp(sy, m=m, ws_=ws_, gw=gw, psem=psem, cd=wsproj.cast_wait(m)):
                if gw > 0:
                    sy.wait_ge(p.h('pe'), gw)
                sy.wait_ge(p.h(cd[0]), cd[1])
                p.inc(sy.dma_start(out=Wp[:, ws_ * 1024:(ws_ + 1) * 1024],
                                   in_=wsproj.block_ap(m)), psem, 16)
            p.block.sync(lwp)
            wl_p = (psem, p.v(psem))
            bk = m % 4
            gn, gv_ = pa_guard[bk]

            def pmm(tt, m=m, ws_=ws_, bk=bk, gn=gn, gv_=gv_, wl_p=wl_p, ve_y=ve_y):
                tt.wait_ge(p.h(wl_p[0]), wl_p[1])
                tt.wait_ge(p.h('ve'), ve_y)
                if gv_ > 0:
                    tt.wait_ge(p.h(gn), gv_)
                for kp in range(PR):
                    ins = tt.matmul(pa[bk][:, :],
                                    Wp[:, ws_ * 1024 + kp * 128:ws_ * 1024 + (kp + 1) * 128],
                                    yTg[:, kp * 512:(kp + 1) * 512],
                                    start=(kp == 0), stop=(kp == PR - 1))
                v_ = p.inc(ins, 'pe')
                _pedrain(p, tt, v_)
            p.block.tensor(pmm)
            Wp_guard[ws_] = p.v('pe')
            pe_p = p.v('pe')

            g_rb3 = rb3_guard[g % 2]

            def resid(v, m=m, bk=bk, pe_p=pe_p, al_xr=al_xr, g_rb3=g_rb3):
                v.wait_ge(p.h('pe'), pe_p)
                v.wait_ge(p.h('axr'), al_xr)
                if g_rb3 > 0:
                    v.wait_ge(p.h('pe'), g_rb3)
                ins = v.tensor_tensor(r_sl[m], x2r[:, m * 512:(m + 1) * 512],
                                      pa[bk][:, :], ALU.add)
                p.inc(ins, 've')
            p.block.vector(resid)
            pa_guard[bk] = ('ve', p.v('ve'))
        x2r_guard = p.v('ve')
        yTg_guard = p.v('pe')
        ve_resid_done = p.v('ve')

        def make_ln3(g=g, gsl=(g % 2), ve_resid_done=ve_resid_done):
            def flush():
                nonlocal xo_guard, yTg_guard
                r_sl2 = [Rb[:, gsl * DC * 512 + mc * 512:gsl * DC * 512 + (mc + 1) * 512]
                         for mc in range(DC)]
                o_sl = [xo3[:, mc * 512:(mc + 1) * 512] for mc in range(DC)]
                ln_state['rows_read_ve'] = max(ln_state.get('rows_read_ve', 0), p.v('ve'))
                ln_state['norm_done_ve'] = max(ln_state.get('norm_done_ve', 0), p.v('ve'))
                if xo_guard > 0:
                    def vwait(v, g_=xo_guard):
                        v.wait_ge(p.h('spx3'), g_)
                    p.block.vector(vwait)
                emit_ln512(p, cst, r_sl2, o_sl, pb, ln_state, ve_r=ve_resid_done)
                rb3_guard[gsl] = p.v('pe')
                for i in range(4):
                    pa_guard[i] = ('ve', p.v('ve'))
                ve_done = p.v('ve')

                def store(gp, g=g, ve_done=ve_done):
                    gp.wait_ge(p.h('ve'), ve_done)
                    p.inc(gp.dma_start(out=x_out_r[:, :, g * 512:(g + 1) * 512],
                                       in_=xo3.rearrange("p (c t) -> p c t", c=DC)), 'spx3', 16)
                p.block.gpsimd(store)
                xo_guard = p.v('spx3')
            return flush

        if pend_ln3 is not None:
            pend_ln3()
        pend_ln3 = make_ln3()
    pend_ln3()

    return ('spx3', p.v('spx3'))


def emit_l1ffn(p, cst, wsf1, wsf2, wsh1, wsh2, wsf3, x_in, y_out, stack, x_ready=None):
    nc = p.nc
    NT = 1024
    ntiles = TL // NT
    A = stack.enter_context(nc.sbuf_tensor("lf_A", [128, HC * NT], BF16)).ap()
    Bh = stack.enter_context(nc.sbuf_tensor("lf_B", [128, HC * NT], BF16)).ap()
    X3 = stack.enter_context(nc.sbuf_tensor("lf_x", [128, DC * NT], BF16)).ap()
    Wb = stack.enter_context(nc.sbuf_tensor("lf_W", [128, 2 * 4096], BF16)).ap()
    Rb = stack.enter_context(nc.sbuf_tensor("lf_R", [128, DC * NT], BF16)).ap()
    sil = stack.enter_context(nc.sbuf_tensor("lf_sil", [128, 2 * NT], BF16)).ap()
    yst = stack.enter_context(nc.sbuf_tensor("lf_y", [128, 512], F32)).ap()
    rowb = stack.enter_context(nc.sbuf_tensor("lf_rows", [128, 8], F32)).ap()
    pbank = [stack.enter_context(nc.psum_tensor(f"lf_p{i}", [128, 512], F32)).ap() for i in range(4)]
    pq = [stack.enter_context(nc.psum_tensor(f"lf_q{i}", [128, 1024], BF16)).ap() for i in range(2)]
    rt = stack.enter_context(nc.psum_tensor("lf_rt", [128, 1024], F32)).ap()

    x_in_r = x_in.ap().rearrange("c p t -> p c t")
    rb_read_guard = [0]
    yout_guard = [0, 0]
    wslot = [Wb[:, i * 4096:(i + 1) * 4096] for i in range(2)]
    wguard = [0, 0]
    wsl_i = 0
    r2t = sil[:, 0:1024]
    bank_guard = [('ac', 0)] * 4
    sil_guard = [0, 0]
    bsc_guard = [0, 0]
    pq_guard = [0, 0]
    rt_guard_ac = 0
    srow, s2row, mu_c, e2_c, ng2, var_c, sd_c, rs_c = (rowb[:, i:i + 1] for i in range(8))

    def wload(ws, m, cols):
        nonlocal wsl_i
        s = wsl_i % 2
        wsl_i += 1
        g = wguard[s]
        cd = ws.cast_done

        wsem = f"lfw{s}"

        def go(sy, ws=ws, m=m, s=s, g=g, cd=ws.cast_wait(m), cols=cols, wsem=wsem):
            if g > 0:
                sy.wait_ge(p.h('pe'), g)
            sy.wait_ge(p.h(cd[0]), cd[1])
            p.inc(sy.dma_start(out=wslot[s][:, 0:cols], in_=ws.block_ap(m)), wsem, 16)
        p.block.sync(go)
        return s, (wsem, p.v(wsem))

    x3guards = [0, 0]
    al_xs = [None] * (ntiles + 1)

    def load_x3(t):
        s = 0
        g_x3 = x3guards[s]
        sem = f"lfx{s}"

        def loadx(sy, t=t, s=s, g_x3=g_x3, sem=sem):
            if x_ready is not None and x_ready[1] > 0:
                sy.wait_ge(p.h(x_ready[0]), x_ready[1])
            if g_x3 > 0:
                sy.wait_ge(p.h('ve'), g_x3)
            p.inc(sy.dma_start(out=X3[:, s * DC * NT:(s + 1) * DC * NT].rearrange("p (c t) -> p c t", c=DC),
                               in_=x_in_r[:, :, t * NT:(t + 1) * NT]), sem, 16)
        p.block.sync(loadx)
        return (sem, p.v(sem))

    for t in range(ntiles):
        al_x = load_x3(t)
        X3t = X3[:, 0:DC * NT]

        def mm_stage(ws, m, src_buf, kK, b0, wl_extra=None):
            s, wlv = wload(ws, m, kK * 128)
            g0, g1 = bank_guard[b0], bank_guard[b0 + 1]

            def mm(tt, s=s, wlv=wlv, m=m, src_buf=src_buf, kK=kK, b0=b0, g0=g0, g1=g1,
                   wl_extra=wl_extra):
                tt.wait_ge(p.h(wlv[0]), wlv[1])
                if wl_extra is not None:
                    for gn, gv in wl_extra:
                        if gv > 0:
                            tt.wait_ge(p.h(gn), gv)
                for gn, gv in {g0, g1}:
                    if gv > 0:
                        tt.wait_ge(p.h(gn), gv)
                for k in range(kK):
                    for n in range(2):
                        ins = tt.matmul(pbank[b0 + n][:, :],
                                        wslot[s][:, k * 128:(k + 1) * 128],
                                        src_buf(k, n),
                                        start=(k == 0), stop=(k == kK - 1))
                p.inc(ins, 'pe')
            p.block.tensor(mm)
            wguard[s] = p.v('pe')
            return p.v('pe')

        xsrc = lambda k, n, X3t=X3t: X3t[:, k * NT + n * 512:k * NT + (n + 1) * 512]

        for m in range(HC):
            pe_a = mm_stage(wsf1, m, xsrc, DC, 0, wl_extra=[al_x])
            pe_b = mm_stage(wsf2, m, xsrc, DC, 2, wl_extra=[al_x])
            mp = m % 2
            g_sil = sil_guard[mp]
            g_bsc = bsc_guard[mp]

            def ev_a(sc, m=m, mp=mp, pe_a=pe_a, g_sil=g_sil):
                sc.wait_ge(p.h('pe'), pe_a)
                if g_sil > 0:
                    sc.wait_ge(p.h('ve'), g_sil)
                for n in range(2):
                    ins = sc.activation(sil[:, mp * NT + n * 512:mp * NT + (n + 1) * 512],
                                        pbank[n][:, :], AF.Silu, scale=cst['s1'][:, m:m + 1])
                p.inc(ins, 'ac')
            p.block.scalar(ev_a)
            bank_guard[0] = bank_guard[1] = ('ac', p.v('ac'))

            ac_here = p.v('ac')

            def hmul(v, m=m, mp=mp, ac_here=ac_here, pe_b=pe_b):
                v.wait_ge(p.h('ac'), ac_here)
                v.wait_ge(p.h('pe'), pe_b)
                for n in range(2):
                    ins = v.scalar_tensor_tensor(
                        A[:, m * NT + n * 512:m * NT + (n + 1) * 512],
                        pbank[2 + n][:, :], cst['s2'][:, m:m + 1],
                        sil[:, mp * NT + n * 512:mp * NT + (n + 1) * 512],
                        ALU.mult, ALU.mult)
                p.inc(ins, 've')
            p.block.vector(hmul)
            bank_guard[2] = bank_guard[3] = ('ve', p.v('ve'))
            sil_guard[mp] = bsc_guard[mp] = p.v('ve')
        ve_hin = p.v('ve')

        hsrc = lambda k, n: A[:, k * NT + n * 512:k * NT + (n + 1) * 512]
        for m in range(HC):
            b0 = (m % 2) * 2
            pe_m = mm_stage(wsh1, m, hsrc, HC, b0, wl_extra=[('ve', ve_hin)])

            def ev(sc, m=m, b0=b0, pe_m=pe_m):
                sc.wait_ge(p.h('pe'), pe_m)
                for n in range(2):
                    ins = sc.activation(Bh[:, m * NT + n * 512:m * NT + (n + 1) * 512],
                                        pbank[b0 + n][:, :], AF.Gelu, scale=cst['sh1'][:, m:m + 1])
                p.inc(ins, 'ac')
            p.block.scalar(ev)
            bank_guard[b0] = bank_guard[b0 + 1] = ('ac', p.v('ac'))
        ac_h1 = p.v('ac')

        h1src = lambda k, n: Bh[:, k * NT + n * 512:k * NT + (n + 1) * 512]
        for m in range(HC):
            b0 = (m % 2) * 2
            pe_m = mm_stage(wsh2, m, h1src, HC, b0, wl_extra=[('ac', ac_h1)])

            def ev(sc, m=m, b0=b0, pe_m=pe_m):
                sc.wait_ge(p.h('pe'), pe_m)
                for n in range(2):
                    ins = sc.activation(A[:, m * NT + n * 512:m * NT + (n + 1) * 512],
                                        pbank[b0 + n][:, :], AF.Gelu, scale=cst['sh2'][:, m:m + 1])
                p.inc(ins, 'ac')
            p.block.scalar(ev)
            bank_guard[b0] = bank_guard[b0 + 1] = ('ac', p.v('ac'))
        ac_h2 = p.v('ac')

        h2src = lambda k, n: A[:, k * NT + n * 512:k * NT + (n + 1) * 512]
        for m3 in range(DC):
            b0 = (m3 % 2) * 2
            pe_m = mm_stage(wsf3, m3, h2src, HC, b0, wl_extra=[('ac', ac_h2)])
            g_rb = rb_read_guard[0]

            def resid(v, m3=m3, b0=b0, pe_m=pe_m, g_rb=g_rb, al_x=al_x, X3t=X3t):
                v.wait_ge(p.h('pe'), pe_m)
                v.wait_ge(p.h(al_x[0]), al_x[1])
                if g_rb > 0:
                    v.wait_ge(p.h('pe'), g_rb)
                for n in range(2):
                    ins = v.scalar_tensor_tensor(
                        Rb[:, m3 * NT + n * 512:m3 * NT + (n + 1) * 512],
                        pbank[b0 + n][:, :], cst['s3'][:, m3:m3 + 1],
                        X3t[:, m3 * NT + n * 512:m3 * NT + (n + 1) * 512],
                        ALU.mult, ALU.add)
                p.inc(ins, 've')
            p.block.vector(resid)
            bank_guard[b0] = bank_guard[b0 + 1] = ('ve', p.v('ve'))
        x3guards[0] = p.v('ve')
        ve_r = p.v('ve')

        for tc in range(DC):
            qp = tc % 2
            g_pq = pq_guard[qp]

            def trans(tt, tc=tc, qp=qp, g_pq=g_pq, ve_r=ve_r):
                tt.wait_ge(p.h('ve'), ve_r)
                if g_pq > 0:
                    tt.wait_ge(p.h('ve'), g_pq)
                for fc in range(DC):
                    ins = tt.transpose(pq[qp][:, fc * 128:(fc + 1) * 128],
                                       Rb[:, fc * NT + tc * 128:fc * NT + (tc + 1) * 128],
                                       cst['identbf'])
                p.inc(ins, 'pe')
            p.block.tensor(trans)
            pe_tr = p.v('pe')
            g_rt = rt_guard_ac

            def ve_sum(v, qp=qp, pe_tr=pe_tr, g_rt=g_rt):
                v.wait_ge(p.h('pe'), pe_tr)
                if g_rt > 0:
                    v.wait_ge(p.h('ac'), g_rt)
                ins = v.tensor_scalar(rt, pq[qp][:, :], 1.0, None, ALU.mult, ALU.add,
                                      accum_out=srow)
                p.inc(ins, 've')
            p.block.vector(ve_sum)
            ve_s = p.v('ve')
            pq_guard[qp] = ve_s

            def ac_sq(sc, ve_s=ve_s):
                sc.wait_ge(p.h('ve'), ve_s)
                ins = sc.activation(r2t, rt, AF.Square, accum_out=s2row)
                p.inc(ins, 'ac')
            p.block.scalar(ac_sq)
            ac_sqv = p.v('ac')
            rt_guard_ac = ac_sqv

            def ve_stats(v, ac_sqv=ac_sqv):
                v.wait_ge(p.h('ac'), ac_sqv)
                v.tensor_scalar(mu_c, srow, 1.0 / 1024.0, None, ALU.mult)
                c1 = p.inc(v.scalar_tensor_tensor(ng2, srow, -1.0 / 1024.0, srow,
                                                  ALU.mult, ALU.mult), 've')
                v.wait_ge(p.h('ve'), c1)
                ins = v.tensor_tensor(var_c, s2row, ng2, ALU.add)
                p.inc(ins, 've')
            p.block.vector(ve_stats)
            ve_st = p.v('ve')

            def ac_sd(sc, ve_st=ve_st):
                sc.wait_ge(p.h('ve'), ve_st)
                ins = sc.activation(sd_c, var_c, AF.Sqrt, bias=cst['eps_ln'],
                                    scale=1.0 / 1024.0)
                p.inc(ins, 'ac')
            p.block.scalar(ac_sd)
            ac_sdv = p.v('ac')

            for half in range(2):
                ysl = yst[:, 0:512]
                g_y = yout_guard[0]

                def ve_fin(v, ac_sdv=ac_sdv, ysl=ysl, g_y=g_y, half=half):
                    v.wait_ge(p.h('ac'), ac_sdv)
                    if half == 0:
                        c1 = p.inc(v.reciprocal(rs_c, sd_c), 've')
                        v.wait_ge(p.h('ve'), c1)
                    if g_y > 0:
                        v.wait_ge(p.h('out'), g_y)
                    ins = v.tensor_scalar(ysl, rt[:, half * 512:(half + 1) * 512],
                                          mu_c, rs_c, ALU.subtract, ALU.mult)
                    p.inc(ins, 've')
                p.block.vector(ve_fin)
                ve_f = p.v('ve')

                def store(sy, t=t, tc=tc, ysl=ysl, ve_f=ve_f, half=half):
                    sy.wait_ge(p.h('ve'), ve_f)
                    p.inc(sy.dma_start(
                        out=y_out.ap()[t * NT + tc * 128:t * NT + (tc + 1) * 128,
                                       half * 512:(half + 1) * 512],
                        in_=ysl), 'out', 16)
                p.block.sync(store)
                yout_guard[0] = p.v('out')
        rb_read_guard[0] = p.v('pe')

    return ('out', p.v('out'))


def phase_barrier(p):
    vals = [(name, s[1]) for name, s in p._sems.items()
            if s[1] > 0 and not name.startswith('cast') and name != 'c_al']

    def w(e):
        for name, v in vals:
            e.wait_ge(p.h(name), v)
    for eng in ('tensor', 'vector', 'scalar', 'sync', 'gpsimd'):
        getattr(p.block, eng)(w)


def alloc_psum(stack, nc, tag):
    pb = {}
    for i in range(6):
        pb[f'p{i}'] = stack.enter_context(nc.psum_tensor(f"{tag}_p{i}", [128, 512], F32)).ap()
    pb['q6'] = stack.enter_context(nc.psum_tensor(f"{tag}_p6", [128, 1024], BF16)).ap()
    pb['q7'] = stack.enter_context(nc.psum_tensor(f"{tag}_p7", [128, 1024], BF16)).ap()
    return pb


def make_consts(p, stack, const_in, row1_in):
    nc = p.nc
    cst = {}
    csb = stack.enter_context(nc.sbuf_tensor("consts", [128, 456], F32)).ap()
    row1 = stack.enter_context(nc.sbuf_tensor("row1", [1, 128], F32)).ap()
    cbf = stack.enter_context(nc.sbuf_tensor("constbf", [128, 4], BF16)).ap()
    identbf = stack.enter_context(nc.sbuf_tensor("identbf", [128, 128], BF16)).ap()
    misc = stack.enter_context(nc.sbuf_tensor("miscf32", [128, 2560], F32)).ap()

    def go(sy):
        p.inc(sy.dma_start(out=csb, in_=const_in), 'al', 16)
        p.inc(sy.dma_start(out=row1, in_=row1_in), 'al', 16)
    p.block.sync(go)

    def gocast(g):
        p.inc(g.dma_start(out=cbf[:, 0:2], in_=const_in[:, 0:2]), 'cast', 16)
        p.inc(g.dma_start(out=identbf, in_=const_in[:, 324:452]), 'cast', 16)
    p.block.gpsimd(gocast)
    cv = p.v('cast')
    av = p.v('al')

    def wall(e):
        e.wait_ge(p.h('al'), av)
        e.wait_ge(p.h('cast'), cv)
    for eng in ('tensor', 'vector', 'scalar', 'sync', 'gpsimd'):
        getattr(p.block, eng)(wall)

    cst.update(
        ones_sc_bf=cbf[:, 0:1], ones_bf=cbf[:, 1:2],
        ones1_f=row1, role=csb[:, 2:3], rolei=csb[:, 3:4],
        b1=csb[:, 4:36], b2=csb[:, 36:44], gamma=csb[:, 44:52], beta=csb[:, 52:60],
        s1=csb[:, 60:92], s2=csb[:, 92:124], sh1=csb[:, 124:156], sh2=csb[:, 156:188],
        s3=csb[:, 188:196], maskT=csb[:, 196:324], identbf=identbf,
        eps_ln=csb[:, 452:453], eps_att=csb[:, 453:454], one_col=csb[:, 454:455],
        lntmp=misc[:, 0:512], rows=misc[0:1, 512:2560])
    return cst


def host_consts(inputs, core_role):
    cc = np.zeros((128, 456), np.float32)
    cc[:, 0] = 1.0 / 1024.0
    cc[:, 1] = 1.0
    cc[:, 2] = float(core_role)
    cc[:, 3] = 1.0 - float(core_role)
    cc[:, 4:36] = inputs['L0_ffn_b1'].reshape(32, 128).T
    cc[:, 36:44] = inputs['L0_ffn_b2'].reshape(8, 128).T
    cc[:, 44:52] = inputs['L0_attn_gamma'].reshape(8, 128).T
    cc[:, 52:60] = inputs['L0_attn_beta'].reshape(8, 128).T
    cc[:, 60:92] = inputs['L1_ffn_w1_s'].reshape(32, 128).T
    cc[:, 92:124] = inputs['L1_ffn_w2_s'].reshape(32, 128).T
    cc[:, 124:156] = inputs['L1_ffn_h1_s'].reshape(32, 128).T
    cc[:, 156:188] = inputs['L1_ffn_h2_s'].reshape(32, 128).T
    cc[:, 188:196] = inputs['L1_ffn_w3_s'].reshape(8, 128).T
    cc[:, 196:324] = np.triu(np.ones((128, 128), np.float32))
    cc[:, 324:452] = np.eye(128, dtype=np.float32)
    cc[:, 452] = LN_EPS
    cc[:, 453] = 1e-6
    cc[:, 454] = 1.0
    row1 = np.ones((1, 128), np.float32)
    return np.ascontiguousarray(cc), row1


TLH = TL + 2 * HALO

_nc_cache = {}


def _build_l0attn():
    nc = bass.Bass("TRN2", target_bir_lowering=False)
    X0F = nc.dram_tensor("X0F", [DC, 128, TLH], F32, kind="ExternalInput")
    WG = nc.dram_tensor("WG", [1024, 9216], F32, kind="ExternalInput")
    WV = nc.dram_tensor("WV", [1024, 1024], F32, kind="ExternalInput")
    CONST = nc.dram_tensor("CONST", [128, 456], F32, kind="ExternalInput")
    ROW1 = nc.dram_tensor("ROW1", [1, 128], F32, kind="ExternalInput")
    X1 = nc.dram_tensor("X1", [DC, 128, TL], BF16, kind="ExternalOutput")
    xbf = nc.dram_tensor("xbf", [DC, 128, TLH], BF16)
    with ExitStack() as stack:
        p = Prog(nc, stack)
        cst = make_consts(p, stack, CONST.ap(), ROW1.ap())
        wsg = WScratch(nc, "wg", 1024, 9216)
        wsv = WScratch(nc, "wv", 1024, 1024)

        def xcast(g):
            p.inc(g.dma_start(out=xbf.ap(), in_=X0F.ap(), max_dma_last_dim=8192), 'cast', 16)
        p.block.gpsimd(xcast)
        x_ready = ('cast', p.v('cast'))
        wsv.emit_cast(p, WV.ap())
        wsg.emit_cast(p, WG.ap())
        done = emit_l0attn(p, cst, wsg, wsv, xbf, X0F, X1, stack, x_ready=x_ready)

        def fin(g):
            g.wait_ge(p.h(done[0]), done[1])
        p.block.gpsimd(fin)
    return nc


def _build_l0ffn():
    nc = bass.Bass("TRN2", target_bir_lowering=False)
    X1 = nc.dram_tensor("X1", [DC, 128, TL], BF16, kind="ExternalInput")
    W1 = nc.dram_tensor("W1", [1024, 4096], F32, kind="ExternalInput")
    W2 = nc.dram_tensor("W2", [4096, 1024], F32, kind="ExternalInput")
    CONST = nc.dram_tensor("CONST", [128, 456], F32, kind="ExternalInput")
    ROW1 = nc.dram_tensor("ROW1", [1, 128], F32, kind="ExternalInput")
    X2 = nc.dram_tensor("X2", [DC, 128, TL], BF16, kind="ExternalOutput")
    with ExitStack() as stack:
        p = Prog(nc, stack)
        cst = make_consts(p, stack, CONST.ap(), ROW1.ap())
        ws1 = WScratch(nc, "w1", 1024, 4096)
        ws2 = WScratch(nc, "w2", 4096, 1024)
        ws1.emit_cast(p, W1.ap())
        ws2.emit_cast(p, W2.ap())
        done = emit_l0ffn(p, cst, ws1, ws2, X1, X2, stack)

        def fin(g):
            g.wait_ge(p.h(done[0]), done[1])
        p.block.gpsimd(fin)
    return nc


def _build_l1attn():
    nc = bass.Bass("TRN2", target_bir_lowering=False)
    X2 = nc.dram_tensor("X2", [DC, 128, TL], BF16, kind="ExternalInput")
    WQKV = nc.dram_tensor("WQKV", [1024, 3072], F32, kind="ExternalInput")
    WP = nc.dram_tensor("WP", [1024, 1024], F32, kind="ExternalInput")
    CONST = nc.dram_tensor("CONST", [128, 456], F32, kind="ExternalInput")
    ROW1 = nc.dram_tensor("ROW1", [1, 128], F32, kind="ExternalInput")
    X3 = nc.dram_tensor("X3", [DC, 128, TL], BF16, kind="ExternalOutput")
    with ExitStack() as stack:
        p = Prog(nc, stack)
        cst = make_consts(p, stack, CONST.ap(), ROW1.ap())
        wsqkv = WScratch(nc, "qkv", 1024, 3072)
        wsp = WScratch(nc, "wp", 1024, 1024)
        wsqkv.emit_cast(p, WQKV.ap())
        wsp.emit_cast(p, WP.ap())
        done = emit_l1attn(p, cst, wsqkv, wsp, X2, X3, stack)

        def fin(g):
            g.wait_ge(p.h(done[0]), done[1])
        p.block.gpsimd(fin)
    return nc


def _build_l1ffn():
    nc = bass.Bass("TRN2", target_bir_lowering=False)
    X3 = nc.dram_tensor("X3", [DC, 128, TL], BF16, kind="ExternalInput")
    F1 = nc.dram_tensor("F1", [1024, 4096], F32, kind="ExternalInput")
    F2 = nc.dram_tensor("F2", [1024, 4096], F32, kind="ExternalInput")
    H1 = nc.dram_tensor("H1", [4096, 4096], F32, kind="ExternalInput")
    H2 = nc.dram_tensor("H2", [4096, 4096], F32, kind="ExternalInput")
    F3 = nc.dram_tensor("F3", [4096, 1024], F32, kind="ExternalInput")
    CONST = nc.dram_tensor("CONST", [128, 456], F32, kind="ExternalInput")
    ROW1 = nc.dram_tensor("ROW1", [1, 128], F32, kind="ExternalInput")
    Y = nc.dram_tensor("Y", [TL, 1024], F32, kind="ExternalOutput")
    with ExitStack() as stack:
        p = Prog(nc, stack)
        cst = make_consts(p, stack, CONST.ap(), ROW1.ap())
        wf1 = WScratch(nc, "f1", 1024, 4096)
        wf2 = WScratch(nc, "f2", 1024, 4096)
        wh1 = WScratch(nc, "h1", 4096, 4096)
        wh2 = WScratch(nc, "h2", 4096, 4096)
        wf3 = WScratch(nc, "f3", 4096, 1024)
        wf1.emit_cast(p, F1.ap())
        wf2.emit_cast(p, F2.ap())
        wh1.emit_cast(p, H1.ap())
        wh2.emit_cast(p, H2.ap())
        wf3.emit_cast(p, F3.ap())
        done = emit_l1ffn(p, cst, wf1, wf2, wh1, wh2, wf3, X3, Y, stack)

        def fin(g):
            g.wait_ge(p.h(done[0]), done[1])
        p.block.gpsimd(fin)
    return nc


FUSED = True


def _build_fused():
    nc = bass.Bass("TRN2", target_bir_lowering=False)
    X0F = nc.dram_tensor("X0F", [DC, 128, TLH], F32, kind="ExternalInput")
    WG = nc.dram_tensor("WG", [1024, 9216], F32, kind="ExternalInput")
    WV = nc.dram_tensor("WV", [1024, 1024], F32, kind="ExternalInput")
    W1 = nc.dram_tensor("W1", [1024, 4096], F32, kind="ExternalInput")
    W2 = nc.dram_tensor("W2", [4096, 1024], F32, kind="ExternalInput")
    WQKV = nc.dram_tensor("WQKV", [1024, 3072], F32, kind="ExternalInput")
    WP = nc.dram_tensor("WP", [1024, 1024], F32, kind="ExternalInput")
    F1 = nc.dram_tensor("F1", [1024, 4096], F32, kind="ExternalInput")
    F2 = nc.dram_tensor("F2", [1024, 4096], F32, kind="ExternalInput")
    H1 = nc.dram_tensor("H1", [4096, 4096], F32, kind="ExternalInput")
    H2 = nc.dram_tensor("H2", [4096, 4096], F32, kind="ExternalInput")
    F3 = nc.dram_tensor("F3", [4096, 1024], F32, kind="ExternalInput")
    CONST = nc.dram_tensor("CONST", [128, 456], F32, kind="ExternalInput")
    ROW1 = nc.dram_tensor("ROW1", [1, 128], F32, kind="ExternalInput")
    Y = nc.dram_tensor("Y", [TL, 1024], F32, kind="ExternalOutput")
    xbf = nc.dram_tensor("xbf", [DC, 128, TLH], BF16)
    X1 = nc.dram_tensor("X1i", [DC, 128, TL], BF16)
    X2 = nc.dram_tensor("X2i", [DC, 128, TL], BF16)
    X3 = nc.dram_tensor("X3i", [DC, 128, TL], BF16)

    with ExitStack() as gstack:
        p = Prog(nc, gstack)
        cst = make_consts(p, gstack, CONST.ap(), ROW1.ap())

        def xcast(g):
            p.inc(g.dma_start(out=xbf.ap(), in_=X0F.ap(), max_dma_last_dim=8192), 'cast', 16)
        p.block.gpsimd(xcast)
        x_ready = ('cast', p.v('cast'))

        wsv = WScratch(nc, "wv", 1024, 1024)
        wsg = WScratch(nc, "wg", 1024, 9216)
        ws1 = WScratch(nc, "w1", 1024, 4096)
        ws2 = WScratch(nc, "w2", 4096, 1024)
        wsqkv = WScratch(nc, "qkv", 1024, 3072)
        wsp = WScratch(nc, "wp", 1024, 1024)
        wf1 = WScratch(nc, "f1", 1024, 4096)
        wf2 = WScratch(nc, "f2", 1024, 4096)
        wh1 = WScratch(nc, "h1", 4096, 4096)
        wh2 = WScratch(nc, "h2", 4096, 4096)
        wf3 = WScratch(nc, "f3", 4096, 1024)
        for ws_, ap_ in [(wsv, WV), (wsg, WG), (ws1, W1), (ws2, W2),
                         (wsqkv, WQKV), (wsp, WP), (wf1, F1), (wf2, F2),
                         (wh1, H1), (wh2, H2), (wf3, F3)]:
            ws_.emit_cast(p, ap_.ap(), n_groups=(8 if ws_.mK == 72 else (4 if ws_.mK >= 32 else 1)))

        with ExitStack() as s1:
            emit_l0attn(p, cst, wsg, wsv, xbf, X0F, X1, s1, x_ready=x_ready)
            phase_barrier(p)
        with ExitStack() as s2:
            emit_l0ffn(p, cst, ws1, ws2, X1, X2, s2)
            phase_barrier(p)
        with ExitStack() as s3:
            emit_l1attn(p, cst, wsqkv, wsp, X2, X3, s3)
            phase_barrier(p)
        with ExitStack() as s4:
            done = emit_l1ffn(p, cst, wf1, wf2, wh1, wh2, wf3, X3, Y, s4)

        def fin(g):
            g.wait_ge(p.h(done[0]), done[1])
        p.block.gpsimd(fin)
    return nc


def kernel(**inputs):
    x = np.asarray(inputs['L0_attn_x'])

    ncs = _nc_cache
    if FUSED and 'fused' not in ncs:
        ncs['fused'] = _build_fused()
    if not FUSED and 'l0a' not in ncs:
        ncs['l0a'] = _build_l0attn()
        ncs['l0f'] = _build_l0ffn()
        ncs['l1a'] = _build_l1attn()
        ncs['l1f'] = _build_l1ffn()

    WGt = np.ascontiguousarray(np.asarray(inputs['L0_attn_gate_W']).T)
    WVt = np.ascontiguousarray(np.asarray(inputs['L0_attn_value_W']).T)
    W1 = np.ascontiguousarray(np.asarray(inputs['L0_ffn_w1']))
    W2 = np.ascontiguousarray(np.asarray(inputs['L0_ffn_w2']))
    WQKVt = np.ascontiguousarray(np.asarray(inputs['L1_attn_w_attn']).T)
    WPt = np.ascontiguousarray(np.asarray(inputs['L1_attn_w_proj']).T)
    F1t = np.ascontiguousarray(np.asarray(inputs['L1_ffn_w1_w']).T)
    F2t = np.ascontiguousarray(np.asarray(inputs['L1_ffn_w2_w']).T)
    H1t = np.ascontiguousarray(np.asarray(inputs['L1_ffn_h1_w']).T)
    H2t = np.ascontiguousarray(np.asarray(inputs['L1_ffn_h2_w']).T)
    F3t = np.ascontiguousarray(np.asarray(inputs['L1_ffn_w3_w']).T)

    ccs, row1s = [], []
    x0fs = []
    idx_all = np.arange(-HALO, TL + HALO)
    for core in range(8):
        b, h = core // 2, core % 2
        cc, row1 = host_consts(inputs, h)
        ccs.append(cc)
        row1s.append(row1)
        idx = (h * TL + idx_all) % T
        x0h = x[b][idx]
        x0fs.append(np.ascontiguousarray(x0h.T).reshape(DC, 128, TLH))

    cores = list(range(8))
    if FUSED:
        rf = run_bass_kernel_spmd(ncs['fused'], [
            {"X0F": x0fs[c], "WG": WGt, "WV": WVt, "W1": W1, "W2": W2,
             "WQKV": WQKVt, "WP": WPt, "F1": F1t, "F2": F2t, "H1": H1t,
             "H2": H2t, "F3": F3t, "CONST": ccs[c], "ROW1": row1s[c]}
            for c in range(8)], core_ids=cores)
        out = np.empty((B, T, D), np.float32)
        for core in range(8):
            b, h = core // 2, core % 2
            out[b, h * TL:(h + 1) * TL, :] = rf.results[core]["Y"]
        return out

    r1 = run_bass_kernel_spmd(ncs['l0a'], [
        {"X0F": x0fs[c], "WG": WGt, "WV": WVt, "CONST": ccs[c], "ROW1": row1s[c]}
        for c in range(8)], core_ids=cores)
    r2 = run_bass_kernel_spmd(ncs['l0f'], [
        {"X1": r1.results[c]["X1"], "W1": W1, "W2": W2, "CONST": ccs[c], "ROW1": row1s[c]}
        for c in range(8)], core_ids=cores)
    r3 = run_bass_kernel_spmd(ncs['l1a'], [
        {"X2": r2.results[c]["X2"], "WQKV": WQKVt, "WP": WPt, "CONST": ccs[c], "ROW1": row1s[c]}
        for c in range(8)], core_ids=cores)
    r4 = run_bass_kernel_spmd(ncs['l1f'], [
        {"X3": r3.results[c]["X3"], "F1": F1t, "F2": F2t, "H1": H1t, "H2": H2t,
         "F3": F3t, "CONST": ccs[c], "ROW1": row1s[c]}
        for c in range(8)], core_ids=cores)

    out = np.empty((B, T, D), np.float32)
    for core in range(8):
        b, h = core // 2, core % 2
        out[b, h * TL:(h + 1) * TL, :] = r4.results[core]["Y"]
    return out


# version 22
# speedup vs baseline: 1.1745x; 1.0077x over previous
import sys
if "/opt/trn_rl_repo" not in sys.path:
    sys.path.insert(0, "/opt/trn_rl_repo")
import numpy as np
import ml_dtypes
from contextlib import ExitStack
import concourse.bass as bass
import concourse.mybir as mybir
from concourse.bass_utils import run_bass_kernel_spmd

F32 = mybir.dt.float32
BF16 = mybir.dt.bfloat16
AF = mybir.ActivationFunctionType
ALU = mybir.AluOpType

B, T, D = 4, 8192, 1024
DBG = {}
TL = 4096
DC = 8
HC = 32
LN_EPS = 1e-5
HALO = 32
SHIFTS = (0, -32, -16, -4, -1, 1, 4, 16, 32)


def _pedrain(p, tt, val):
    if DBG.get('drain'):
        tt.wait_ge(p.h('pe'), val)


class Prog:
    def __init__(self, nc, stack):
        self.nc = nc
        self.stack = stack
        self.block = stack.enter_context(nc.Block())
        self._sems = {}

    def sem(self, name):
        if name not in self._sems:
            h = self.stack.enter_context(self.nc.semaphore(name))
            self._sems[name] = [h, 0]
        return self._sems[name]

    def h(self, name):
        return self.sem(name)[0]

    def v(self, name):
        return self.sem(name)[1]

    def inc(self, instr, name, n=1):
        s = self.sem(name)
        instr.then_inc(s[0], n)
        s[1] += n
        return s[1]


class WScratch:
    def __init__(self, nc, name, d_in, d_out):
        self.d_in, self.d_out = d_in, d_out
        self.kK, self.mK = d_in // 128, d_out // 128
        self.cols = self.kK * 128
        self.scr = nc.dram_tensor(f"{name}_scr", [self.mK, 128, self.cols], BF16)
        self.cast_sem = f"cast_{name}"
        self.cast_done = 0

    def emit_cast(self, p, w_hbm_ap, n_groups=1):
        src = w_hbm_ap.rearrange("(k p) (m c) -> m p k c", p=128, c=128)
        scr = self.scr.ap().rearrange("m p (k c) -> m p k c", c=128)
        mK = self.mK
        self.grp = max(1, (mK + n_groups - 1) // n_groups)
        self.gsems = []
        self.gdone = []
        for g0 in range(0, mK, self.grp):
            sem = f"{self.cast_sem}_{g0}"

            def go(g, g0=g0, sem=sem):
                for m in range(g0, min(g0 + self.grp, mK)):
                    p.inc(g.dma_start(out=scr[m], in_=src[m]), sem, 16)
            p.block.gpsimd(go)
            self.gsems.append(sem)
            self.gdone.append(p.v(sem))
        self.cast_done = -1

    def cast_wait(self, m):
        gi = min(m // self.grp, len(self.gsems) - 1)
        return (self.gsems[gi], self.gdone[gi])

    def block_ap(self, m):
        return self.scr.ap()[m]


def emit_ln512(p, cst, r_tiles, out_tiles, pb, state, norm_emit=None, ve_r=None):
    if ve_r is None:
        ve_r = p.v('ve')
    rows = cst['rows']
    mu_row = rows[:, 0:512]
    e2_row = rows[:, 512:1024]
    var_row = rows[:, 1024:1536]
    sd_row = rows[:, 512:1024]
    rs_row = rows[:, 1536:2048]
    r2 = state['r2']

    g_rows_v = state.get('rows_read_ve', 0)

    def mm_sum2(t):
        if ve_r > 0:
            t.wait_ge(p.h('ve'), ve_r)
        if g_rows_v > 0:
            t.wait_ge(p.h('ve'), g_rows_v)
        for mc in range(DC):
            mm = t.matmul(pb['sum'][0:1, :], cst['ones_sc_bf'], r_tiles[mc],
                          start=(mc == 0), stop=(mc == DC - 1))
        p.inc(mm, 'pe')
    p.block.tensor(mm_sum2)

    g_sq_ac = state.get('sq_read_pe', 0)

    def ve_sq(v):
        if g_sq_ac > 0:
            v.wait_ge(p.h('pe'), g_sq_ac)
        for mc in range(DC):
            ins = v.tensor_tensor(r2[:, mc * 512:(mc + 1) * 512], r_tiles[mc],
                                  r_tiles[mc], ALU.mult)
        p.inc(ins, 've')
    p.block.vector(ve_sq)
    ac_sq = p.v('ve')

    def mm_sq(t):
        t.wait_ge(p.h('ve'), ac_sq)
        if g_rows_v > 0:
            t.wait_ge(p.h('ve'), g_rows_v)
        for mc in range(DC):
            mm = t.matmul(pb['sq'][0:1, :], cst['ones_sc_bf'], r2[:, mc * 512:(mc + 1) * 512],
                          start=(mc == 0), stop=(mc == DC - 1))
        p.inc(mm, 'pe')
    p.block.tensor(mm_sq)
    pe_sq = p.v('pe')
    state['sq_read_pe'] = pe_sq

    def ve_rows(v):
        v.wait_ge(p.h('pe'), pe_sq)
        v.tensor_copy(mu_row, pb['sum'][0:1, :])
        c0 = p.inc(v.tensor_copy(e2_row, pb['sq'][0:1, :]), 've')
        v.wait_ge(p.h('ve'), c0)
        c1 = p.inc(v.scalar_tensor_tensor(var_row, mu_row, -1.0, mu_row,
                                          ALU.mult, ALU.mult), 've')
        v.wait_ge(p.h('ve'), c1)
        ins = v.tensor_tensor(var_row, e2_row, var_row, ALU.add)
        p.inc(ins, 've')
    p.block.vector(ve_rows)
    ve_var = p.v('ve')
    state['rows_read_ve'] = ve_var

    def act_sd(s):
        s.wait_ge(p.h('ve'), ve_var)
        ins = s.activation(sd_row, var_row, AF.Sqrt, bias=cst['eps_ln'][0:1, :])
        p.inc(ins, 'ac')
    p.block.scalar(act_sd)
    ac_sd = p.v('ac')

    def ve_rs(v):
        v.wait_ge(p.h('ac'), ac_sd)
        ins = v.reciprocal(rs_row, sd_row)
        p.inc(ins, 've')
    p.block.vector(ve_rs)
    ve_rsv = p.v('ve')

    g_norm = state.get('norm_done_ve', 0)

    def mm_bc(t):
        t.wait_ge(p.h('ve'), ve_rsv)
        if g_norm > 0:
            t.wait_ge(p.h('ve'), g_norm)
        t.matmul(pb['mu'], cst['ones1_f'], mu_row, start=True, stop=True)
        mm = t.matmul(pb['rs'], cst['ones1_f'], rs_row, start=True, stop=True)
        p.inc(mm, 'pe')
    p.block.tensor(mm_bc)
    pe_bc = p.v('pe')

    tmp = cst['lntmp']

    def ve_norm(v):
        v.wait_ge(p.h('pe'), pe_bc)
        if norm_emit is not None:
            ins = norm_emit(v, pb['mu'], pb['rs'], tmp)
        else:
            for mc in range(DC):
                v.tensor_tensor(tmp, r_tiles[mc], pb['mu'], ALU.subtract)
                ins = v.tensor_tensor(out_tiles[mc], tmp, pb['rs'], ALU.mult)
        p.inc(ins, 've')
    p.block.vector(ve_norm)
    state['norm_done_ve'] = p.v('ve')
    return p.v('ve')


def emit_l0attn(p, cst, wsg, wsv, x0bf, x0f, x_out, stack, x_ready=None):
    nc = p.nc
    NT = 1024
    WH = NT + 64
    ntiles = TL // NT
    xw_sb = [stack.enter_context(nc.sbuf_tensor(f"at_xw{i}", [128, DC * WH], BF16)).ap()
             for i in range(2)]
    vsb = stack.enter_context(nc.sbuf_tensor("at_v", [128, DC * WH], BF16)).ap()
    accb = stack.enter_context(nc.sbuf_tensor("at_accb", [128, 3 * DC * NT], BF16)).ap()
    gsb = stack.enter_context(nc.sbuf_tensor("at_g", [128, 9 * NT], BF16)).ap()
    gtmp = stack.enter_context(nc.sbuf_tensor("at_gt", [128, NT], BF16)).ap()
    Rb = stack.enter_context(nc.sbuf_tensor("at_R", [128, DC * NT], BF16)).ap()
    r2 = stack.enter_context(nc.sbuf_tensor("at_r2", [128, 4096], BF16)).ap()
    xout_st = stack.enter_context(nc.sbuf_tensor("at_xo", [128, DC * NT], BF16)).ap()
    Wb = stack.enter_context(nc.sbuf_tensor("at_W", [128, 4 * 1024], BF16)).ap()
    t2f = stack.enter_context(nc.sbuf_tensor("at_t2", [128, 512], F32)).ap()
    pg = [stack.enter_context(nc.psum_tensor(f"at_pg{i}", [128, 1024], F32)).ap() for i in range(2)]
    pln = stack.enter_context(nc.psum_tensor("at_pln", [1, 1024], F32)).ap()
    pbc = stack.enter_context(nc.psum_tensor("at_pbc", [128, 1024], F32)).ap()
    pb = {'sum': pln[0:1, 0:512], 'sq': pln[0:1, 512:1024],
          'mu': pbc[:, 0:512], 'rs': pbc[:, 512:1024]}
    ln_state = {'r2': r2}

    x0bf_r = x0bf.ap().rearrange("c p t -> p c t")
    x0f_r = x0f.ap().rearrange("c p t -> p c t")
    x_out_r = x_out.ap().rearrange("c p t -> p c t")

    wslot = [Wb[:, i * 1024:(i + 1) * 1024] for i in range(4)]
    pend_q = []
    wguard = [0, 0, 0, 0]
    xw_guard = [0, 0]
    xf_guard = 0
    gsb_guard = [0]
    pg_guard_ac = [0, 0]
    pg_guard_ve = [0, 0]
    accb_guard = (0, 0)
    rb_guard = (0, 0)
    xout_guard = 0
    wsl_i = 0

    def wload(ws, m):
        nonlocal wsl_i
        s = wsl_i % 4
        wsl_i += 1
        g = wguard[s]
        cd = ws.cast_done
        sem = f"atw{s}"

        def go(sy, ws=ws, m=m, s=s, g=g, cd=ws.cast_wait(m), sem=sem):
            if g > 0:
                sy.wait_ge(p.h('pe'), g)
            sy.wait_ge(p.h(cd[0]), cd[1])
            p.inc(sy.dma_start(out=wslot[s], in_=ws.block_ap(m)), sem, 16)
        p.block.sync(go)
        return s, (sem, p.v(sem))

    for t in range(ntiles):
        if len(pend_q) == 2:
            pend_q.pop(0)()
        xw = xw_sb[t % 2]
        g_xw = xw_guard[t % 2]
        g_xf = xf_guard

        xw_sem = f"atxw{t % 2}"

        def loadx(sy, t=t, g_xw=g_xw, g_xf=g_xf, xw=xw, xw_sem=xw_sem):
            if x_ready is not None and x_ready[1] > 0:
                sy.wait_ge(p.h(x_ready[0]), x_ready[1])
            if g_xw > 0:
                sy.wait_ge(p.h('pe'), g_xw)
            if g_xf > 0:
                sy.wait_ge(p.h('ve'), g_xf)
            p.inc(sy.dma_start(out=xw.rearrange("p (c t) -> p c t", c=DC),
                               in_=x0bf_r[:, :, t * NT:t * NT + WH]), xw_sem, 16)
        p.block.sync(loadx)
        al_x = (xw_sem, p.v(xw_sem))
        al_f = al_x

        for m in range(DC):
            s, wlv = wload(wsv, m)
            gv0, gv1 = pg_guard_ac[0], pg_guard_ac[1]
            gve0, gve1 = pg_guard_ve[0], pg_guard_ve[1]

            def vmm(tt, s=s, wlv=wlv, al_x=al_x, gv0=gv0, gv1=gv1, gve0=gve0, gve1=gve1, xw=xw):
                tt.wait_ge(p.h(wlv[0]), wlv[1])
                tt.wait_ge(p.h(al_x[0]), al_x[1])
                for g in (gv0, gv1):
                    if g > 0:
                        tt.wait_ge(p.h('ac'), g)
                for g in (gve0, gve1):
                    if g > 0:
                        tt.wait_ge(p.h('ve'), g)
                for k in range(DC):
                    st = (k == 0)
                    sp = (k == DC - 1)
                    tt.matmul(pg[0][:, 0:512], wslot[s][:, k * 128:(k + 1) * 128],
                              xw[:, k * WH:k * WH + 512], start=st, stop=sp)
                    tt.matmul(pg[0][:, 512:1024], wslot[s][:, k * 128:(k + 1) * 128],
                              xw[:, k * WH + 512:k * WH + 1024], start=st, stop=sp)
                    ins = tt.matmul(pg[1][:, 0:64], wslot[s][:, k * 128:(k + 1) * 128],
                                    xw[:, k * WH + 1024:k * WH + 1088], start=st, stop=sp)
                p.inc(ins, 'pe')
            p.block.tensor(vmm)
            wguard[s] = p.v('pe')
            pe_v = p.v('pe')

            def vevac(v, m=m, pe_v=pe_v):
                v.wait_ge(p.h('pe'), pe_v)
                v.tensor_copy(vsb[:, m * WH:m * WH + 1024], pg[0][:, :])
                ins = v.tensor_copy(vsb[:, m * WH + 1024:m * WH + 1088], pg[1][:, 0:64])
                p.inc(ins, 've')
            p.block.vector(vevac)
            pg_guard_ve[0] = pg_guard_ve[1] = p.v('ve')
            pg_guard_ac[0] = pg_guard_ac[1] = 0

        acb_pe, acb_ac = accb_guard
        for m in range(DC):
            ac_base = p.v('ac')
            ve_base = p.v('ve')
            pe_g_j = []
            for j in range(9):
                gm = j * DC + m
                s, wlv = wload(wsg, gm)
                jp = j % 2
                g_ve = pg_guard_ve[jp]
                g_ac = pg_guard_ac[jp] if j < 2 else ac_base + (j - 2) + 1

                def gmm(tt, s=s, wlv=wlv, jp=jp, g_ac=g_ac, g_ve=g_ve, al_x=al_x, xw=xw):
                    tt.wait_ge(p.h(wlv[0]), wlv[1])
                    tt.wait_ge(p.h(al_x[0]), al_x[1])
                    if g_ac > 0:
                        tt.wait_ge(p.h('ac'), g_ac)
                    if g_ve > 0:
                        tt.wait_ge(p.h('ve'), g_ve)
                    for k in range(DC):
                        st = (k == 0)
                        sp = (k == DC - 1)
                        tt.matmul(pg[jp][:, 0:512], wslot[s][:, k * 128:(k + 1) * 128],
                                  xw[:, k * WH + HALO:k * WH + HALO + 512], start=st, stop=sp)
                        ins = tt.matmul(pg[jp][:, 512:1024], wslot[s][:, k * 128:(k + 1) * 128],
                                        xw[:, k * WH + HALO + 512:k * WH + HALO + 1024], start=st, stop=sp)
                    p.inc(ins, 'pe')
                p.block.tensor(gmm)
                wguard[s] = p.v('pe')
                pe_g_j.append(p.v('pe'))
                pg_guard_ve[jp] = 0

            mp9 = 0
            gsb_prev = gsb_guard[0]

            def sigs(sc, pe_g_j=tuple(pe_g_j), gsb_prev=gsb_prev, mp9=mp9):
                for j in range(9):
                    if gsb_prev > 0:
                        sc.wait_ge(p.h('ve'), gsb_prev + j + 1)
                    sc.wait_ge(p.h('pe'), pe_g_j[j])
                    ins = sc.activation(gsb[:, (mp9 + j) * NT:(mp9 + j + 1) * NT],
                                        pg[j % 2][:, :], AF.Sigmoid)
                    p.inc(ins, 'ac')
            p.block.scalar(sigs)
            pg_guard_ac[0] = p.v('ac')
            pg_guard_ac[1] = p.v('ac') - 1

            def gates(v, m=m, t=t, mp9=mp9, ac_base=ac_base, acb_pe=acb_pe, acb_ac=acb_ac):
                asl = accb[:, (t % 3) * DC * NT + m * NT:(t % 3) * DC * NT + (m + 1) * NT]
                for j in range(9):
                    v.wait_ge(p.h('ac'), ac_base + j + 1)
                    gg = gsb[:, (mp9 + j) * NT:(mp9 + j + 1) * NT]
                    vv = vsb[:, m * WH + HALO + SHIFTS[j]:m * WH + HALO + SHIFTS[j] + NT]
                    if j == 0:
                        if acb_pe > 0:
                            v.wait_ge(p.h('pe'), acb_pe)
                        if acb_ac > 0:
                            v.wait_ge(p.h('ac'), acb_ac)
                        ins = v.tensor_tensor(asl, gg, vv, ALU.mult)
                    else:
                        v.tensor_tensor(gtmp[:, 0:NT], gg, vv, ALU.mult)
                        ins = v.tensor_tensor(asl, asl, gtmp[:, 0:NT], ALU.add)
                    p.inc(ins, 've')
            p.block.vector(gates)
            gsb_guard[0] = ve_base
        xw_guard[t % 2] = p.v('pe')

        ve_gate_done = p.v('ve')
        al_this = al_f

        def make_ln(t=t, ve_gate_done=ve_gate_done, al_f=al_this, xw=xw, asl=(t % 3)):
            def flush():
                nonlocal xout_guard, accb_guard, rb_guard, xf_guard
                rb_pe, rb_ac = rb_guard
                for n in range(2):
                    acc_sl = [accb[:, asl * DC * NT + mc * NT + n * 512:
                                   asl * DC * NT + mc * NT + n * 512 + 512] for mc in range(DC)]
                    r_sl = [Rb[:, mc * NT + n * 512:mc * NT + n * 512 + 512] for mc in range(DC)]
                    o_sl = [xout_st[:, mc * NT + n * 512:mc * NT + n * 512 + 512] for mc in range(DC)]

                    def inner_norm(v, mu_ap, rs_ap, tmp, n=n, acc_sl=acc_sl, r_sl=r_sl,
                                   al_f=al_f, rb_pe=rb_pe, rb_ac=rb_ac, xw=xw):
                        v.wait_ge(p.h(al_f[0]), al_f[1])
                        if rb_pe > 0:
                            v.wait_ge(p.h('pe'), rb_pe)
                        if rb_ac > 0:
                            v.wait_ge(p.h('ac'), rb_ac)
                        for mc in range(DC):
                            v.tensor_tensor(tmp, acc_sl[mc], mu_ap, ALU.subtract)
                            v.scalar_tensor_tensor(t2f, tmp, cst['gamma'][:, mc:mc + 1], rs_ap,
                                                   ALU.mult, ALU.mult)
                            ins = v.scalar_tensor_tensor(
                                r_sl[mc], t2f, cst['beta'][:, mc:mc + 1],
                                xw[:, mc * WH + HALO + n * 512:mc * WH + HALO + n * 512 + 512],
                                ALU.add, ALU.add)
                        return ins

                    emit_ln512(p, cst, acc_sl, r_sl, pb, ln_state, norm_emit=inner_norm,
                               ve_r=(ve_gate_done if n == 0 else None))
                    if xout_guard > 0 and n == 0:
                        def vwait(v, g=xout_guard):
                            v.wait_ge(p.h('spx1'), g)
                        p.block.vector(vwait)
                    emit_ln512(p, cst, r_sl, o_sl, pb, ln_state)
                accb_guard = (p.v('pe'), p.v('ac'))
                rb_guard = accb_guard
                xf_guard = p.v('ve')
                ve_done = p.v('ve')

                def store(g, t=t, ve_done=ve_done):
                    g.wait_ge(p.h('ve'), ve_done)
                    p.inc(g.dma_start(out=x_out_r[:, :, t * NT:(t + 1) * NT],
                                      in_=xout_st.rearrange("p (c t) -> p c t", c=DC)), 'spx1', 16)
                p.block.gpsimd(store)
                xout_guard = p.v('spx1')
            return flush

        pend_q.append(make_ln())
        if len(pend_q) > 2:
            raise RuntimeError("pend depth")

    while pend_q:
        pend_q.pop(0)()
    return ('spx1', p.v('spx1'))


def emit_l0ffn(p, cst, ws1, ws2, x_in, x_out, stack, x_ready=None):
    nc = p.nc
    NT = 1024
    ntiles = TL // NT
    A = stack.enter_context(nc.sbuf_tensor("ff_A", [128, 3 * DC * NT], BF16)).ap()
    Bh = stack.enter_context(nc.sbuf_tensor("ff_B", [128, HC * NT], BF16)).ap()
    W = stack.enter_context(nc.sbuf_tensor("ff_W", [128, 8192], BF16)).ap()
    R = stack.enter_context(nc.sbuf_tensor("ff_R", [128, 8192], BF16)).ap()
    r2 = stack.enter_context(nc.sbuf_tensor("ff_r2", [128, 4096], BF16)).ap()
    pbx = [stack.enter_context(nc.psum_tensor(f"ff_p{i}", [128, 512], F32)).ap() for i in range(8)]
    pbank = pbx[:4]
    pb = {'sum': pbx[4], 'sq': pbx[5], 'mu': pbx[6], 'rs': pbx[7]}

    xin_ring = [A[:, 0:DC * NT], A[:, DC * NT:2 * DC * NT]]
    xout_st = A[:, 2 * DC * NT:3 * DC * NT]
    wslot = [W[:, 0:4096], W[:, 4096:8192]]
    x_in_r = x_in.ap().rearrange("c p t -> p c t")
    x_out_r = x_out.ap().rearrange("c p t -> p c t")

    xin_guard = [0, 0]
    wslot_guard = [0, 0]
    bank_guard = [('ac', 0)] * 4
    xout_guard = 0
    ln_state = {'r2': r2}

    def load_xtile(t):
        s = t % 2
        g = xin_guard[s]
        sem = f"ffx{s}"

        def go(sy, s=s, g=g, sem=sem, t=t):
            if x_ready is not None and x_ready[1] > 0:
                sy.wait_ge(p.h(x_ready[0]), x_ready[1])
            if g > 0:
                sy.wait_ge(p.h('ve'), g)
            p.inc(sy.dma_start(out=xin_ring[s].rearrange("p (c t) -> p c t", c=DC),
                               in_=x_in_r[:, :, t * NT:(t + 1) * NT]), sem, 16)
        p.block.sync(go)
        return (sem, p.v(sem))

    al_tile = [None] * (ntiles + 1)
    al_tile[0] = load_xtile(0)

    for t in range(ntiles):
        xin = xin_ring[t % 2]
        if t + 1 < ntiles:
            al_tile[t + 1] = load_xtile(t + 1)
        al_need = al_tile[t]

        for mg in range(0, HC, 4):
            sidx = (mg // 4) % 2
            wg = wslot_guard[sidx]
            cast_need = ws1.cast_done

            wsem = f"ffw{sidx}"

            def wload(sy, mg=mg, sidx=sidx, wg=wg, cast_need=ws1.cast_wait(mg + 3), ws=ws1, wsem=wsem):
                if wg > 0:
                    sy.wait_ge(p.h('pe'), wg)
                sy.wait_ge(p.h(cast_need[0]), cast_need[1])
                src = ws.scr.ap()[mg:mg + 4].rearrange("m p c -> p m c")
                dst = wslot[sidx].rearrange("p (m c) -> p m c", m=4)
                p.inc(sy.dma_start(out=dst, in_=src), wsem, 16)
            p.block.sync(wload)
            wl_here = (wsem, p.v(wsem))

            for mi in range(4):
                m = mg + mi
                wblk = wslot[sidx][:, mi * 1024:(mi + 1) * 1024]
                b0 = (m % 2) * 2

                def mm(tt, wblk=wblk, b0=b0, g0=bank_guard[b0], g1=bank_guard[b0 + 1],
                       wl_here=wl_here, al_need=al_need, xin=xin):
                    tt.wait_ge(p.h(wl_here[0]), wl_here[1])
                    tt.wait_ge(p.h(al_need[0]), al_need[1])
                    for gn, gv in (g0, g1):
                        if gv > 0:
                            tt.wait_ge(p.h(gn), gv)
                    for k in range(DC):
                        for n in range(2):
                            ins = tt.matmul(pbank[b0 + n][:, :],
                                            wblk[:, k * 128:(k + 1) * 128],
                                            xin[:, k * NT + n * 512:k * NT + (n + 1) * 512],
                                            start=(k == 0), stop=(k == DC - 1))
                    p.inc(ins, 'pe')
                p.block.tensor(mm)
                pe_here = p.v('pe')

                def evac(s, m=m, b0=b0, pe_here=pe_here, t=t):
                    s.wait_ge(p.h('pe'), pe_here)
                    for n in range(2):
                        ins = s.activation(Bh[:, m * NT + n * 512:m * NT + (n + 1) * 512],
                                           pbank[b0 + n][:, :], AF.Gelu,
                                           bias=cst['b1'][:, m:m + 1])
                    p.inc(ins, 'ac')
                p.block.scalar(evac)
                bank_guard[b0] = bank_guard[b0 + 1] = ('ac', p.v('ac'))
            wslot_guard[sidx] = p.v('pe')
        ac_h_done = p.v('ac')

        r_half = [[R[:, (n * DC + mc) * 512:(n * DC + mc) * 512 + 512] for mc in range(DC)]
                  for n in range(2)]
        ve_r_guard = ln_state.get('norm_done_ve', 0)
        for m2 in range(DC):
            sidx = m2 % 2
            wg = wslot_guard[sidx]

            wsem = f"ffw{sidx}"

            def wload2(sy, m2=m2, sidx=sidx, wg=wg, cast_need=ws2.cast_wait(m2), wsem=wsem):
                if wg > 0:
                    sy.wait_ge(p.h('pe'), wg)
                sy.wait_ge(p.h(cast_need[0]), cast_need[1])
                p.inc(sy.dma_start(out=wslot[sidx], in_=ws2.scr.ap()[m2]), wsem, 16)
            p.block.sync(wload2)
            wl_here = (wsem, p.v(wsem))
            b0 = (m2 % 2) * 2

            def mm2(tt, m2=m2, sidx=sidx, b0=b0, g0=bank_guard[b0], g1=bank_guard[b0 + 1],
                    wl_here=wl_here, ac_h_done=ac_h_done):
                tt.wait_ge(p.h(wl_here[0]), wl_here[1])
                tt.wait_ge(p.h('ac'), ac_h_done)
                for gn, gv in (g0, g1):
                    if gv > 0:
                        tt.wait_ge(p.h(gn), gv)
                for k2 in range(HC):
                    for n in range(2):
                        ins = tt.matmul(pbank[b0 + n][:, :],
                                        wslot[sidx][:, k2 * 128:(k2 + 1) * 128],
                                        Bh[:, k2 * NT + n * 512:k2 * NT + (n + 1) * 512],
                                        start=(k2 == 0), stop=(k2 == HC - 1))
                p.inc(ins, 'pe')
            p.block.tensor(mm2)
            pe_here = p.v('pe')

            def resid(v, m2=m2, b0=b0, pe_here=pe_here, xin=xin, ve_r_guard=ve_r_guard,
                      al_need=al_need):
                v.wait_ge(p.h('pe'), pe_here)
                v.wait_ge(p.h(al_need[0]), al_need[1])
                if ve_r_guard > 0:
                    v.wait_ge(p.h('ve'), ve_r_guard)
                for n in range(2):
                    ins = v.scalar_tensor_tensor(
                        r_half[n][m2], pbank[b0 + n][:, :], cst['b2'][:, m2:m2 + 1],
                        xin[:, m2 * NT + n * 512:m2 * NT + (n + 1) * 512],
                        ALU.add, ALU.add)
                p.inc(ins, 've')
            p.block.vector(resid)
            bank_guard[b0] = bank_guard[b0 + 1] = ('ve', p.v('ve'))
            wslot_guard[sidx] = pe_here
        xin_guard[t % 2] = p.v('ve')

        sp_guard = xout_guard
        for n in range(2):
            out_tiles = [xout_st[:, mc * NT + n * 512:mc * NT + (n + 1) * 512] for mc in range(DC)]
            if sp_guard > 0 and n == 0:
                def vwait(v, sp_guard=sp_guard):
                    v.wait_ge(p.h('spx2'), sp_guard)
                p.block.vector(vwait)
            emit_ln512(p, cst, r_half[n], out_tiles, pb, ln_state)
        ve_done = p.v('ve')

        DBG['xin'] = xin
        DBG['Bh'] = Bh
        DBG['R'] = R

        def store(g, t=t, ve_done=ve_done):
            g.wait_ge(p.h('ve'), ve_done)
            p.inc(g.dma_start(out=x_out_r[:, :, t * NT:(t + 1) * NT],
                              in_=xout_st.rearrange("p (c t) -> p c t", c=DC)), 'spx2', 16)
        p.block.gpsimd(store)
        xout_guard = p.v('spx2')

    return ('spx2', p.v('spx2'))


def emit_l1attn(p, cst, wsqkv, wsproj, x_in, x_out, stack, tag="", groups=None):
    nc = p.nc
    NH, DH, CH = 16, 64, 128
    NC_ = TL // CH
    PR = 8
    kT_hbm = nc.dram_tensor(f"l1a{tag}_kT", [DC, 128, TL], BF16)
    v_hbm = nc.dram_tensor(f"l1a{tag}_v", [NC_, 128, 1024], BF16)
    cc_in = nc.dram_tensor(f"l1a{tag}_ccin", [128, 520], F32)
    cc_out = nc.dram_tensor(f"l1a{tag}_ccout", [128, 520], F32)

    qT = stack.enter_context(nc.sbuf_tensor("aa_qT", [128, PR * TL], BF16)).ap()
    ckpt = stack.enter_context(nc.sbuf_tensor("aa_ck", [128, NC_ * 520], BF16)).ap()
    qkw = stack.enter_context(nc.sbuf_tensor("aa_qkw", [128, 2 * 1024], BF16)).ap()
    vw = stack.enter_context(nc.sbuf_tensor("aa_vw", [128, 8 * 1024], BF16)).ap()
    x2g = vw
    kcr = stack.enter_context(nc.sbuf_tensor("aa_kcr", [128, 2 * 1024], BF16)).ap()
    vcr = stack.enter_context(nc.sbuf_tensor("aa_vcr", [128, 2 * 1024], BF16)).ap()
    ktm = stack.enter_context(nc.sbuf_tensor("aa_ktm", [128, 2 * 1024], BF16)).ap()
    ypair = stack.enter_context(nc.sbuf_tensor("aa_yp", [128, 2 * 1024], BF16)).ap()
    yTg = stack.enter_context(nc.sbuf_tensor("aa_yTg", [128, PR * 512], BF16)).ap()
    x2r = stack.enter_context(nc.sbuf_tensor("aa_x2r", [128, DC * 512], BF16)).ap()
    r2 = stack.enter_context(nc.sbuf_tensor("aa_r2", [128, 4096], BF16)).ap()
    ssb = stack.enter_context(nc.sbuf_tensor("aa_ssb", [128, 2 * 128], BF16)).ap()
    S32 = stack.enter_context(nc.sbuf_tensor("aa_S", [128, 520], F32)).ap()
    Sini = stack.enter_context(nc.sbuf_tensor("aa_Si", [128, 520], BF16)).ap()
    ccs = stack.enter_context(nc.sbuf_tensor("aa_ccs", [128, 520], F32)).ap()
    recv = stack.enter_context(nc.sbuf_tensor("aa_rcv", [128, 520], F32)).ap()
    drow = stack.enter_context(nc.sbuf_tensor("aa_dr", [128, 4], F32)).ap()
    Wp = stack.enter_context(nc.sbuf_tensor("aa_Wp", [128, 2 * 1024], BF16)).ap()

    partA_stack = ExitStack()
    x2c = partA_stack.enter_context(nc.sbuf_tensor("aa_x2c", [128, 2 * 1024], BF16)).ap()
    vstage = partA_stack.enter_context(nc.sbuf_tensor("aa_vst", [128, 2 * 1024], BF16)).ap()
    kstage = partA_stack.enter_context(nc.sbuf_tensor("aa_kst", [128, 4 * 512], BF16)).ap()
    msb = partA_stack.enter_context(nc.sbuf_tensor("aa_msb", [128, 4 * 512], BF16)).ap()
    esb = partA_stack.enter_context(nc.sbuf_tensor("aa_esb", [128, 4 * 512], BF16)).ap()
    pa = [stack.enter_context(nc.psum_tensor(f"aa_pa{i}", [128, 512], F32)).ap() for i in range(4)]
    pq = [stack.enter_context(nc.psum_tensor(f"aa_pq{i}", [128, 1024], BF16)).ap() for i in range(2)]
    pv_stack = ExitStack()
    pv = pv_stack.enter_context(nc.psum_tensor("aa_pv", [128, 1024], F32)).ap()
    ln_state = {'r2': r2}

    x_in_r = x_in.ap().rearrange("c p t -> p c t")
    x_out_r = x_out.ap().rearrange("c p t -> p c t")
    kT_r = kT_hbm.ap().rearrange("c p t -> p c t")

    pa_guard = [('pe', 0), ('pe', 0), ('pe', 0), ('pe', 0)]
    pq_guard = [0, 0]

    def load_vw(sy):
        cd = wsqkv.cast_wait(23)
        sy.wait_ge(p.h(cd[0]), cd[1])
        for m in range(8):
            p.inc(sy.dma_start(out=vw[:, m * 1024:(m + 1) * 1024],
                               in_=wsqkv.block_ap(16 + m)), 'avw', 16)
    p.block.sync(load_vw)
    wl_vw = p.v('avw')

    x2c_guard = [0, 0]
    vst_guard = [0, 0]
    pv_guard = 0
    for tc in range(NC_):
        s = tc % 2
        g = x2c_guard[s]

        xsem = f"axc{s}"

        def lx(sy, tc=tc, s=s, g=g, xsem=xsem):
            if g > 0:
                sy.wait_ge(p.h('pe'), g)
            p.inc(sy.dma_start(out=x2c[:, s * 1024:(s + 1) * 1024].rearrange("p (k t) -> p k t", k=DC),
                               in_=x_in_r[:, :, tc * 128:(tc + 1) * 128]), xsem, 16)
        p.block.sync(lx)
        al_h = (xsem, p.v(xsem))
        gv = pv_guard

        def mmv(tt, s=s, al_h=al_h, gv=gv):
            tt.wait_ge(p.h('avw'), wl_vw)
            tt.wait_ge(p.h(al_h[0]), al_h[1])
            if gv > 0:
                tt.wait_ge(p.h('ve'), gv)
            for k in range(DC):
                for n in range(2):
                    ins = tt.matmul(
                        pv[:, n * 512:(n + 1) * 512],
                        x2c[:, s * 1024 + k * 128:s * 1024 + (k + 1) * 128],
                        vw.rearrange("p (m t) -> p m t", m=8)[:, n * 4:(n + 1) * 4,
                                                             k * 128:(k + 1) * 128],
                        start=(k == 0), stop=(k == DC - 1))
            p.inc(ins, 'pe')
        p.block.tensor(mmv)
        x2c_guard[s] = p.v('pe')
        pe_v = p.v('pe')
        gs = vst_guard[s]

        ssem = f"asv{s}"

        def ev(v, s=s, pe_v=pe_v, gs=gs, ssem=ssem):
            v.wait_ge(p.h('pe'), pe_v)
            if gs > 0:
                v.wait_ge(p.h(ssem), gs)
            ins = v.tensor_copy(vstage[:, s * 1024:(s + 1) * 1024], pv[:, :])
            p.inc(ins, 've')
        p.block.vector(ev)
        pv_guard = p.v('ve')
        ve_v = p.v('ve')

        def spl(g, tc=tc, s=s, ve_v=ve_v, ssem=ssem):
            g.wait_ge(p.h('ve'), ve_v)
            p.inc(g.dma_start(out=v_hbm.ap()[tc], in_=vstage[:, s * 1024:(s + 1) * 1024]),
                  ssem, 16)
        p.block.gpsimd(spl)
        vst_guard[s] = p.v(ssem)

    pv_stack.close()
    pb = {'sum': pa[0][0:1, 0:512], 'sq': pa[1][0:1, 0:512],
          'mu': pa[2][:, :], 'rs': pa[3][:, :]}

    pe_vdone = p.v('pe')
    x2g_guard = [pe_vdone, pe_vdone]
    kst_guard = [0, 0, 0, 0]
    msb_guard = 0
    esb_guard = 0
    qkw_guard = [0, 0]
    for g in range(8):
        s = g % 2
        gg = x2g_guard[s]

        gsem = f"axg{s}"

        def lg(sy, g=g, s=s, gg=gg, gsem=gsem):
            if gg > 0:
                sy.wait_ge(p.h('pe'), gg)
            p.inc(sy.dma_start(out=x2g[:, s * 4096:(s + 1) * 4096].rearrange("p (k t) -> p k t", k=DC),
                               in_=x_in_r[:, :, g * 512:(g + 1) * 512]), gsem, 16)
        p.block.sync(lg)
        al_g = (gsem, p.v(gsem))

        for mg in range(0, 16, 4):
            pe_m = [0] * 4
            for mi in range(4):
                m = mg + mi
                ws_ = m % 2
                gq = qkw_guard[ws_]
                qsem = f"aqk{ws_}"

                def lw(sy, m=m, ws_=ws_, gq=gq, qsem=qsem, cd=wsqkv.cast_wait(m)):
                    if gq > 0:
                        sy.wait_ge(p.h('pe'), gq)
                    sy.wait_ge(p.h(cd[0]), cd[1])
                    p.inc(sy.dma_start(out=qkw[:, ws_ * 1024:(ws_ + 1) * 1024],
                                       in_=wsqkv.block_ap(m)), qsem, 16)
                p.block.sync(lw)
                wl_h = (qsem, p.v(qsem))
                bk = m % 4
                gn, gvv = pa_guard[bk]

                def mmq(tt, s=s, m=m, ws_=ws_, bk=bk, gn=gn, gvv=gvv, wl_h=wl_h, al_g=al_g):
                    tt.wait_ge(p.h(wl_h[0]), wl_h[1])
                    tt.wait_ge(p.h(al_g[0]), al_g[1])
                    if gvv > 0:
                        tt.wait_ge(p.h(gn), gvv)
                    for k in range(DC):
                        ins = tt.matmul(pa[bk][:, :],
                                        qkw[:, ws_ * 1024 + k * 128:ws_ * 1024 + (k + 1) * 128],
                                        x2g[:, s * 4096 + k * 512:s * 4096 + (k + 1) * 512],
                                        start=(k == 0), stop=(k == DC - 1))
                    p.inc(ins, 'pe')
                p.block.tensor(mmq)
                qkw_guard[ws_] = p.v('pe')
                pe_m[mi] = p.v('pe')

            ac_base = p.v('ac')
            ve_base = p.v('ve')
            g_msb = msb_guard

            def vmins(v, pe_m=tuple(pe_m), g_msb=g_msb):
                for mi in range(4):
                    if g_msb > 0:
                        v.wait_ge(p.h('ac'), g_msb)
                    v.wait_ge(p.h('pe'), pe_m[mi])
                    ins = v.tensor_scalar(msb[:, mi * 512:(mi + 1) * 512],
                                          pa[mi % 4][:, :], 0.0, None, ALU.min)
                    p.inc(ins, 've')
            p.block.vector(vmins)

            g_esb = esb_guard

            def aexps(sc, ve_base=ve_base, g_esb=g_esb):
                for mi in range(4):
                    if g_esb > 0:
                        sc.wait_ge(p.h('ve'), g_esb)
                    sc.wait_ge(p.h('ve'), ve_base + mi + 1)
                    ins = sc.activation(esb[:, mi * 512:(mi + 1) * 512],
                                        msb[:, mi * 512:(mi + 1) * 512], AF.Exp)
                    p.inc(ins, 'ac')
            p.block.scalar(aexps)
            msb_guard = p.v('ac')

            for mi in range(4):
                m = mg + mi
                if m < 8:
                    dst = qT[:, m * TL + g * 512:m * TL + (g + 1) * 512]
                else:
                    dst = kstage[:, mi * 512:(mi + 1) * 512]

                def vphi(v, mi=mi, m=m, dst=dst, ac_need=ac_base + mi + 1,
                         gk=(kst_guard[mi] if m >= 8 else 0), ksem=f"ask{mi}"):
                    v.wait_ge(p.h('ac'), ac_need)
                    if m >= 8 and gk > 0:
                        v.wait_ge(p.h(ksem), gk)
                    ins = v.scalar_tensor_tensor(dst, pa[mi % 4][:, :], 1.0,
                                                 esb[:, mi * 512:(mi + 1) * 512],
                                                 ALU.add, ALU.max)
                    p.inc(ins, 've')
                p.block.vector(vphi)
                pa_guard[mi % 4] = ('ve', p.v('ve'))
                ve_phi = p.v('ve')

                if m >= 8:
                    def spk(gp, m=m, g=g, mi=mi, ve_phi=ve_phi, ksem=f"ask{mi}"):
                        gp.wait_ge(p.h('ve'), ve_phi)
                        p.inc(gp.dma_start(out=kT_hbm.ap()[m - 8][:, g * 512:(g + 1) * 512],
                                           in_=kstage[:, mi * 512:(mi + 1) * 512]), ksem, 16)
                    p.block.gpsimd(spk)
                    kst_guard[mi] = p.v(f"ask{mi}")
            esb_guard = p.v('ve')
        x2g_guard[s] = p.v('pe')
    sp_partA = [('asv0', p.v('asv0')), ('asv1', p.v('asv1'))] + \
               [(f'ask{i}', p.v(f'ask{i}')) for i in range(4)]
    if DBG.get('attn_kv_tap') is not None:
        kvt = DBG['attn_kv_tap']

        def dkv(gp, sp_partA=tuple(sp_partA)):
            for sn, sv in sp_partA:
                if sv > 0:
                    gp.wait_ge(p.h(sn), sv)
            p.inc(gp.dma_start(out=kvt[0].rearrange("p (m t) -> p m t", m=DC),
                               in_=kT_r[:, :, 0:128]), 'sp', 16)
            p.inc(gp.dma_start(out=kvt[1], in_=v_hbm.ap()[0]), 'sp', 16)
            p.inc(gp.dma_start(out=kvt[2].rearrange("p (m t) -> p m t", t=128),
                               in_=qT.rearrange("p (m t) -> p m t", m=PR)[:, :, 0:128]), 'sp', 16)
        p.block.gpsimd(dkv)

    partA_stack.close()
    Rb = stack.enter_context(nc.sbuf_tensor("aa_R", [128, 2 * DC * 512], BF16)).ap()
    xo3 = stack.enter_context(nc.sbuf_tensor("aa_xo3", [128, DC * 512], BF16)).ap()

    def zeroS(g):
        ins = g.memset(S32, 0.0)
        p.inc(ins, 'pool')
    p.block.gpsimd(zeroS)
    pool_z = p.v('pool')

    def zwait(v):
        v.wait_ge(p.h('pool'), pool_z)
    p.block.vector(zwait)

    kcr_guard = [0, 0]
    vcr_guard = [0, 0]
    ktm_guard = [0, 0]
    ps_guard = [('pe', 0), ('pe', 0), ('pe', 0), ('pe', 0)]
    for c in range(NC_):
        s = c % 2
        g1, g2 = kcr_guard[s], vcr_guard[s]

        ksem, vsem = f"akc{s}", f"avc{s}"

        def lkv(sy, c=c, s=s, g1=g1, g2=g2, ksem=ksem, vsem=vsem, spA=tuple(sp_partA)):
            for sn, sv in spA:
                if sv > 0:
                    sy.wait_ge(p.h(sn), sv)
            if g1 > 0:
                sy.wait_ge(p.h('pe'), g1)
            p.inc(sy.dma_start(out=kcr[:, s * 1024:(s + 1) * 1024].rearrange("p (m t) -> p m t", m=DC),
                               in_=kT_r[:, :, c * 128:(c + 1) * 128]), ksem, 16)
            if g2 > 0:
                sy.wait_ge(p.h('pe'), g2)
            p.inc(sy.dma_start(out=vcr[:, s * 1024:(s + 1) * 1024], in_=v_hbm.ap()[c]), vsem, 16)
        p.block.sync(lkv)
        al_c = [(ksem, p.v(ksem)), (vsem, p.v(vsem))]
        gq_ = pq_guard[s]

        def ktr(tt, c=c, s=s, al_c=tuple(al_c), gq_=gq_):
            for sn, sv in al_c:
                tt.wait_ge(p.h(sn), sv)
            if gq_ > 0:
                tt.wait_ge(p.h('ve'), gq_)
            for m in range(PR):
                ins = tt.transpose(pq[s][:, m * 128:(m + 1) * 128],
                                   kcr[:, s * 1024 + m * 128:s * 1024 + (m + 1) * 128],
                                   cst['identbf'])
            v_ = p.inc(ins, 'pe')
            _pedrain(p, tt, v_)
        p.block.tensor(ktr)
        pe_tr = p.v('pe')

        def kev(v, s=s, pe_tr=pe_tr):
            v.wait_ge(p.h('pe'), pe_tr)
            ins = v.tensor_copy(ktm[:, s * 1024:(s + 1) * 1024], pq[s][:, :])
            p.inc(ins, 've')
        p.block.vector(kev)
        pq_guard[s] = p.v('ve')
        ktm_guard[s] = 0
        ve_k = p.v('ve')

        for pr in range(PR):
            bk = pr % 4
            gn, gv_ = ps_guard[bk]

            def smm(tt, c=c, s=s, pr=pr, bk=bk, gn=gn, gv_=gv_, ve_k=ve_k, al_c=tuple(al_c)):
                tt.wait_ge(p.h('ve'), ve_k)
                for sn, sv in al_c:
                    tt.wait_ge(p.h(sn), sv)
                if gv_ > 0:
                    tt.wait_ge(p.h(gn), gv_)
                kb = ktm[:, s * 1024 + pr * 128:s * 1024 + pr * 128 + 64]
                ko = ktm[:, s * 1024 + pr * 128 + 64:s * 1024 + pr * 128 + 128]
                ve_ = vcr[:, s * 1024 + (2 * pr) * 64:s * 1024 + (2 * pr) * 64 + 64]
                vo = vcr[:, s * 1024 + (2 * pr + 1) * 64:s * 1024 + (2 * pr + 1) * 64 + 64]
                tt.matmul(pa[bk][0:64, 0:64], kb, ve_, start=True, stop=False)
                tt.matmul(pa[bk][0:64, 64:65], kb, cst['ones_bf'], start=False, stop=True)
                tt.matmul(pa[bk][64:128, 0:64], ko, vo, start=True, stop=False,
                          tile_position=(0, 64))
                ins = tt.matmul(pa[bk][64:128, 64:65], ko, cst['ones_bf'], start=False,
                                stop=True, tile_position=(0, 64))
                v_ = p.inc(ins, 'pe')
                _pedrain(p, tt, v_)
            p.block.tensor(smm)
            pe_s = p.v('pe')

            def sadd(v, c=c, pr=pr, bk=bk, pe_s=pe_s):
                v.tensor_copy(ckpt[:, c * 520 + pr * 65:c * 520 + (pr + 1) * 65],
                              S32[:, pr * 65:(pr + 1) * 65])
                v.wait_ge(p.h('pe'), pe_s)
                ins = v.tensor_tensor(S32[:, pr * 65:(pr + 1) * 65],
                                      S32[:, pr * 65:(pr + 1) * 65],
                                      pa[bk][:, 0:65], ALU.add)
                p.inc(ins, 've')
            p.block.vector(sadd)
            ps_guard[bk] = ('ve', p.v('ve'))
        kcr_guard[s] = vcr_guard[s] = p.v('pe')
    pa_guard = ps_guard

    if DBG.get('ck_tap') is not None:
        ckt = DBG['ck_tap']
        vck = p.v('ve')

        def dck(gp, vck=vck):
            gp.wait_ge(p.h('ve'), vck)
            p.inc(gp.dma_start(out=ckt[0], in_=ckpt[:, 0:1040]), 'sp', 16)
            p.inc(gp.dma_start(out=ckt[1][:, 0:520], in_=S32), 'sp', 16)
        p.block.gpsimd(dck)

    def vccs(v):
        ins = v.tensor_scalar(ccs, S32, cst['rolei'], None, ALU.mult)
        p.inc(ins, 've')
    p.block.vector(vccs)
    ve_ccs = p.v('ve')

    def stccs(sy, ve_ccs=ve_ccs):
        sy.wait_ge(p.h('ve'), ve_ccs)
        p.inc(sy.dma_start(out=cc_in.ap(), in_=ccs), 'accs', 16)
    p.block.sync(stccs)
    al_ccs = p.v('accs')

    def docc(gp, al_ccs=al_ccs):
        gp.wait_ge(p.h('accs'), al_ccs)
        p.inc(gp.collective_compute(
            "AllReduce", ALU.add,
            replica_groups=(groups if groups is not None else [[0, 1], [2, 3], [4, 5], [6, 7]]),
            ins=[cc_in.ap().opt()],
            outs=[cc_out.ap().opt()]), 'cc', 1)
    p.block.gpsimd(docc)
    cc_v = p.v('cc')

    def ldcc(sy, cc_v=cc_v):
        sy.wait_ge(p.h('cc'), cc_v)
        p.inc(sy.dma_start(out=recv, in_=cc_out.ap()), 'arcv', 16)
    p.block.sync(ldcc)
    al_rcv = p.v('arcv')

    def vsini(v, al_rcv=al_rcv):
        v.wait_ge(p.h('arcv'), al_rcv)
        ins = v.tensor_scalar(Sini, recv, cst['role'], None, ALU.mult)
        p.inc(ins, 've')
    p.block.vector(vsini)
    ve_sini = p.v('ve')

    x2r_guard = 0
    pend_ln3 = None
    rb3_guard = [0, 0]
    yp_guard = [0, 0]
    ssb_guard = [0, 0]
    dr_guard = [0, 0]
    Wp_guard = [0, 0]
    wp_i = 0
    yTg_guard = 0
    xo_guard = 0
    for g in range(8):
        gx = x2r_guard

        def lxr(sy, g=g, gx=gx):
            if gx > 0:
                sy.wait_ge(p.h('ve'), gx)
            p.inc(sy.dma_start(out=x2r.rearrange("p (k t) -> p k t", k=DC),
                               in_=x_in_r[:, :, g * 512:(g + 1) * 512]), 'axr', 16)
        p.block.sync(lxr)
        al_xr = p.v('axr')

        for ci in range(4):
            c = g * 4 + ci
            s = c % 2
            g1, g2 = kcr_guard[s], vcr_guard[s]

            ksem, vsem = f"akc{s}", f"avc{s}"

            def lkv2(sy, c=c, s=s, g1=g1, g2=g2, ksem=ksem, vsem=vsem):
                if g1 > 0:
                    sy.wait_ge(p.h('pe'), g1)
                p.inc(sy.dma_start(out=kcr[:, s * 1024:(s + 1) * 1024].rearrange("p (m t) -> p m t", m=DC),
                                   in_=kT_r[:, :, c * 128:(c + 1) * 128]), ksem, 16)
                if g2 > 0:
                    sy.wait_ge(p.h('pe'), g2)
                p.inc(sy.dma_start(out=vcr[:, s * 1024:(s + 1) * 1024], in_=v_hbm.ap()[c]), vsem, 16)
            p.block.sync(lkv2)
            al_c = [(ksem, p.v(ksem)), (vsem, p.v(vsem))]

            def emit_scores(hd):
                pr, hf = hd // 2, hd % 2
                base = hf * 64
                bk = hd % 4
                gn, gv_ = pa_guard[bk]
                tp = (64, 0) if hf else None

                def smm2(tt, c=c, s=s, pr=pr, hf=hf, base=base, bk=bk, gn=gn, gv_=gv_,
                         al_c=tuple(al_c), tp=tp, first=(c == 0 and hd == 0), ve_sini=ve_sini):
                    if first:
                        tt.wait_ge(p.h('ve'), ve_sini)
                    for sn, sv in al_c:
                        tt.wait_ge(p.h(sn), sv)
                    if gv_ > 0:
                        tt.wait_ge(p.h(gn), gv_)
                    ins = tt.matmul(pa[bk][:, 0:128],
                                    kcr[base:base + 64, s * 1024 + pr * 128:s * 1024 + (pr + 1) * 128],
                                    qT[base:base + 64, pr * TL + c * 128:pr * TL + (c + 1) * 128],
                                    start=True, stop=True, tile_position=tp)
                    p.inc(ins, 'pe')
                p.block.tensor(smm2)
                pe_sc = p.v('pe')
                gsb_ = ssb_guard[hd % 2]

                def mask(v, bk=bk, hh=hd % 2, pe_sc=pe_sc, gsb_=gsb_):
                    v.wait_ge(p.h('pe'), pe_sc)
                    if gsb_ > 0:
                        v.wait_ge(p.h('pe'), gsb_)
                    ins = v.tensor_tensor(ssb[:, hh * 128:(hh + 1) * 128], cst['maskT'],
                                          pa[bk][:, 0:128], ALU.mult)
                    p.inc(ins, 've')
                p.block.vector(mask)
                return p.v('ve')

            def emit_numdiv(hd, ve_msk):
                pr, hf = hd // 2, hd % 2
                base = hf * 64
                bk = hd % 4
                tp = (64, 0) if hf else None

                def nmm(tt, c=c, s=s, pr=pr, hf=hf, hd=hd, base=base, bk=bk, hh=hd % 2,
                        ve_msk=ve_msk, tp=tp):
                    tt.wait_ge(p.h('ve'), ve_msk)
                    tt.matmul(pa[bk][:, 0:64], ssb[:, hh * 128:(hh + 1) * 128],
                              vcr[:, s * 1024 + hd * 64:s * 1024 + (hd + 1) * 64],
                              start=True, stop=False)
                    tt.matmul(pa[bk][:, 64:65], ssb[:, hh * 128:(hh + 1) * 128],
                              cst['ones_bf'], start=False, stop=False)
                    qsl = qT[base:base + 64, pr * TL + c * 128:pr * TL + (c + 1) * 128]
                    tt.matmul(pa[bk][:, 0:65], qsl,
                              ckpt[base:base + 64, c * 520 + pr * 65:c * 520 + (pr + 1) * 65],
                              start=False, stop=False, tile_position=tp)
                    ins = tt.matmul(pa[bk][:, 0:65], qsl,
                                    Sini[base:base + 64, pr * 65:(pr + 1) * 65],
                                    start=False, stop=True, tile_position=tp)
                    p.inc(ins, 'pe')
                p.block.tensor(nmm)
                pe_n = p.v('pe')

                def vden(v, bk=bk, hh=hd % 2, pe_n=pe_n):
                    v.wait_ge(p.h('pe'), pe_n)
                    c1 = p.inc(v.tensor_scalar(drow[:, hh * 2:hh * 2 + 1], pa[bk][:, 64:65],
                                               1e-6, None, ALU.add), 've')
                    v.wait_ge(p.h('ve'), c1)
                    ins = v.reciprocal(drow[:, hh * 2 + 1:hh * 2 + 2],
                                       drow[:, hh * 2:hh * 2 + 1])
                    p.inc(ins, 've')
                p.block.vector(vden)
                ve_d = p.v('ve')
                gyp = yp_guard[s] if hd == 0 else 0

                def ydiv(v, bk=bk, hh=hd % 2, s=s, pr=pr, base=base, ve_d=ve_d, gyp=gyp):
                    v.wait_ge(p.h('ve'), ve_d)
                    if gyp > 0:
                        v.wait_ge(p.h('pe'), gyp)
                    ins = v.tensor_scalar(
                        ypair[:, s * 1024 + pr * 128 + base:s * 1024 + pr * 128 + base + 64],
                        pa[bk][:, 0:64], drow[:, hh * 2 + 1:hh * 2 + 2], None, ALU.mult)
                    p.inc(ins, 've')
                p.block.vector(ydiv)
                pa_guard[bk] = ('ve', p.v('ve'))
                ssb_guard[hd % 2] = pe_n

            vmsk = [0] * NH
            for hd in range(NH):
                vmsk[hd] = emit_scores(hd)
                if hd >= 1:
                    emit_numdiv(hd - 1, vmsk[hd - 1])
            emit_numdiv(NH - 1, vmsk[NH - 1])
            kcr_guard[s] = vcr_guard[s] = p.v('pe')
            ac_y = p.v('ve')
            tapv = 0
            if DBG.get('yp_tap') is not None and c == DBG['attn_c']:
                tap2 = DBG['yp_tap']

                def dyp(v, s=s, ac_y=ac_y):
                    v.wait_ge(p.h('ve'), ac_y)
                    ins = v.tensor_copy(tap2, ypair[:, s * 1024:(s + 1) * 1024])
                    p.inc(ins, 've')
                p.block.vector(dyp)
                tapv = p.v('ve')
            gq_ = pq_guard[s]

            def ytr(tt, s=s, ac_y=ac_y, gq_=gq_, tapv=tapv):
                tt.wait_ge(p.h('ve'), ac_y)
                if tapv > 0:
                    tt.wait_ge(p.h('ve'), tapv)
                if gq_ > 0:
                    tt.wait_ge(p.h('ve'), gq_)
                for pr2 in range(PR):
                    ins = tt.transpose(pq[s][:, pr2 * 128:(pr2 + 1) * 128],
                                       ypair[:, s * 1024 + pr2 * 128:s * 1024 + (pr2 + 1) * 128],
                                       cst['identbf'])
                v_ = p.inc(ins, 'pe')
                _pedrain(p, tt, v_)
            p.block.tensor(ytr)
            yp_guard[s] = p.v('pe')
            pe_y = p.v('pe')
            gyt = yTg_guard if ci == 0 else 0
            gsp = xo_guard if ci == 0 else 0

            def yev(v, s=s, ci=ci, pe_y=pe_y, gyt=gyt, gsp=gsp):
                v.wait_ge(p.h('pe'), pe_y)
                if gyt > 0:
                    v.wait_ge(p.h('pe'), gyt)
                if gsp > 0:
                    v.wait_ge(p.h('spx3'), gsp)
                ins = v.tensor_copy(
                    yTg.rearrange("p (m t) -> p m t", m=PR)[:, :, ci * 128:(ci + 1) * 128],
                    pq[s][:, :].rearrange("p (m t) -> p m t", m=PR))
                p.inc(ins, 've')
            p.block.vector(yev)
            pq_guard[s] = p.v('ve')
        ve_y = p.v('ve')

        r_sl = [Rb[:, (g % 2) * DC * 512 + mc * 512:(g % 2) * DC * 512 + (mc + 1) * 512]
                for mc in range(DC)]
        for m in range(DC):
            ws_ = wp_i % 2
            wp_i += 1
            gw = Wp_guard[ws_]

            psem = f"awp{ws_}"

            def lwp(sy, m=m, ws_=ws_, gw=gw, psem=psem, cd=wsproj.cast_wait(m)):
                if gw > 0:
                    sy.wait_ge(p.h('pe'), gw)
                sy.wait_ge(p.h(cd[0]), cd[1])
                p.inc(sy.dma_start(out=Wp[:, ws_ * 1024:(ws_ + 1) * 1024],
                                   in_=wsproj.block_ap(m)), psem, 16)
            p.block.sync(lwp)
            wl_p = (psem, p.v(psem))
            bk = m % 4
            gn, gv_ = pa_guard[bk]

            def pmm(tt, m=m, ws_=ws_, bk=bk, gn=gn, gv_=gv_, wl_p=wl_p, ve_y=ve_y):
                tt.wait_ge(p.h(wl_p[0]), wl_p[1])
                tt.wait_ge(p.h('ve'), ve_y)
                if gv_ > 0:
                    tt.wait_ge(p.h(gn), gv_)
                for kp in range(PR):
                    ins = tt.matmul(pa[bk][:, :],
                                    Wp[:, ws_ * 1024 + kp * 128:ws_ * 1024 + (kp + 1) * 128],
                                    yTg[:, kp * 512:(kp + 1) * 512],
                                    start=(kp == 0), stop=(kp == PR - 1))
                v_ = p.inc(ins, 'pe')
                _pedrain(p, tt, v_)
            p.block.tensor(pmm)
            Wp_guard[ws_] = p.v('pe')
            pe_p = p.v('pe')

            g_rb3 = rb3_guard[g % 2]

            def resid(v, m=m, bk=bk, pe_p=pe_p, al_xr=al_xr, g_rb3=g_rb3):
                v.wait_ge(p.h('pe'), pe_p)
                v.wait_ge(p.h('axr'), al_xr)
                if g_rb3 > 0:
                    v.wait_ge(p.h('pe'), g_rb3)
                ins = v.tensor_tensor(r_sl[m], x2r[:, m * 512:(m + 1) * 512],
                                      pa[bk][:, :], ALU.add)
                p.inc(ins, 've')
            p.block.vector(resid)
            pa_guard[bk] = ('ve', p.v('ve'))
        x2r_guard = p.v('ve')
        yTg_guard = p.v('pe')
        ve_resid_done = p.v('ve')

        def make_ln3(g=g, gsl=(g % 2), ve_resid_done=ve_resid_done):
            def flush():
                nonlocal xo_guard, yTg_guard
                r_sl2 = [Rb[:, gsl * DC * 512 + mc * 512:gsl * DC * 512 + (mc + 1) * 512]
                         for mc in range(DC)]
                o_sl = [xo3[:, mc * 512:(mc + 1) * 512] for mc in range(DC)]
                ln_state['rows_read_ve'] = max(ln_state.get('rows_read_ve', 0), p.v('ve'))
                ln_state['norm_done_ve'] = max(ln_state.get('norm_done_ve', 0), p.v('ve'))
                if xo_guard > 0:
                    def vwait(v, g_=xo_guard):
                        v.wait_ge(p.h('spx3'), g_)
                    p.block.vector(vwait)
                emit_ln512(p, cst, r_sl2, o_sl, pb, ln_state, ve_r=ve_resid_done)
                rb3_guard[gsl] = p.v('pe')
                for i in range(4):
                    pa_guard[i] = ('ve', p.v('ve'))
                ve_done = p.v('ve')

                def store(gp, g=g, ve_done=ve_done):
                    gp.wait_ge(p.h('ve'), ve_done)
                    p.inc(gp.dma_start(out=x_out_r[:, :, g * 512:(g + 1) * 512],
                                       in_=xo3.rearrange("p (c t) -> p c t", c=DC)), 'spx3', 16)
                p.block.gpsimd(store)
                xo_guard = p.v('spx3')
            return flush

        if pend_ln3 is not None:
            pend_ln3()
        pend_ln3 = make_ln3()
    pend_ln3()

    return ('spx3', p.v('spx3'))


def emit_l1ffn(p, cst, wsf1, wsf2, wsh1, wsh2, wsf3, x_in, y_out, stack, x_ready=None):
    nc = p.nc
    NT = 1024
    ntiles = TL // NT
    A = stack.enter_context(nc.sbuf_tensor("lf_A", [128, HC * NT], BF16)).ap()
    Bh = stack.enter_context(nc.sbuf_tensor("lf_B", [128, HC * NT], BF16)).ap()
    X3 = stack.enter_context(nc.sbuf_tensor("lf_x", [128, DC * NT], BF16)).ap()
    Wb = stack.enter_context(nc.sbuf_tensor("lf_W", [128, 2 * 4096], BF16)).ap()
    Rb = stack.enter_context(nc.sbuf_tensor("lf_R", [128, DC * NT], BF16)).ap()
    sil = stack.enter_context(nc.sbuf_tensor("lf_sil", [128, 2 * NT], BF16)).ap()
    yst = stack.enter_context(nc.sbuf_tensor("lf_y", [128, 512], F32)).ap()
    rowb = stack.enter_context(nc.sbuf_tensor("lf_rows", [128, 8], F32)).ap()
    pbank = [stack.enter_context(nc.psum_tensor(f"lf_p{i}", [128, 512], F32)).ap() for i in range(4)]
    pq = [stack.enter_context(nc.psum_tensor(f"lf_q{i}", [128, 1024], BF16)).ap() for i in range(2)]
    rt = stack.enter_context(nc.psum_tensor("lf_rt", [128, 1024], F32)).ap()

    x_in_r = x_in.ap().rearrange("c p t -> p c t")
    rb_read_guard = [0]
    yout_guard = [0, 0]
    wslot = [Wb[:, i * 4096:(i + 1) * 4096] for i in range(2)]
    wguard = [0, 0]
    wsl_i = 0
    r2t = sil[:, 0:1024]
    bank_guard = [('ac', 0)] * 4
    sil_guard = [0, 0]
    bsc_guard = [0, 0]
    pq_guard = [0, 0]
    rt_guard_ac = 0
    srow, s2row, mu_c, e2_c, ng2, var_c, sd_c, rs_c = (rowb[:, i:i + 1] for i in range(8))

    def wload(ws, m, cols):
        nonlocal wsl_i
        s = wsl_i % 2
        wsl_i += 1
        g = wguard[s]
        cd = ws.cast_done

        wsem = f"lfw{s}"

        def go(sy, ws=ws, m=m, s=s, g=g, cd=ws.cast_wait(m), cols=cols, wsem=wsem):
            if g > 0:
                sy.wait_ge(p.h('pe'), g)
            sy.wait_ge(p.h(cd[0]), cd[1])
            p.inc(sy.dma_start(out=wslot[s][:, 0:cols], in_=ws.block_ap(m)), wsem, 16)
        p.block.sync(go)
        return s, (wsem, p.v(wsem))

    x3guards = [0, 0]
    al_xs = [None] * (ntiles + 1)

    def load_x3(t):
        s = 0
        g_x3 = x3guards[s]
        sem = f"lfx{s}"

        def loadx(sy, t=t, s=s, g_x3=g_x3, sem=sem):
            if x_ready is not None and x_ready[1] > 0:
                sy.wait_ge(p.h(x_ready[0]), x_ready[1])
            if g_x3 > 0:
                sy.wait_ge(p.h('ve'), g_x3)
            p.inc(sy.dma_start(out=X3[:, s * DC * NT:(s + 1) * DC * NT].rearrange("p (c t) -> p c t", c=DC),
                               in_=x_in_r[:, :, t * NT:(t + 1) * NT]), sem, 16)
        p.block.sync(loadx)
        return (sem, p.v(sem))

    for t in range(ntiles):
        al_x = load_x3(t)
        X3t = X3[:, 0:DC * NT]

        def mm_stage(ws, m, src_buf, kK, b0, wl_extra=None):
            s, wlv = wload(ws, m, kK * 128)
            g0, g1 = bank_guard[b0], bank_guard[b0 + 1]

            def mm(tt, s=s, wlv=wlv, m=m, src_buf=src_buf, kK=kK, b0=b0, g0=g0, g1=g1,
                   wl_extra=wl_extra):
                tt.wait_ge(p.h(wlv[0]), wlv[1])
                if wl_extra is not None:
                    for gn, gv in wl_extra:
                        if gv > 0:
                            tt.wait_ge(p.h(gn), gv)
                for gn, gv in {g0, g1}:
                    if gv > 0:
                        tt.wait_ge(p.h(gn), gv)
                for k in range(kK):
                    for n in range(2):
                        ins = tt.matmul(pbank[b0 + n][:, :],
                                        wslot[s][:, k * 128:(k + 1) * 128],
                                        src_buf(k, n),
                                        start=(k == 0), stop=(k == kK - 1))
                p.inc(ins, 'pe')
            p.block.tensor(mm)
            wguard[s] = p.v('pe')
            return p.v('pe')

        xsrc = lambda k, n, X3t=X3t: X3t[:, k * NT + n * 512:k * NT + (n + 1) * 512]

        for m in range(HC):
            pe_a = mm_stage(wsf1, m, xsrc, DC, 0, wl_extra=[al_x])
            pe_b = mm_stage(wsf2, m, xsrc, DC, 2, wl_extra=[al_x])
            mp = m % 2
            g_sil = sil_guard[mp]
            g_bsc = bsc_guard[mp]

            def ev_a(sc, m=m, mp=mp, pe_a=pe_a, g_sil=g_sil):
                sc.wait_ge(p.h('pe'), pe_a)
                if g_sil > 0:
                    sc.wait_ge(p.h('ve'), g_sil)
                for n in range(2):
                    ins = sc.activation(sil[:, mp * NT + n * 512:mp * NT + (n + 1) * 512],
                                        pbank[n][:, :], AF.Silu, scale=cst['s1'][:, m:m + 1])
                p.inc(ins, 'ac')
            p.block.scalar(ev_a)
            bank_guard[0] = bank_guard[1] = ('ac', p.v('ac'))

            ac_here = p.v('ac')

            def hmul(v, m=m, mp=mp, ac_here=ac_here, pe_b=pe_b):
                v.wait_ge(p.h('ac'), ac_here)
                v.wait_ge(p.h('pe'), pe_b)
                for n in range(2):
                    ins = v.scalar_tensor_tensor(
                        A[:, m * NT + n * 512:m * NT + (n + 1) * 512],
                        pbank[2 + n][:, :], cst['s2'][:, m:m + 1],
                        sil[:, mp * NT + n * 512:mp * NT + (n + 1) * 512],
                        ALU.mult, ALU.mult)
                p.inc(ins, 've')
            p.block.vector(hmul)
            bank_guard[2] = bank_guard[3] = ('ve', p.v('ve'))
            sil_guard[mp] = bsc_guard[mp] = p.v('ve')
        ve_hin = p.v('ve')

        hsrc = lambda k, n: A[:, k * NT + n * 512:k * NT + (n + 1) * 512]
        for m in range(HC):
            b0 = (m % 2) * 2
            pe_m = mm_stage(wsh1, m, hsrc, HC, b0, wl_extra=[('ve', ve_hin)])

            def ev(sc, m=m, b0=b0, pe_m=pe_m):
                sc.wait_ge(p.h('pe'), pe_m)
                for n in range(2):
                    ins = sc.activation(Bh[:, m * NT + n * 512:m * NT + (n + 1) * 512],
                                        pbank[b0 + n][:, :], AF.Gelu, scale=cst['sh1'][:, m:m + 1])
                p.inc(ins, 'ac')
            p.block.scalar(ev)
            bank_guard[b0] = bank_guard[b0 + 1] = ('ac', p.v('ac'))
        ac_h1 = p.v('ac')

        h1src = lambda k, n: Bh[:, k * NT + n * 512:k * NT + (n + 1) * 512]
        for m in range(HC):
            b0 = (m % 2) * 2
            pe_m = mm_stage(wsh2, m, h1src, HC, b0, wl_extra=[('ac', ac_h1)])

            def ev(sc, m=m, b0=b0, pe_m=pe_m):
                sc.wait_ge(p.h('pe'), pe_m)
                for n in range(2):
                    ins = sc.activation(A[:, m * NT + n * 512:m * NT + (n + 1) * 512],
                                        pbank[b0 + n][:, :], AF.Gelu, scale=cst['sh2'][:, m:m + 1])
                p.inc(ins, 'ac')
            p.block.scalar(ev)
            bank_guard[b0] = bank_guard[b0 + 1] = ('ac', p.v('ac'))
        ac_h2 = p.v('ac')

        h2src = lambda k, n: A[:, k * NT + n * 512:k * NT + (n + 1) * 512]
        for m3 in range(DC):
            b0 = (m3 % 2) * 2
            pe_m = mm_stage(wsf3, m3, h2src, HC, b0, wl_extra=[('ac', ac_h2)])
            g_rb = rb_read_guard[0]

            def resid(v, m3=m3, b0=b0, pe_m=pe_m, g_rb=g_rb, al_x=al_x, X3t=X3t):
                v.wait_ge(p.h('pe'), pe_m)
                v.wait_ge(p.h(al_x[0]), al_x[1])
                if g_rb > 0:
                    v.wait_ge(p.h('pe'), g_rb)
                for n in range(2):
                    ins = v.scalar_tensor_tensor(
                        Rb[:, m3 * NT + n * 512:m3 * NT + (n + 1) * 512],
                        pbank[b0 + n][:, :], cst['s3'][:, m3:m3 + 1],
                        X3t[:, m3 * NT + n * 512:m3 * NT + (n + 1) * 512],
                        ALU.mult, ALU.add)
                p.inc(ins, 've')
            p.block.vector(resid)
            bank_guard[b0] = bank_guard[b0 + 1] = ('ve', p.v('ve'))
        x3guards[0] = p.v('ve')
        ve_r = p.v('ve')

        for tc in range(DC):
            qp = tc % 2
            g_pq = pq_guard[qp]

            def trans(tt, tc=tc, qp=qp, g_pq=g_pq, ve_r=ve_r):
                tt.wait_ge(p.h('ve'), ve_r)
                if g_pq > 0:
                    tt.wait_ge(p.h('ve'), g_pq)
                for fc in range(DC):
                    ins = tt.transpose(pq[qp][:, fc * 128:(fc + 1) * 128],
                                       Rb[:, fc * NT + tc * 128:fc * NT + (tc + 1) * 128],
                                       cst['identbf'])
                p.inc(ins, 'pe')
            p.block.tensor(trans)
            pe_tr = p.v('pe')
            g_rt = rt_guard_ac

            def ve_sum(v, qp=qp, pe_tr=pe_tr, g_rt=g_rt):
                v.wait_ge(p.h('pe'), pe_tr)
                if g_rt > 0:
                    v.wait_ge(p.h('ac'), g_rt)
                ins = v.tensor_scalar(rt, pq[qp][:, :], 1.0, None, ALU.mult, ALU.add,
                                      accum_out=srow)
                p.inc(ins, 've')
            p.block.vector(ve_sum)
            ve_s = p.v('ve')
            pq_guard[qp] = ve_s

            def ac_sq(sc, ve_s=ve_s):
                sc.wait_ge(p.h('ve'), ve_s)
                ins = sc.activation(r2t, rt, AF.Square, accum_out=s2row)
                p.inc(ins, 'ac')
            p.block.scalar(ac_sq)
            ac_sqv = p.v('ac')
            rt_guard_ac = ac_sqv

            def ve_stats(v, ac_sqv=ac_sqv):
                v.wait_ge(p.h('ac'), ac_sqv)
                v.tensor_scalar(mu_c, srow, 1.0 / 1024.0, None, ALU.mult)
                c1 = p.inc(v.scalar_tensor_tensor(ng2, srow, -1.0 / 1024.0, srow,
                                                  ALU.mult, ALU.mult), 've')
                v.wait_ge(p.h('ve'), c1)
                ins = v.tensor_tensor(var_c, s2row, ng2, ALU.add)
                p.inc(ins, 've')
            p.block.vector(ve_stats)
            ve_st = p.v('ve')

            def ac_sd(sc, ve_st=ve_st):
                sc.wait_ge(p.h('ve'), ve_st)
                ins = sc.activation(sd_c, var_c, AF.Sqrt, bias=cst['eps_ln'],
                                    scale=1.0 / 1024.0)
                p.inc(ins, 'ac')
            p.block.scalar(ac_sd)
            ac_sdv = p.v('ac')

            for half in range(2):
                ysl = yst[:, 0:512]
                g_y = yout_guard[0]

                def ve_fin(v, ac_sdv=ac_sdv, ysl=ysl, g_y=g_y, half=half):
                    v.wait_ge(p.h('ac'), ac_sdv)
                    if half == 0:
                        c1 = p.inc(v.reciprocal(rs_c, sd_c), 've')
                        v.wait_ge(p.h('ve'), c1)
                    if g_y > 0:
                        v.wait_ge(p.h('out'), g_y)
                    ins = v.tensor_scalar(ysl, rt[:, half * 512:(half + 1) * 512],
                                          mu_c, rs_c, ALU.subtract, ALU.mult)
                    p.inc(ins, 've')
                p.block.vector(ve_fin)
                ve_f = p.v('ve')

                def store(sy, t=t, tc=tc, ysl=ysl, ve_f=ve_f, half=half):
                    sy.wait_ge(p.h('ve'), ve_f)
                    p.inc(sy.dma_start(
                        out=y_out.ap()[t * NT + tc * 128:t * NT + (tc + 1) * 128,
                                       half * 512:(half + 1) * 512],
                        in_=ysl), 'out', 16)
                p.block.sync(store)
                yout_guard[0] = p.v('out')
        rb_read_guard[0] = p.v('pe')

    return ('out', p.v('out'))


def phase_barrier(p):
    vals = [(name, s[1]) for name, s in p._sems.items()
            if s[1] > 0 and not name.startswith('cast') and name != 'c_al']

    def w(e):
        for name, v in vals:
            e.wait_ge(p.h(name), v)
    for eng in ('tensor', 'vector', 'scalar', 'sync', 'gpsimd'):
        getattr(p.block, eng)(w)


def alloc_psum(stack, nc, tag):
    pb = {}
    for i in range(6):
        pb[f'p{i}'] = stack.enter_context(nc.psum_tensor(f"{tag}_p{i}", [128, 512], F32)).ap()
    pb['q6'] = stack.enter_context(nc.psum_tensor(f"{tag}_p6", [128, 1024], BF16)).ap()
    pb['q7'] = stack.enter_context(nc.psum_tensor(f"{tag}_p7", [128, 1024], BF16)).ap()
    return pb


def make_consts(p, stack, const_in, row1_in):
    nc = p.nc
    cst = {}
    csb = stack.enter_context(nc.sbuf_tensor("consts", [128, 456], F32)).ap()
    row1 = stack.enter_context(nc.sbuf_tensor("row1", [1, 128], F32)).ap()
    cbf = stack.enter_context(nc.sbuf_tensor("constbf", [128, 4], BF16)).ap()
    identbf = stack.enter_context(nc.sbuf_tensor("identbf", [128, 128], BF16)).ap()
    misc = stack.enter_context(nc.sbuf_tensor("miscf32", [128, 2560], F32)).ap()

    def go(sy):
        p.inc(sy.dma_start(out=csb, in_=const_in), 'al', 16)
        p.inc(sy.dma_start(out=row1, in_=row1_in), 'al', 16)
    p.block.sync(go)

    def gocast(g):
        p.inc(g.dma_start(out=cbf[:, 0:2], in_=const_in[:, 0:2]), 'cast', 16)
        p.inc(g.dma_start(out=identbf, in_=const_in[:, 324:452]), 'cast', 16)
    p.block.gpsimd(gocast)
    cv = p.v('cast')
    av = p.v('al')

    def wall(e):
        e.wait_ge(p.h('al'), av)
        e.wait_ge(p.h('cast'), cv)
    for eng in ('tensor', 'vector', 'scalar', 'sync', 'gpsimd'):
        getattr(p.block, eng)(wall)

    cst.update(
        ones_sc_bf=cbf[:, 0:1], ones_bf=cbf[:, 1:2],
        ones1_f=row1, role=csb[:, 2:3], rolei=csb[:, 3:4],
        b1=csb[:, 4:36], b2=csb[:, 36:44], gamma=csb[:, 44:52], beta=csb[:, 52:60],
        s1=csb[:, 60:92], s2=csb[:, 92:124], sh1=csb[:, 124:156], sh2=csb[:, 156:188],
        s3=csb[:, 188:196], maskT=csb[:, 196:324], identbf=identbf,
        eps_ln=csb[:, 452:453], eps_att=csb[:, 453:454], one_col=csb[:, 454:455],
        lntmp=misc[:, 0:512], rows=misc[0:1, 512:2560])
    return cst


def host_consts(inputs, core_role):
    cc = np.zeros((128, 456), np.float32)
    cc[:, 0] = 1.0 / 1024.0
    cc[:, 1] = 1.0
    cc[:, 2] = float(core_role)
    cc[:, 3] = 1.0 - float(core_role)
    cc[:, 4:36] = inputs['L0_ffn_b1'].reshape(32, 128).T
    cc[:, 36:44] = inputs['L0_ffn_b2'].reshape(8, 128).T
    cc[:, 44:52] = inputs['L0_attn_gamma'].reshape(8, 128).T
    cc[:, 52:60] = inputs['L0_attn_beta'].reshape(8, 128).T
    cc[:, 60:92] = inputs['L1_ffn_w1_s'].reshape(32, 128).T
    cc[:, 92:124] = inputs['L1_ffn_w2_s'].reshape(32, 128).T
    cc[:, 124:156] = inputs['L1_ffn_h1_s'].reshape(32, 128).T
    cc[:, 156:188] = inputs['L1_ffn_h2_s'].reshape(32, 128).T
    cc[:, 188:196] = inputs['L1_ffn_w3_s'].reshape(8, 128).T
    cc[:, 196:324] = np.triu(np.ones((128, 128), np.float32))
    cc[:, 324:452] = np.eye(128, dtype=np.float32)
    cc[:, 452] = LN_EPS
    cc[:, 453] = 1e-6
    cc[:, 454] = 1.0
    row1 = np.ones((1, 128), np.float32)
    return np.ascontiguousarray(cc), row1


TLH = TL + 2 * HALO

_nc_cache = {}


def _build_l0attn():
    nc = bass.Bass("TRN2", target_bir_lowering=False)
    X0F = nc.dram_tensor("X0F", [DC, 128, TLH], F32, kind="ExternalInput")
    WG = nc.dram_tensor("WG", [1024, 9216], F32, kind="ExternalInput")
    WV = nc.dram_tensor("WV", [1024, 1024], F32, kind="ExternalInput")
    CONST = nc.dram_tensor("CONST", [128, 456], F32, kind="ExternalInput")
    ROW1 = nc.dram_tensor("ROW1", [1, 128], F32, kind="ExternalInput")
    X1 = nc.dram_tensor("X1", [DC, 128, TL], BF16, kind="ExternalOutput")
    xbf = nc.dram_tensor("xbf", [DC, 128, TLH], BF16)
    with ExitStack() as stack:
        p = Prog(nc, stack)
        cst = make_consts(p, stack, CONST.ap(), ROW1.ap())
        wsg = WScratch(nc, "wg", 1024, 9216)
        wsv = WScratch(nc, "wv", 1024, 1024)

        def xcast(g):
            p.inc(g.dma_start(out=xbf.ap(), in_=X0F.ap(), max_dma_last_dim=8192), 'cast', 16)
        p.block.gpsimd(xcast)
        x_ready = ('cast', p.v('cast'))
        wsv.emit_cast(p, WV.ap())
        wsg.emit_cast(p, WG.ap())
        done = emit_l0attn(p, cst, wsg, wsv, xbf, X0F, X1, stack, x_ready=x_ready)

        def fin(g):
            g.wait_ge(p.h(done[0]), done[1])
        p.block.gpsimd(fin)
    return nc


def _build_l0ffn():
    nc = bass.Bass("TRN2", target_bir_lowering=False)
    X1 = nc.dram_tensor("X1", [DC, 128, TL], BF16, kind="ExternalInput")
    W1 = nc.dram_tensor("W1", [1024, 4096], F32, kind="ExternalInput")
    W2 = nc.dram_tensor("W2", [4096, 1024], F32, kind="ExternalInput")
    CONST = nc.dram_tensor("CONST", [128, 456], F32, kind="ExternalInput")
    ROW1 = nc.dram_tensor("ROW1", [1, 128], F32, kind="ExternalInput")
    X2 = nc.dram_tensor("X2", [DC, 128, TL], BF16, kind="ExternalOutput")
    with ExitStack() as stack:
        p = Prog(nc, stack)
        cst = make_consts(p, stack, CONST.ap(), ROW1.ap())
        ws1 = WScratch(nc, "w1", 1024, 4096)
        ws2 = WScratch(nc, "w2", 4096, 1024)
        ws1.emit_cast(p, W1.ap())
        ws2.emit_cast(p, W2.ap())
        done = emit_l0ffn(p, cst, ws1, ws2, X1, X2, stack)

        def fin(g):
            g.wait_ge(p.h(done[0]), done[1])
        p.block.gpsimd(fin)
    return nc


def _build_l1attn():
    nc = bass.Bass("TRN2", target_bir_lowering=False)
    X2 = nc.dram_tensor("X2", [DC, 128, TL], BF16, kind="ExternalInput")
    WQKV = nc.dram_tensor("WQKV", [1024, 3072], F32, kind="ExternalInput")
    WP = nc.dram_tensor("WP", [1024, 1024], F32, kind="ExternalInput")
    CONST = nc.dram_tensor("CONST", [128, 456], F32, kind="ExternalInput")
    ROW1 = nc.dram_tensor("ROW1", [1, 128], F32, kind="ExternalInput")
    X3 = nc.dram_tensor("X3", [DC, 128, TL], BF16, kind="ExternalOutput")
    with ExitStack() as stack:
        p = Prog(nc, stack)
        cst = make_consts(p, stack, CONST.ap(), ROW1.ap())
        wsqkv = WScratch(nc, "qkv", 1024, 3072)
        wsp = WScratch(nc, "wp", 1024, 1024)
        wsqkv.emit_cast(p, WQKV.ap())
        wsp.emit_cast(p, WP.ap())
        done = emit_l1attn(p, cst, wsqkv, wsp, X2, X3, stack)

        def fin(g):
            g.wait_ge(p.h(done[0]), done[1])
        p.block.gpsimd(fin)
    return nc


def _build_l1ffn():
    nc = bass.Bass("TRN2", target_bir_lowering=False)
    X3 = nc.dram_tensor("X3", [DC, 128, TL], BF16, kind="ExternalInput")
    F1 = nc.dram_tensor("F1", [1024, 4096], F32, kind="ExternalInput")
    F2 = nc.dram_tensor("F2", [1024, 4096], F32, kind="ExternalInput")
    H1 = nc.dram_tensor("H1", [4096, 4096], F32, kind="ExternalInput")
    H2 = nc.dram_tensor("H2", [4096, 4096], F32, kind="ExternalInput")
    F3 = nc.dram_tensor("F3", [4096, 1024], F32, kind="ExternalInput")
    CONST = nc.dram_tensor("CONST", [128, 456], F32, kind="ExternalInput")
    ROW1 = nc.dram_tensor("ROW1", [1, 128], F32, kind="ExternalInput")
    Y = nc.dram_tensor("Y", [TL, 1024], F32, kind="ExternalOutput")
    with ExitStack() as stack:
        p = Prog(nc, stack)
        cst = make_consts(p, stack, CONST.ap(), ROW1.ap())
        wf1 = WScratch(nc, "f1", 1024, 4096)
        wf2 = WScratch(nc, "f2", 1024, 4096)
        wh1 = WScratch(nc, "h1", 4096, 4096)
        wh2 = WScratch(nc, "h2", 4096, 4096)
        wf3 = WScratch(nc, "f3", 4096, 1024)
        wf1.emit_cast(p, F1.ap())
        wf2.emit_cast(p, F2.ap())
        wh1.emit_cast(p, H1.ap())
        wh2.emit_cast(p, H2.ap())
        wf3.emit_cast(p, F3.ap())
        done = emit_l1ffn(p, cst, wf1, wf2, wh1, wh2, wf3, X3, Y, stack)

        def fin(g):
            g.wait_ge(p.h(done[0]), done[1])
        p.block.gpsimd(fin)
    return nc


FUSED = True


def _build_fused():
    nc = bass.Bass("TRN2", target_bir_lowering=False)
    X0F = nc.dram_tensor("X0F", [DC, 128, TLH], F32, kind="ExternalInput")
    WG = nc.dram_tensor("WG", [1024, 9216], F32, kind="ExternalInput")
    WV = nc.dram_tensor("WV", [1024, 1024], F32, kind="ExternalInput")
    W1 = nc.dram_tensor("W1", [1024, 4096], F32, kind="ExternalInput")
    W2 = nc.dram_tensor("W2", [4096, 1024], F32, kind="ExternalInput")
    WQKV = nc.dram_tensor("WQKV", [1024, 3072], F32, kind="ExternalInput")
    WP = nc.dram_tensor("WP", [1024, 1024], F32, kind="ExternalInput")
    F1 = nc.dram_tensor("F1", [1024, 4096], F32, kind="ExternalInput")
    F2 = nc.dram_tensor("F2", [1024, 4096], F32, kind="ExternalInput")
    H1 = nc.dram_tensor("H1", [4096, 4096], F32, kind="ExternalInput")
    H2 = nc.dram_tensor("H2", [4096, 4096], F32, kind="ExternalInput")
    F3 = nc.dram_tensor("F3", [4096, 1024], F32, kind="ExternalInput")
    CONST = nc.dram_tensor("CONST", [128, 456], F32, kind="ExternalInput")
    ROW1 = nc.dram_tensor("ROW1", [1, 128], F32, kind="ExternalInput")
    Y = nc.dram_tensor("Y", [TL, 1024], F32, kind="ExternalOutput")
    xbf = nc.dram_tensor("xbf", [DC, 128, TLH], BF16)
    X1 = nc.dram_tensor("X1i", [DC, 128, TL], BF16)
    X2 = nc.dram_tensor("X2i", [DC, 128, TL], BF16)
    X3 = nc.dram_tensor("X3i", [DC, 128, TL], BF16)

    with ExitStack() as gstack:
        p = Prog(nc, gstack)
        cst = make_consts(p, gstack, CONST.ap(), ROW1.ap())

        def xcast(g):
            p.inc(g.dma_start(out=xbf.ap(), in_=X0F.ap(), max_dma_last_dim=8192), 'cast', 16)
        p.block.gpsimd(xcast)
        x_ready = ('cast', p.v('cast'))

        wsv = WScratch(nc, "wv", 1024, 1024)
        wsg = WScratch(nc, "wg", 1024, 9216)
        ws1 = WScratch(nc, "w1", 1024, 4096)
        ws2 = WScratch(nc, "w2", 4096, 1024)
        wsqkv = WScratch(nc, "qkv", 1024, 3072)
        wsp = WScratch(nc, "wp", 1024, 1024)
        wf1 = WScratch(nc, "f1", 1024, 4096)
        wf2 = WScratch(nc, "f2", 1024, 4096)
        wh1 = WScratch(nc, "h1", 4096, 4096)
        wh2 = WScratch(nc, "h2", 4096, 4096)
        wf3 = WScratch(nc, "f3", 4096, 1024)
        for ws_, ap_ in [(wsv, WV), (wsg, WG), (ws1, W1), (ws2, W2),
                         (wsqkv, WQKV), (wsp, WP), (wf1, F1), (wf2, F2),
                         (wh1, H1), (wh2, H2), (wf3, F3)]:
            ws_.emit_cast(p, ap_.ap(), n_groups=(8 if ws_.mK == 72 else (4 if ws_.mK >= 32 else 1)))

        with ExitStack() as s1:
            emit_l0attn(p, cst, wsg, wsv, xbf, X0F, X1, s1, x_ready=x_ready)
            phase_barrier(p)
        with ExitStack() as s2:
            emit_l0ffn(p, cst, ws1, ws2, X1, X2, s2)
            phase_barrier(p)
        with ExitStack() as s3:
            emit_l1attn(p, cst, wsqkv, wsp, X2, X3, s3)
            phase_barrier(p)
        with ExitStack() as s4:
            done = emit_l1ffn(p, cst, wf1, wf2, wh1, wh2, wf3, X3, Y, s4)

        def fin(g):
            g.wait_ge(p.h(done[0]), done[1])
        p.block.gpsimd(fin)
    return nc


def kernel(**inputs):
    x = np.asarray(inputs['L0_attn_x'])

    ncs = _nc_cache
    if FUSED and 'fused' not in ncs:
        ncs['fused'] = _build_fused()
    if not FUSED and 'l0a' not in ncs:
        ncs['l0a'] = _build_l0attn()
        ncs['l0f'] = _build_l0ffn()
        ncs['l1a'] = _build_l1attn()
        ncs['l1f'] = _build_l1ffn()

    WGt = np.ascontiguousarray(np.asarray(inputs['L0_attn_gate_W']).T)
    WVt = np.ascontiguousarray(np.asarray(inputs['L0_attn_value_W']).T)
    W1 = np.ascontiguousarray(np.asarray(inputs['L0_ffn_w1']))
    W2 = np.ascontiguousarray(np.asarray(inputs['L0_ffn_w2']))
    WQKVt = np.ascontiguousarray(np.asarray(inputs['L1_attn_w_attn']).T)
    WPt = np.ascontiguousarray(np.asarray(inputs['L1_attn_w_proj']).T)
    F1t = np.ascontiguousarray(np.asarray(inputs['L1_ffn_w1_w']).T)
    F2t = np.ascontiguousarray(np.asarray(inputs['L1_ffn_w2_w']).T)
    H1t = np.ascontiguousarray(np.asarray(inputs['L1_ffn_h1_w']).T)
    H2t = np.ascontiguousarray(np.asarray(inputs['L1_ffn_h2_w']).T)
    F3t = np.ascontiguousarray(np.asarray(inputs['L1_ffn_w3_w']).T)

    ccs, row1s = [], []
    x0fs = []
    idx_all = np.arange(-HALO, TL + HALO)
    for core in range(8):
        b, h = core // 2, core % 2
        cc, row1 = host_consts(inputs, h)
        ccs.append(cc)
        row1s.append(row1)
        idx = (h * TL + idx_all) % T
        x0h = x[b][idx]
        x0fs.append(np.ascontiguousarray(x0h.T).reshape(DC, 128, TLH))

    cores = list(range(8))
    if FUSED:
        rf = run_bass_kernel_spmd(ncs['fused'], [
            {"X0F": x0fs[c], "WG": WGt, "WV": WVt, "W1": W1, "W2": W2,
             "WQKV": WQKVt, "WP": WPt, "F1": F1t, "F2": F2t, "H1": H1t,
             "H2": H2t, "F3": F3t, "CONST": ccs[c], "ROW1": row1s[c]}
            for c in range(8)], core_ids=cores)
        out = np.empty((B, T, D), np.float32)
        for core in range(8):
            b, h = core // 2, core % 2
            out[b, h * TL:(h + 1) * TL, :] = rf.results[core]["Y"]
        return out

    r1 = run_bass_kernel_spmd(ncs['l0a'], [
        {"X0F": x0fs[c], "WG": WGt, "WV": WVt, "CONST": ccs[c], "ROW1": row1s[c]}
        for c in range(8)], core_ids=cores)
    r2 = run_bass_kernel_spmd(ncs['l0f'], [
        {"X1": r1.results[c]["X1"], "W1": W1, "W2": W2, "CONST": ccs[c], "ROW1": row1s[c]}
        for c in range(8)], core_ids=cores)
    r3 = run_bass_kernel_spmd(ncs['l1a'], [
        {"X2": r2.results[c]["X2"], "WQKV": WQKVt, "WP": WPt, "CONST": ccs[c], "ROW1": row1s[c]}
        for c in range(8)], core_ids=cores)
    r4 = run_bass_kernel_spmd(ncs['l1f'], [
        {"X3": r3.results[c]["X3"], "F1": F1t, "F2": F2t, "H1": H1t, "H2": H2t,
         "F3": F3t, "CONST": ccs[c], "ROW1": row1s[c]}
        for c in range(8)], core_ids=cores)

    out = np.empty((B, T, D), np.float32)
    for core in range(8):
        b, h = core // 2, core % 2
        out[b, h * TL:(h + 1) * TL, :] = r4.results[core]["Y"]
    return out
